# version 04
# baseline (speedup 1.0000x reference)
import numpy as np
import ml_dtypes
from contextlib import ExitStack

import concourse.bass as bass
import concourse.mybir as mybir
from concourse.bass_utils import run_bass_kernel_spmd
from concourse.aot_env import allow_ici_all_to_all

F32 = mybir.dt.float32
BF16 = mybir.dt.bfloat16
U8 = mybir.dt.uint8
I32 = mybir.dt.int32
ALU = mybir.AluOpType
AF = mybir.ActivationFunctionType
AX = mybir.AxisListType.X

NCORE = 8
B = 4
L = 4096
D = 1024
CH = 256
TOK = 2048
NKT = 8
EPS = 1e-5
BF = ml_dtypes.bfloat16


class Sem:
    def __init__(self, stack, nc, name):
        self.h = stack.enter_context(nc.semaphore(name))
        self.n = 0


class DmaRing:

    def __init__(self, stack, nc, name, n=10):
        self.sems = [Sem(stack, nc, f"{name}{i}") for i in range(n)]
        self.i = 0

    def take(self):
        s = self.sems[self.i % len(self.sems)]
        self.i += 1
        gate = [(s, s.n)]
        return s, gate

    def all(self):
        return [(s, s.n) for s in self.sems if s.n > 0]


class Kb:
    ENGS = ("tensor", "vector", "scalar", "sync", "gpsimd")

    def __init__(self, nc, stack):
        self.nc = nc
        self.q = {e: [] for e in self.ENGS}
        self.rings = {"sync": DmaRing(stack, nc, "rs"),
                      "gpsimd": DmaRing(stack, nc, "rg")}
        self.smm = Sem(stack, nc, "smm")
        self.sev = Sem(stack, nc, "sev")
        self.svv = Sem(stack, nc, "svv")
        self.sgp = Sem(stack, nc, "sgp")

    _uid = [0]

    def sb(self, st, name, shape, dt):
        Kb._uid[0] += 1
        return st.enter_context(self.nc.sbuf_tensor(
            f"{name}_{Kb._uid[0]}", list(shape), dt))

    def ps(self, st, name, shape, dt=F32):
        Kb._uid[0] += 1
        return st.enter_context(self.nc.psum_tensor(
            f"{name}_{Kb._uid[0]}", list(shape), dt))

    def op(self, eng, build, waits=(), incs=()):
        wv = [(s.h, v) for (s, v) in waits if v > 0]
        iv = []
        for s, kk in incs:
            s.n += kk
            iv.append((s.h, kk))

        def run(e, wv=wv, iv=iv, build=build):
            for h, v in wv:
                e.wait_ge(h, v)
            ins = build(e)
            for h, kk in iv:
                ins.then_inc(h, kk)
        self.q[eng].append(run)

    def wait(self, eng, waits):
        wv = [(s.h, v) for (s, v) in waits if v > 0]
        if not wv:
            return

        def run(e, wv=wv):
            for h, v in wv:
                e.wait_ge(h, v)
        self.q[eng].append(run)

    def barrier(self):
        w = [(s, s.n) for s in (self.smm, self.sev, self.svv, self.sgp)]
        w += self.rings["sync"].all() + self.rings["gpsimd"].all()
        for e in self.ENGS:
            self.wait(e, w)

    def dma_all(self, queue=None):
        if queue:
            return self.rings[queue].all()
        return self.rings["sync"].all() + self.rings["gpsimd"].all()

    def dma_op(self, queue, build, waits=()):
        s, gate = self.rings[queue].take()
        self.op(queue, build, waits=list(waits) + gate, incs=[(s, 16)])
        return [(s, s.n)]

    def flush(self, block):
        for name in self.ENGS:
            fns = self.q[name]
            if not fns:
                continue

            def body(e, fns=fns):
                for f in fns:
                    f(e)
            getattr(block, name)(body)

    def load(self, dst, src, queue="sync", waits=(), slow=False):
        if slow:
            return self.dma_op(queue, lambda e, d=dst, sr=src: e.dma_start(
                out=d, in_=sr, allow_slow_non_contiguous=True), waits=waits)
        return self.dma_op(queue, lambda e, d=dst, sr=src: e.dma_start(
            out=d, in_=sr), waits=waits)

    def store(self, dst, src, waits=(), queue="sync"):
        return self.dma_op(queue, lambda e, d=dst, sr=src: e.dma_start(
            out=d, in_=sr), waits=waits)


class Acc:
    def __init__(self, banks, init_war=None):
        self.banks = banks
        self.war = [list(init_war or []) for _ in banks]
        self.i = 0

    def next(self):
        j = self.i % len(self.banks)
        self.i += 1
        return self.banks[j], list(self.war[j]), j

    def done(self, j, waits):
        self.war[j] = list(waits)


_LN_N = [0]


def ln_env(k, st, ones128, onesrow, psS, psBr, psBm, pre=None, dt=BF16):
    _LN_N[0] += 1
    tag = str(_LN_N[0])
    return dict(ones128=ones128, onesrow=onesrow, psS=psS, psBr=psBr, psBm=psBm,
                srow=k.sb(st, "ln_srow" + tag, (1, 4 * 512), F32),
                st16=k.sb(st, "ln_st16" + tag, (1, 2 * 512), dt),
                sq0=k.sb(st, "ln_sq0" + tag, (128, 512), dt),
                sq1=k.sb(st, "ln_sq1" + tag, (128, 512), dt),
                stats_war=[], sq_war=[[], []], bc_war=[], pre=pre or [])


def emit_ln_group(k, X, colstride, g, env):
    ones128, onesrow = env["ones128"], env["onesrow"]
    psS, psBr, psBm = env["psS"], env["psBr"], env["psBm"]
    srow, st16 = env["srow"], env["st16"]
    sqs = (env["sq0"], env["sq1"])

    def Xs(kt):
        return X[:, kt * colstride + g * 512: kt * colstride + g * 512 + 512]

    for kt in range(NKT):
        k.op("tensor", lambda t, kt=kt: t.matmul(
            psS[0:1, :], ones128[:, 0:1], Xs(kt),
            start=(kt == 0), stop=(kt == NKT - 1)),
            waits=(env["pre"] + env["stats_war"]) if kt == 0 else ())
    for kt in range(NKT):
        sq = sqs[kt % 2]
        k.op("scalar", lambda s, kt=kt, sq=sq: s.activation(
            out=sq[:], in_=Xs(kt), func=AF.Square),
            waits=(env["pre"] if kt == 0 else []) + env["sq_war"][kt % 2],
            incs=[(k.sev, 1)])
        evq = k.sev.n
        k.op("tensor", lambda t, kt=kt, sq=sq: t.matmul(
            psBr[0:1, :], ones128[:, 0:1], sq[:],
            start=(kt == 0), stop=(kt == NKT - 1)),
            waits=[(k.sev, evq)] + (env["bc_war"] if kt == 0 else []),
            incs=[(k.smm, 1)])
        env["sq_war"][kt % 2] = [(k.smm, k.smm.n)]
    mmq = k.smm.n
    mrow = srow[0:1, 0:512]
    vrow = srow[0:1, 512:1024]
    rrow = srow[0:1, 1024:1536]
    trow = srow[0:1, 1536:2048]
    k.op("vector", lambda v: v.tensor_scalar(
        out=mrow, in0=psS[0:1, :], scalar1=1.0 / D, scalar2=None, op0=ALU.mult),
        waits=[(k.smm, mmq)])
    k.op("vector", lambda v: v.tensor_scalar(
        out=vrow, in0=psBr[0:1, :], scalar1=1.0 / D, scalar2=None, op0=ALU.mult))
    k.op("vector", lambda v: v.tensor_tensor(out=trow, in0=mrow, in1=mrow,
                                             op=ALU.mult))
    k.op("vector", lambda v: v.tensor_tensor(out=vrow, in0=vrow, in1=trow,
                                             op=ALU.subtract), incs=[(k.svv, 1)])
    env["stats_war"] = [(k.svv, k.svv.n)]
    vvq = k.svv.n
    k.op("vector", lambda v: v.tensor_scalar(
        out=vrow, in0=vrow, scalar1=EPS, scalar2=None, op0=ALU.add),
        incs=[(k.svv, 1)])
    vve = k.svv.n
    k.op("scalar", lambda s: s.activation(out=rrow, in_=vrow, func=AF.Ln),
         waits=[(k.svv, vve)], incs=[(k.sev, 1)])
    evl = k.sev.n
    k.op("vector", lambda v: v.tensor_scalar(
        out=rrow, in0=rrow, scalar1=-0.5, scalar2=None, op0=ALU.mult),
        waits=[(k.sev, evl)], incs=[(k.svv, 1)])
    vvh = k.svv.n
    k.op("scalar", lambda s: s.activation(out=rrow, in_=rrow, func=AF.Exp),
         waits=[(k.svv, vvh)],
         incs=[(k.sev, 1)])
    evr = k.sev.n
    k.op("vector", lambda v: v.tensor_tensor(out=trow, in0=mrow, in1=rrow,
                                             op=ALU.mult), waits=[(k.sev, evr)])
    k.op("vector", lambda v: v.tensor_copy(out=st16[0:1, 0:512], in_=rrow))
    k.op("vector", lambda v: v.tensor_copy(out=st16[0:1, 512:1024], in_=trow),
         incs=[(k.svv, 1)])
    vvr = k.svv.n
    k.op("tensor", lambda t: t.matmul(psBr[:], onesrow[:], st16[0:1, 0:512],
                                      start=True, stop=True),
         waits=[(k.svv, vvr)] + env["bc_war"])
    k.op("tensor", lambda t: t.matmul(psBm[:], onesrow[:], st16[0:1, 512:1024],
                                      start=True, stop=True), incs=[(k.smm, 1)])
    mmb = k.smm.n
    for kt in range(NKT):
        k.op("vector", lambda v, kt=kt: v.tensor_tensor(
            out=Xs(kt), in0=Xs(kt), in1=psBr[:], op=ALU.mult),
            waits=[(k.smm, mmb)] if kt == 0 else ())
        k.op("vector", lambda v, kt=kt: v.tensor_tensor(
            out=Xs(kt), in0=Xs(kt), in1=psBm[:], op=ALU.subtract),
            incs=[(k.svv, 1)])
    env["bc_war"] = [(k.svv, k.svv.n)]
    return k.svv.n


A_INPUTS = ("xbatT", "xownT", "wqT", "wkT", "wvT", "bqh", "bkh", "bvrow",
            "woT", "scaleh", "cosT", "sinST", "Psw", "band", "w3T", "u3T",
            "b3h", "wo0T", "uo0T", "bo0h", "inch")
B_INPUTS = ("xownT", "gateT", "boh", "waT", "wabh", "wbT")
C_INPUTS = ("sel2h", "wc1T", "wdT", "w1", "b1h", "w2", "b2", "Esel", "id16")


def _new_nc():
    return bass.Bass("TRN2", target_bir_lowering=False, debug=False,
                     detect_race_conditions=False)


def build_A():
    nc = _new_nc()
    DI = lambda n, s, d=F32: nc.dram_tensor(n, list(s), d, kind="ExternalInput").ap()
    DO = lambda n, s, d=F32: nc.dram_tensor(n, list(s), d, kind="ExternalOutput").ap()
    io = {}
    io["xbatT"] = DI("xbatT", (D, L))
    io["xownT"] = DI("xownT", (D, TOK))
    for n_, sh in (("wqT", (D, 512)), ("wkT", (D, 512)), ("wvT", (D, 512)),
                   ("bqh", (128, 4)), ("bkh", (128, 4)), ("bvrow", (1, 512)),
                   ("woT", (512, D)), ("scaleh", (128, 1)), ("cosT", (128, L)),
                   ("sinST", (128, L)), ("Psw", (128, 128)),
                   ("band", (4 * 128, 512)), ("w3T", (D, 3072)),
                   ("u3T", (D, 3072)), ("b3h", (128, 24)), ("wo0T", (D, D)),
                   ("uo0T", (D, D)), ("bo0h", (128, 8)), ("inch", (128, 8))):
        io[n_] = DI(n_, sh)
    io["partialT"] = DO("partialT", (D, L))
    io["gT"] = DO("gT", (D, TOK))
    with ExitStack() as top:
        k = Kb(nc, top)
        block = top.enter_context(nc.Block())
        emit_A(k, top, io)
        k.flush(block)
    return nc


def emit_A(k, top, io):
    nc = k.nc
    (xbatT, xownT, wqT, wkT, wvT, bqh, bkh, bvrow, woT, scaleh, cosT, sinST,
     Psw, band, w3T, u3T, b3h, wo0T, uo0T, bo0h, inch) = (
        io[n_] for n_ in A_INPUTS)
    gT = io["gT"]
    partialT = io.get("partialT")
    a2a_in = io.get("a2a_in")
    _dbg = False
    if True:
        sA = ExitStack()
        sA.__enter__()
        xring = k.sb(sA, "xring", (128, 2 * NKT * 512), BF16)
        qT = k.sb(sA, "qT", (128, 4 * L), BF16)
        kTt = k.sb(sA, "kTt", (128, 4 * L), BF16)
        vS = k.sb(sA, "vS", (128, 32 * 520), BF16)
        oT = k.sb(sA, "oT", (128, 4 * L), BF16)
        rrow = k.sb(sA, "rrow", (1, 512), F32)
        r16row = k.sb(sA, "r16row", (1, 512), BF16)
        ones128 = k.sb(sA, "ones128", (128, 1), BF16)
        onesrow = k.sb(sA, "onesrow", (1, 128), BF16)

        psST0 = k.ps(sA, "psST0", (128, 512))
        psST1 = k.ps(sA, "psST1", (128, 512))
        psO = k.ps(sA, "psO", (65, 512))
        psC = k.ps(sA, "psC", (128, 512))
        psD = k.ps(sA, "psD", (128, 512))
        psE = k.ps(sA, "psE", (128, 512))
        psF = k.ps(sA, "psF", (128, 512))
        psG = k.ps(sA, "psG", (128, 512))

        k.op("gpsimd", lambda g: g.memset(ones128[:], 1.0))
        k.op("gpsimd", lambda g: g.memset(onesrow[:], 1.0), incs=[(k.sgp, 1)])
        ones_rdy = [(k.sgp, k.sgp.n)]

        with ExitStack() as sW:
            wq_sb = k.sb(sW, "wq_sb", (128, 8 * 512), BF16)
            wk_sb = k.sb(sW, "wk_sb", (128, 8 * 512), BF16)
            wv_sb = k.sb(sW, "wv_sb", (128, 8 * 512), BF16)
            bq_sb = k.sb(sW, "bq_sb", (128, 4), F32)
            bk_sb = k.sb(sW, "bk_sb", (128, 4), F32)
            bv_sb = k.sb(sW, "bv_sb", (1, 512), BF16)
            cos_sb = k.sb(sW, "cos_sb", (128, L), BF16)
            sin_sb = k.sb(sW, "sin_sb", (128, L), BF16)
            psw_sb = k.sb(sW, "psw_sb", (128, 128), BF16)
            scol = k.sb(sW, "scol", (128, 1), F32)
            rt1 = k.sb(sW, "rt1", (128, 512), BF16)
            rt2 = k.sb(sW, "rt2", (128, 512), BF16)

            for wsb, wdr in ((wq_sb, wqT), (wk_sb, wkT), (wv_sb, wvT)):
                k.load(wsb[:].rearrange("p (kt m) -> p kt m", kt=8),
                       wdr.rearrange("(kt p) m -> p kt m", kt=8, p=128), "gpsimd")
            k.load(bv_sb[:], bvrow, "gpsimd")
            k.load(psw_sb[:], Psw, "gpsimd")
            k.load(scol[:], scaleh)
            k.load(bq_sb[:], bqh)
            k.load(bk_sb[:], bkh)
            k.load(cos_sb[:], cosT, "gpsimd")
            k.load(sin_sb[:], sinST, "gpsimd")
            wload = k.dma_all()

            env = ln_env(k, sW, ones128, onesrow, psE, psF, psG, pre=ones_rdy)
            accQK = Acc([psC, psD])
            qkv_done = []
            for g8 in range(8):
                slot = g8 % 2
                Xv = xring[:, slot * NKT * 512:(slot + 1) * NKT * 512]

                def Xb(kt, Xv=Xv):
                    return Xv[:, kt * 512:(kt + 1) * 512]

                war = [(k.svv, qkv_done[g8 - 2])] if g8 >= 2 else []
                ld = k.load(Xv.rearrange("p (kt n) -> p kt n", kt=8),
                            xbatT.rearrange("(kt p) l -> p kt l", kt=8, p=128)
                            [:, :, g8 * 512:(g8 + 1) * 512], "gpsimd", waits=war)
                env["pre"] = ld + ones_rdy
                lnv = emit_ln_group(k, Xv, 512, 0, env)
                for (wsb, bsb, dstT) in ((wq_sb, bq_sb, qT), (wk_sb, bk_sb, kTt)):
                    for mt in range(4):
                        bank, war2, bj = accQK.next()
                        for kt in range(NKT):
                            k.op("tensor", lambda t, kt=kt, mt=mt, w=wsb,
                                 bank=bank, Xb=Xb: t.matmul(
                                     bank[:],
                                     w[:, kt * 512 + mt * 128:
                                       kt * 512 + mt * 128 + 128],
                                     Xb(kt), start=(kt == 0),
                                     stop=(kt == NKT - 1)),
                                 waits=([(k.svv, lnv)] + war2 + wload)
                                 if kt == 0 else (),
                                 incs=[(k.smm, 1)] if kt == NKT - 1 else ())
                        mmq = k.smm.n
                        k.op("vector", lambda v, mt=mt, b=bsb, d=dstT, g8=g8,
                             bank=bank: v.tensor_scalar(
                                 out=d[:, mt * L + g8 * 512:
                                       mt * L + g8 * 512 + 512],
                                 in0=bank[:], scalar1=b[:, mt:mt + 1],
                                 scalar2=None, op0=ALU.add),
                             waits=[(k.smm, mmq)], incs=[(k.svv, 1)])
                        accQK.done(bj, [(k.svv, k.svv.n)])
                for tmt in range(4):
                    bank, war2, bj = accQK.next()
                    ktile = g8 * 4 + tmt
                    for kt in range(NKT):
                        k.op("tensor", lambda t, kt=kt, tmt=tmt, bank=bank,
                             Xb=Xb: t.matmul(
                                 bank[:], Xb(kt)[:, tmt * 128:(tmt + 1) * 128],
                                 wv_sb[:, kt * 512:(kt + 1) * 512],
                                 start=(kt == 0), stop=False),
                             waits=(war2 + [(k.svv, lnv)] + wload)
                             if kt == 0 else ())
                    k.op("tensor", lambda t, bank=bank: t.matmul(
                        bank[:], onesrow[:], bv_sb[:], start=False, stop=True),
                        incs=[(k.smm, 1)])
                    mmv = k.smm.n
                    for h in range(8):
                        k.op("vector", lambda v, h=h, kt2=ktile, bank=bank:
                             v.tensor_copy(
                                 out=vS[:, kt2 * 520 + h * 65:
                                        kt2 * 520 + h * 65 + 64],
                                 in_=bank[:, h * 64: h * 64 + 64]),
                             waits=[(k.smm, mmv)] if h == 0 else (),
                             incs=[(k.svv, 1)] if h == 7 else ())
                    accQK.done(bj, [(k.svv, k.svv.n)])
                qkv_done.append(k.svv.n)

            k.op("gpsimd", lambda g: g.memset(
                vS[:].rearrange("p (kt h s) -> p kt h s", kt=32, h=8)
                [:, :, :, 64:65], 1.0),
                waits=[(k.svv, k.svv.n)], incs=[(k.sgp, 1)])
            vones = k.sgp.n

            accR = Acc([psC, psD], init_war=[(k.svv, k.svv.n)])
            for dstT in (qT, kTt):
                for pt in range(4):
                    for ng in range(8):
                        bank, war2, bj = accR.next()
                        sl = dstT[:, pt * L + ng * 512: pt * L + ng * 512 + 512]
                        k.op("tensor", lambda t, bank=bank, sl=sl: t.matmul(
                            bank[:], psw_sb[:], sl, start=True, stop=True),
                            waits=war2, incs=[(k.smm, 1)])
                        mmr = k.smm.n
                        csl = cos_sb[:, ng * 512:(ng + 1) * 512]
                        ssl = sin_sb[:, ng * 512:(ng + 1) * 512]
                        k.op("vector", lambda v, sl=sl, csl=csl: v.tensor_tensor(
                            out=rt1[:], in0=sl, in1=csl, op=ALU.mult),
                            waits=[(k.smm, mmr)])
                        k.op("vector", lambda v, bank=bank, ssl=ssl:
                             v.tensor_tensor(out=rt2[:], in0=bank[:], in1=ssl,
                                             op=ALU.mult))
                        k.op("vector", lambda v, sl=sl: v.tensor_tensor(
                            out=sl, in0=rt1[:], in1=rt2[:], op=ALU.add),
                            incs=[(k.svv, 1)])
                        accR.done(bj, [(k.svv, k.svv.n)])
            k.op("vector", lambda v: v.tensor_scalar(
                out=qT[:], in0=qT[:], scalar1=scol[:], scalar2=None,
                op0=ALU.mult), waits=wload, incs=[(k.svv, 1)])
            k.barrier()

        with ExitStack() as sAtt:
            at0 = k.sb(sAtt, "at0", (128, 512), BF16)
            at1 = k.sb(sAtt, "at1", (128, 512), BF16)
            at2 = k.sb(sAtt, "at2", (128, 512), BF16)
            at3 = k.sb(sAtt, "at3", (128, 512), BF16)
            band_sb = k.sb(sAtt, "band_sb", (128, 4 * 512), BF16)
            k.load(band_sb[:].rearrange("p (o n) -> p o n", o=4),
                   band.rearrange("(o p) n -> p o n", o=4, p=128), "gpsimd")
            bnd = k.dma_all("gpsimd")
            k.wait("vector", bnd)
            k.wait("scalar", bnd)
            k.wait("tensor", bnd + [(k.sgp, vones)])
            ats = (at0, at1, at2, at3)
            stb = (psST0, psST1, psC, psD)
            DEPTH = 4
            exp_c = {}
            mk_c = {}
            av_c = {}
            last_exp = 0
            last_av = 0
            rr_war = 0
            accNrm = Acc([psE, psF])
            o_ev = []

            def emit_st(j, h, qg, kt, hp, ho):
                w = []
                if kt >= DEPTH:
                    w.append((k.sev, exp_c[(j, kt - DEPTH)]))
                elif j > 0:
                    w.append((k.sev, last_exp))
                k.op("tensor", lambda t, kt=kt, hp=hp, ho=ho, qg=qg:
                     t.matmul(stb[kt % DEPTH][:],
                              kTt[ho:ho + 64, hp * L + kt * 128:
                                  hp * L + kt * 128 + 128],
                              qT[ho:ho + 64, hp * L + qg * 512:
                                 hp * L + qg * 512 + 512],
                              start=True, stop=True),
                     waits=w, incs=[(k.smm, 1)])
                stc = k.smm.n
                w = [(k.smm, stc)]
                if kt >= DEPTH:
                    w.append((k.smm, av_c[(j, kt - DEPTH)]))
                elif j > 0:
                    w.append((k.smm, last_av))
                k.op("scalar", lambda sc, kt=kt: sc.activation(
                    out=ats[kt % DEPTH][:], in_=stb[kt % DEPTH][:],
                    func=AF.Exp), waits=w, incs=[(k.sev, 1)])
                exp_c[(j, kt)] = k.sev.n
                o = kt - 4 * qg
                if o >= 0:
                    k.op("vector", lambda v, kt=kt, o=o: v.tensor_tensor(
                        out=ats[kt % DEPTH][:], in0=ats[kt % DEPTH][:],
                        in1=band_sb[:, o * 512:(o + 1) * 512], op=ALU.mult),
                        waits=[(k.sev, exp_c[(j, kt)])], incs=[(k.svv, 1)])
                    mk_c[(j, kt)] = k.svv.n

            def emit_av(j, h, qg, kt, K):
                o = kt - 4 * qg
                w = ([(k.svv, mk_c[(j, kt)])] if o >= 0
                     else [(k.sev, exp_c[(j, kt)])])
                if kt == 0 and j > 0:
                    w += [(k.sev, o_ev[j - 1][0]), (k.svv, o_ev[j - 1][1])]
                k.op("tensor", lambda t, kt=kt, h=h, K=K: t.matmul(
                    psO[:],
                    vS[:, kt * 520 + h * 65: kt * 520 + h * 65 + 65],
                    ats[kt % DEPTH][:], start=(kt == 0), stop=(kt == K - 1)),
                    waits=w, incs=[(k.smm, 1)])
                av_c[(j, kt)] = k.smm.n

            for h in range(8):
                hp, ho = h // 2, 64 * (h % 2)
                for qg in range(8):
                    j = h * 8 + qg
                    K = 4 * qg + 4
                    for kt in range(K):
                        emit_st(j, h, qg, kt, hp, ho)
                        if kt >= DEPTH - 1:
                            emit_av(j, h, qg, kt - (DEPTH - 1), K)
                    for kt in range(max(0, K - (DEPTH - 1)), K):
                        emit_av(j, h, qg, kt, K)
                    last_exp = exp_c[(j, K - 1)]
                    last_av = av_c[(j, K - 1)]
                    osl = oT[ho:ho + 64, hp * L + qg * 512:
                             hp * L + qg * 512 + 512]
                    k.op("scalar", lambda sc, osl=osl: sc.copy(
                        out=osl, in_=psO[0:64, :]), waits=[(k.smm, last_av)],
                        incs=[(k.sev, 1)])
                    oev = k.sev.n
                    k.op("vector", lambda v: v.reciprocal(
                        out=rrow[:], in_=psO[64:65, :]),
                        waits=[(k.smm, last_av), (k.svv, rr_war)],
                        incs=[(k.svv, 1)])
                    o_ev.append((oev, k.svv.n))
                    k.op("vector", lambda v: v.tensor_copy(
                        out=r16row[:], in_=rrow[:]),
                        waits=[(k.svv, k.svv.n)], incs=[(k.svv, 1)])
                    r16v = k.svv.n
                    nbank, nwar, nbj = accNrm.next()
                    k.op("tensor", lambda t, nbank=nbank: t.matmul(
                        nbank[0:64, :], onesrow[:, 0:64], r16row[:],
                        start=True, stop=True),
                        waits=[(k.svv, r16v)] + nwar, incs=[(k.smm, 1)])
                    mmn = k.smm.n
                    k.op("vector", lambda v, osl=osl, nbank=nbank:
                         v.tensor_tensor(out=osl, in0=osl,
                                         in1=nbank[0:64, :], op=ALU.mult),
                         waits=[(k.smm, mmn), (k.sev, oev)], incs=[(k.svv, 1)])
                    accNrm.done(nbj, [(k.svv, k.svv.n)])
                    rr_war = k.svv.n
            normv = k.svv.n
            k.barrier()

        with ExitStack() as sProj:
            wo_sb = k.sb(sProj, "wo_sb", (128, 4 * 8 * 128), BF16)
            stg = (k.sb(sProj, "stg0", (128, 512), F32),
                   k.sb(sProj, "stg1", (128, 512), F32))
            wl = k.load(wo_sb[:].rearrange("p (kt m) -> p kt m", kt=4),
                        woT.rearrange("(kt p) m -> p kt m", kt=4, p=128),
                        "gpsimd")
            accP = Acc([psC, psD])
            stg_war = [[], []]
            for mt in range(8):
                for ng in range(8):
                    bank, war2, bj = accP.next()
                    for kt in range(4):
                        k.op("tensor", lambda t, kt=kt, mt=mt, ng=ng, bank=bank:
                             t.matmul(bank[:],
                                      wo_sb[:, (kt * 8 + mt) * 128:
                                            (kt * 8 + mt) * 128 + 128],
                                      oT[:, kt * L + ng * 512:
                                         kt * L + ng * 512 + 512],
                                      start=(kt == 0), stop=(kt == 3)),
                             waits=(war2 + wl + [(k.svv, normv)])
                             if kt == 0 else (),
                             incs=[(k.smm, 1)] if kt == 3 else ())
                    mmp = k.smm.n
                    sg = stg[ng % 2]
                    k.op("vector", lambda v, sg=sg, bank=bank: v.tensor_copy(
                        out=sg[:], in_=bank[:]),
                        waits=[(k.smm, mmp)] + stg_war[ng % 2],
                        incs=[(k.svv, 1)])
                    accP.done(bj, [(k.svv, k.svv.n)])
                    vvp = k.svv.n
                    if a2a_in is None:
                        stg_war[ng % 2] = k.store(
                            partialT[mt * 128:(mt + 1) * 128,
                                     ng * 512:(ng + 1) * 512], sg[:],
                            waits=[(k.svv, vvp)])
                    else:
                        toks = []
                        for half in (0, 1):
                            ck = 2 * ng + half
                            r_ = ck if ck < 8 else 15 - ck
                            pos = 0 if ck < 8 else 1
                            toks += k.store(
                                a2a_in[r_, mt * 128:(mt + 1) * 128,
                                       pos * 256:(pos + 1) * 256],
                                sg[:, half * 256:(half + 1) * 256],
                                waits=[(k.svv, vvp)])
                        stg_war[ng % 2] = toks
            if _dbg:
                k.store(qdbg, qT[:], queue="gpsimd")
                k.store(kdbg, kTt[:], queue="gpsimd")
                k.store(odbg, oT[:], queue="gpsimd")
                k.store(vdbg, vS[:], queue="gpsimd")
            k.barrier()

        sA.close()

        sF = ExitStack()
        sF.__enter__()
        ones128b = k.sb(sF, "ones128b", (128, 1), BF16)
        onesrowb = k.sb(sF, "onesrowb", (1, 128), BF16)
        xn = k.sb(sF, "xn", (128, NKT * TOK), BF16)
        cT = k.sb(sF, "cT", (128, 8 * 512), F32)
        c16 = k.sb(sF, "c16", (128, 8 * 512), BF16)
        uT = k.sb(sF, "uT", (128, 8 * 512), F32)
        gsb = k.sb(sF, "gsb", (128, NKT * TOK), BF16)
        wring = k.sb(sF, "wring", (128, 8 * 1024), BF16)
        gate_f = k.sb(sF, "gate_f", (128, 3 * 512), F32)
        tmp_a = k.sb(sF, "tmp_a", (128, 512), F32)
        tmp_b = k.sb(sF, "tmp_b", (128, 512), F32)
        tmp_c = k.sb(sF, "tmp_c", (128, 512), F32)
        b3_sb = k.sb(sF, "b3_sb", (128, 24), F32)
        bo0_sb = k.sb(sF, "bo0_sb", (128, 8), F32)
        inc_sb = k.sb(sF, "inc_sb", (128, 8), F32)
        onesf = k.sb(sF, "onesf", (128, 512), F32)
        psH0 = k.ps(sF, "psH0", (128, 512))
        psH1 = k.ps(sF, "psH1", (128, 512))
        psH2 = k.ps(sF, "psH2", (128, 512))
        psB0 = k.ps(sF, "psB0", (128, 512))
        psB1 = k.ps(sF, "psB1", (128, 512))
        psB2 = k.ps(sF, "psB2", (128, 512))

        k.op("gpsimd", lambda g: g.memset(ones128b[:], 1.0))
        k.op("gpsimd", lambda g: g.memset(onesrowb[:], 1.0), incs=[(k.sgp, 1)])
        k.op("vector", lambda v: v.memset(onesf[:], 1.0), incs=[(k.svv, 1)])
        ones2 = [(k.sgp, k.sgp.n), (k.svv, k.svv.n)]
        k.load(xn[:].rearrange("p (kt n) -> p kt n", kt=8),
               xownT.rearrange("(kt p) n -> p kt n", kt=8, p=128), "gpsimd")
        k.load(b3_sb[:], b3h)
        k.load(bo0_sb[:], bo0h)
        k.load(inc_sb[:], inch)
        ldf = k.dma_all()

        env2 = ln_env(k, sF, ones128b, onesrowb, psB0, psB1, psB2,
                      pre=ldf + ones2)
        lnv2 = 0
        for g in range(4):
            lnv2 = emit_ln_group(k, xn, TOK, g, env2)

        for kt in range(NKT):
            k.op("vector", lambda v, kt=kt: v.tensor_scalar(
                out=cT[:, kt * 512:(kt + 1) * 512], in0=onesf[:],
                scalar1=inc_sb[:, kt:kt + 1], scalar2=None, op0=ALU.mult),
                waits=ldf if kt == 0 else (), incs=[(k.svv, 1)])

        gatebanks = (psH0, psH1, psH2)
        gb_war = [[], [], []]
        wslot_war = [[] for _ in range(8)]
        for t_ in range(B):
            for kt in range(NKT):
                k.op("scalar", lambda s, kt=kt: s.copy(
                    out=c16[:, kt * 512:(kt + 1) * 512],
                    in_=cT[:, kt * 512:(kt + 1) * 512]),
                    waits=[(k.svv, k.svv.n)] if kt == 0 else (),
                    incs=[(k.sev, 1)])
            cv = k.sev.n
            for dm in range(8):
                gd = []
                for gi in range(3):
                    mcol = gi * 1024 + dm * 128
                    bank = gatebanks[gi]
                    s_w = (dm * 3 + gi) % 4
                    s_u = 4 + (dm * 3 + gi) % 4
                    ldw = k.load(wring[:, s_w * 1024:(s_w + 1) * 1024]
                                 .rearrange("p (kt m) -> p kt m", kt=8),
                                 w3T.rearrange("(kt p) m -> p kt m", kt=8, p=128)
                                 [:, :, mcol:mcol + 128], "gpsimd",
                                 waits=wslot_war[s_w])
                    ldw = ldw + k.load(
                        wring[:, s_u * 1024:(s_u + 1) * 1024]
                        .rearrange("p (kt m) -> p kt m", kt=8),
                        u3T.rearrange("(kt p) m -> p kt m", kt=8, p=128)
                        [:, :, mcol:mcol + 128], "gpsimd",
                        waits=wslot_war[s_u])
                    for kt in range(NKT):
                        k.op("tensor", lambda t, kt=kt, s_w=s_w, t_=t_,
                             bank=bank: t.matmul(
                                 bank[:],
                                 wring[:, s_w * 1024 + kt * 128:
                                       s_w * 1024 + kt * 128 + 128],
                                 xn[:, kt * TOK + t_ * 512:
                                    kt * TOK + t_ * 512 + 512],
                                 start=(kt == 0), stop=False),
                             waits=(ldw + [(k.sev, cv), (k.svv, lnv2)]
                                    + gb_war[gi]) if kt == 0 else ())
                    for kt in range(NKT):
                        k.op("tensor", lambda t, kt=kt, s_u=s_u, bank=bank:
                             t.matmul(
                                 bank[:],
                                 wring[:, s_u * 1024 + kt * 128:
                                       s_u * 1024 + kt * 128 + 128],
                                 c16[:, kt * 512:(kt + 1) * 512],
                                 start=False, stop=(kt == NKT - 1)),
                             incs=[(k.smm, 1)] if kt == NKT - 1 else ())
                    wslot_war[s_w] = [(k.smm, k.smm.n)]
                    wslot_war[s_u] = [(k.smm, k.smm.n)]
                    mmg = k.smm.n
                    k.op("scalar", lambda s, gi=gi, dm=dm, bank=bank:
                         s.activation(
                             out=gate_f[:, gi * 512:(gi + 1) * 512],
                             in_=bank[:],
                             func=(AF.Sigmoid if gi < 2 else AF.Tanh),
                             bias=b3_sb[:, gi * 8 + dm: gi * 8 + dm + 1],
                             scale=1.0),
                         waits=[(k.smm, mmg)], incs=[(k.sev, 1)])
                    gb_war[gi] = [(k.sev, k.sev.n)]
                    gd.append(k.sev.n)
                k.op("vector", lambda v, dm=dm: v.tensor_tensor(
                    out=tmp_a[:], in0=gate_f[:, 0:512],
                    in1=cT[:, dm * 512:(dm + 1) * 512], op=ALU.mult),
                    waits=[(k.sev, gd[-1])])
                k.op("vector", lambda v: v.tensor_tensor(
                    out=tmp_b[:], in0=gate_f[:, 512:1024],
                    in1=gate_f[:, 1024:1536], op=ALU.mult))
                k.op("vector", lambda v, dm=dm: v.tensor_tensor(
                    out=cT[:, dm * 512:(dm + 1) * 512], in0=tmp_a[:],
                    in1=tmp_b[:], op=ALU.add), incs=[(k.svv, 1)])
                gb_war[0] = gb_war[1] = gb_war[2] = [(k.svv, k.svv.n)]

        for kt in range(NKT):
            k.op("scalar", lambda s, kt=kt: s.copy(
                out=c16[:, kt * 512:(kt + 1) * 512],
                in_=cT[:, kt * 512:(kt + 1) * 512]),
                waits=[(k.svv, k.svv.n)] if kt == 0 else (), incs=[(k.sev, 1)])
        cfv = k.sev.n
        accU = Acc([psH0, psH1], init_war=[(k.sev, k.sev.n)])
        for mt in range(8):
            slot = mt % 4
            ldu = k.load(wring[:, slot * 1024:(slot + 1) * 1024]
                         .rearrange("p (kt m) -> p kt m", kt=8),
                         uo0T.rearrange("(kt p) m -> p kt m", kt=8, p=128)
                         [:, :, mt * 128:(mt + 1) * 128], "gpsimd",
                         waits=wslot_war[slot])
            bank, war2, bj = accU.next()
            for kt in range(NKT):
                k.op("tensor", lambda t, kt=kt, slot=slot, bank=bank: t.matmul(
                    bank[:], wring[:, slot * 1024 + kt * 128:
                                   slot * 1024 + kt * 128 + 128],
                    c16[:, kt * 512:(kt + 1) * 512],
                    start=(kt == 0), stop=(kt == NKT - 1)),
                    waits=(ldu + [(k.sev, cfv)] + war2) if kt == 0 else (),
                    incs=[(k.smm, 1)] if kt == NKT - 1 else ())
            wslot_war[slot] = [(k.smm, k.smm.n)]
            mmu = k.smm.n
            k.op("vector", lambda v, mt=mt, bank=bank: v.tensor_copy(
                out=uT[:, mt * 512:(mt + 1) * 512], in_=bank[:]),
                waits=[(k.smm, mmu)], incs=[(k.svv, 1)])
            accU.done(bj, [(k.svv, k.svv.n)])
        uv = k.svv.n

        accX = Acc([psH0, psH1], init_war=[(k.svv, uv)])
        gv = 0
        for mt in range(8):
            slot = mt % 4
            ldx = k.load(wring[:, slot * 1024:(slot + 1) * 1024]
                         .rearrange("p (kt m) -> p kt m", kt=8),
                         wo0T.rearrange("(kt p) m -> p kt m", kt=8, p=128)
                         [:, :, mt * 128:(mt + 1) * 128], "gpsimd",
                         waits=wslot_war[slot])
            for tg in range(4):
                bank, war2, bj = accX.next()
                for kt in range(NKT):
                    k.op("tensor", lambda t, kt=kt, slot=slot, tg=tg, bank=bank:
                         t.matmul(bank[:],
                                  wring[:, slot * 1024 + kt * 128:
                                        slot * 1024 + kt * 128 + 128],
                                  xn[:, kt * TOK + tg * 512:
                                     kt * TOK + tg * 512 + 512],
                                  start=(kt == 0), stop=(kt == NKT - 1)),
                         waits=(ldx + war2) if kt == 0 else (),
                         incs=[(k.smm, 1)] if kt == NKT - 1 else ())
                mmx = k.smm.n
                if tg == 3:
                    wslot_war[slot] = [(k.smm, mmx)]
                k.op("vector", lambda v, mt=mt, bank=bank: v.tensor_tensor(
                    out=tmp_a[:], in0=uT[:, mt * 512:(mt + 1) * 512],
                    in1=bank[:], op=ALU.add),
                    waits=[(k.smm, mmx), (k.sev, k.sev.n)], incs=[(k.svv, 1)])
                accX.done(bj, [(k.svv, k.svv.n)])
                vvx = k.svv.n
                k.op("scalar", lambda s, mt=mt: s.activation(
                    out=tmp_b[:], in_=tmp_a[:], func=AF.Tanh,
                    bias=bo0_sb[:, mt:mt + 1], scale=1.0),
                    waits=[(k.svv, vvx)])
                k.op("scalar", lambda s: s.activation(
                    out=tmp_c[:], in_=tmp_b[:], func=AF.Sigmoid),
                    incs=[(k.sev, 1)])
                evx = k.sev.n
                k.op("vector", lambda v, mt=mt, tg=tg: v.tensor_tensor(
                    out=gsb[:, mt * TOK + tg * 512: mt * TOK + tg * 512 + 512],
                    in0=tmp_b[:], in1=tmp_c[:], op=ALU.mult),
                    waits=[(k.sev, evx)], incs=[(k.svv, 1)])
                gv = k.svv.n
        k.store(gT.rearrange("(kt p) n -> p kt n", kt=8, p=128),
                gsb[:].rearrange("p (kt n) -> p kt n", kt=8),
                waits=[(k.svv, gv)], queue="gpsimd")
        k.wait("sync", k.dma_all())
        k.wait("gpsimd", k.dma_all())
        sF.close()
        k.barrier()


def build_B():
    nc = _new_nc()
    DI = lambda n, s, d=F32: nc.dram_tensor(n, list(s), d, kind="ExternalInput").ap()
    DO = lambda n, s, d=F32: nc.dram_tensor(n, list(s), d, kind="ExternalOutput").ap()
    io = {}
    io["xownT"] = DI("xownT", (D, TOK))
    io["gT"] = DI("gT", (D, TOK))
    io["recv"] = DI("recv", (NCORE, D, 512))
    io["boh"] = DI("boh", (128, 8))
    io["waT"] = DI("waT", (D, CH))
    io["wabh"] = DI("wabh", (128, 2))
    io["wbT"] = DI("wbT", (D, CH))
    io["gateT"] = DI("gateT", (D, 8))
    io["x1T"] = DO("x1T", (D, TOK))
    io["xn1T"] = DO("xn1T", (D, TOK))
    io["aTo"] = DO("aTo", (CH, TOK))
    io["bTo"] = DO("bTo", (CH, TOK))
    io["summ"] = DO("summ", (1, 4096))
    io["logitsT"] = DO("logitsT", (TOK, 8))
    with ExitStack() as top:
        k = Kb(nc, top)
        block = top.enter_context(nc.Block())
        emit_B(k, top, io)
        k.flush(block)
    return nc


def emit_B(k, top, io):
    nc = k.nc
    xownT = io["xownT"]
    gT = io["gT"]
    recv = io["recv"]
    boh, waT, wabh, wbT, gateT = (io[n_] for n_ in
                                  ("boh", "waT", "wabh", "wbT", "gateT"))
    x1T, xn1T, aTo, bTo, summ, logitsT = (io[n_] for n_ in
                                          ("x1T", "xn1T", "aTo", "bTo",
                                           "summ", "logitsT"))
    recv_wait = io.get("recv_wait", [])
    if True:
        sB = ExitStack()
        sB.__enter__()
        xo = k.sb(sB, "xo", (128, NKT * TOK), F32)
        ones128 = k.sb(sB, "ones128", (128, 1), F32)
        onesrow = k.sb(sB, "onesrow", (1, 128), F32)
        psL = k.ps(sB, "psL", (128, 8))
        psS = k.ps(sB, "psS", (128, 512))
        psBr = k.ps(sB, "psBr", (128, 512))
        psBm = k.ps(sB, "psBm", (128, 512))
        psA0 = k.ps(sB, "psA0", (128, 512))
        psA1 = k.ps(sB, "psA1", (128, 512))

        k.op("gpsimd", lambda g: g.memset(ones128[:], 1.0))
        k.op("gpsimd", lambda g: g.memset(onesrow[:], 1.0), incs=[(k.sgp, 1)])
        ones_rdy = [(k.sgp, k.sgp.n)]

        k.load(xo[:].rearrange("p (kt n) -> p kt n", kt=8),
               xownT.rearrange("(kt p) n -> p kt n", kt=8, p=128))
        xol = k.dma_all("sync")

        x1v = 0
        with ExitStack() as s1:
            gsb = k.sb(s1, "gsb", (128, NKT * TOK), BF16)
            rv = k.sb(s1, "rv", (128, 2 * NKT * 512), F32)
            bo_sb = k.sb(s1, "bo_sb", (128, 8), F32)
            tmp = k.sb(s1, "tmp", (128, 512), F32)
            tmp2 = k.sb(s1, "tmp2", (128, 512), F32)
            k.load(gsb[:].rearrange("p (kt n) -> p kt n", kt=8),
                   gT.rearrange("(kt p) n -> p kt n", kt=8, p=128), "gpsimd")
            k.load(bo_sb[:], boh)
            base_ld = k.dma_all()
            rv_war = []
            for b in range(B):
                for two in range(2):
                    k.load(rv[:, two * NKT * 512:(two + 1) * NKT * 512]
                           .rearrange("p (kt n) -> p kt n", kt=8),
                           recv[2 * b + two, :, :]
                           .rearrange("(kt p) n -> p kt n", kt=8, p=128),
                           waits=(rv_war + recv_wait) if two == 0 else ())
                ldr = k.dma_all("sync") + (base_ld if b == 0 else [])
                for mt in range(8):
                    r0 = rv[:, mt * 512: mt * 512 + 512]
                    r1 = rv[:, NKT * 512 + mt * 512: NKT * 512 + mt * 512 + 512]
                    col = mt * TOK + b * 512
                    k.op("vector", lambda v, r0=r0, r1=r1, mt=mt:
                         v.scalar_tensor_tensor(
                             out=tmp[:], in0=r0, scalar=bo_sb[:, mt:mt + 1],
                             in1=r1, op0=ALU.add, op1=ALU.add),
                         waits=ldr if mt == 0 else ())
                    k.op("vector", lambda v, col=col: v.tensor_tensor(
                        out=tmp2[:], in0=tmp[:], in1=gsb[:, col:col + 512],
                        op=ALU.mult))
                    k.op("vector", lambda v, col=col: v.tensor_tensor(
                        out=xo[:, col:col + 512], in0=xo[:, col:col + 512],
                        in1=tmp2[:], op=ALU.add), incs=[(k.svv, 1)])
                    x1v = k.svv.n
                rv_war = [(k.svv, x1v)]
            x1tok = k.store(
                x1T.rearrange("(kt p) n -> p kt n", kt=8, p=128),
                xo[:].rearrange("p (kt n) -> p kt n", kt=8),
                waits=[(k.svv, x1v)])
            k.barrier()

        xn1 = k.sb(sB, "xn1", (128, NKT * TOK), BF16)
        wa_sb = k.sb(sB, "wa_sb", (128, 8 * 256), BF16)
        wb_sb = k.sb(sB, "wb_sb", (128, 8 * 256), BF16)
        wab_sb = k.sb(sB, "wab_sb", (128, 2), F32)
        a_sb = k.sb(sB, "a_sb", (128, 2 * TOK), F32)
        b_sb = k.sb(sB, "b_sb", (128, 2 * TOK), F32)
        scr = k.sb(sB, "scr", (128, 2 * 256), F32)
        csb = k.sb(sB, "csb", (128, 32), F32)
        k.load(wab_sb[:], wabh)
        k.load(wa_sb[:].rearrange("p (kt m) -> p kt m", kt=8),
               waT.rearrange("(kt p) m -> p kt m", kt=8, p=128), "gpsimd")
        k.load(wb_sb[:].rearrange("p (kt m) -> p kt m", kt=8),
               wbT.rearrange("(kt p) m -> p kt m", kt=8, p=128), "gpsimd")
        base_ld = k.dma_all()

        env = ln_env(k, sB, ones128, onesrow, psS, psBr, psBm,
                     pre=[(k.svv, x1v)] + ones_rdy + x1tok, dt=F32)
        lnv = 0
        for g in range(4):
            lnv = emit_ln_group(k, xo, TOK, g, env)
        k.store(xn1T.rearrange("(kt p) n -> p kt n", kt=8, p=128),
                xo[:].rearrange("p (kt n) -> p kt n", kt=8),
                waits=[(k.svv, lnv)])
        for kt in range(NKT):
            k.op("scalar", lambda s, kt=kt: s.copy(
                out=xn1[:, kt * TOK:(kt + 1) * TOK],
                in_=xo[:, kt * TOK:(kt + 1) * TOK]),
                waits=[(k.svv, lnv)] if kt == 0 else (), incs=[(k.sev, 1)])
        cast_done = k.sev.n
        gwf = k.sb(sB, "gwf", (128, 8 * 8), F32)
        lg = k.sb(sB, "lg", (128, 16 * 8), F32)
        gwl = k.load(gwf[:].rearrange("p (kt e) -> p kt e", kt=8),
                     gateT.rearrange("(kt p) e -> p kt e", kt=8, p=128))
        lg_done = 0
        for tt in range(16):
            for kt in range(NKT):
                k.op("tensor", lambda t, kt=kt, tt=tt: t.matmul(
                    psL[:], xo[:, kt * TOK + tt * 128: kt * TOK + tt * 128 + 128],
                    gwf[:, kt * 8:(kt + 1) * 8],
                    start=(kt == 0), stop=(kt == NKT - 1)),
                    waits=(gwl + [(k.svv, lnv), (k.svv, lg_done)])
                    if kt == 0 else (),
                    incs=[(k.smm, 1)] if kt == NKT - 1 else ())
            mml = k.smm.n
            k.op("vector", lambda v, tt=tt: v.tensor_copy(
                out=lg[:, tt * 8:(tt + 1) * 8], in_=psL[:]),
                waits=[(k.smm, mml)], incs=[(k.svv, 1)])
            lg_done = k.svv.n
        k.store(logitsT.rearrange("(tt p) e -> p tt e", tt=16, p=128),
                lg[:].rearrange("p (tt e) -> p tt e", tt=16),
                waits=[(k.svv, lg_done)])

        accA = Acc([psA0, psA1])
        av = bv = 0
        for (wsb, dst, bias, isa) in ((wa_sb, a_sb, wab_sb, True),
                                      (wb_sb, b_sb, None, False)):
            for mt in range(2):
                for tg in range(4):
                    bank, war2, bj = accA.next()
                    for kt in range(NKT):
                        k.op("tensor", lambda t, kt=kt, mt=mt, tg=tg, w=wsb,
                             bank=bank: t.matmul(
                                 bank[:],
                                 w[:, kt * 256 + mt * 128:
                                   kt * 256 + mt * 128 + 128],
                                 xn1[:, kt * TOK + tg * 512:
                                     kt * TOK + tg * 512 + 512],
                                 start=(kt == 0), stop=(kt == NKT - 1)),
                             waits=(war2 + [(k.sev, cast_done)] + base_ld)
                             if kt == 0 else (),
                             incs=[(k.smm, 1)] if kt == NKT - 1 else ())
                    mma = k.smm.n
                    if isa:
                        k.op("scalar", lambda s, mt=mt, tg=tg, bank=bank:
                             s.activation(
                                 out=a_sb[:, mt * TOK + tg * 512:
                                          mt * TOK + tg * 512 + 512],
                                 in_=bank[:], func=AF.Sigmoid,
                                 bias=wab_sb[:, mt:mt + 1], scale=1.0),
                             waits=[(k.smm, mma)], incs=[(k.sev, 1)])
                        accA.done(bj, [(k.sev, k.sev.n)])
                        av = k.sev.n
                    else:
                        k.op("scalar", lambda s, mt=mt, tg=tg, bank=bank:
                             s.copy(out=b_sb[:, mt * TOK + tg * 512:
                                             mt * TOK + tg * 512 + 512],
                                    in_=bank[:]),
                             waits=[(k.smm, mma)], incs=[(k.sev, 1)])
                        accA.done(bj, [(k.sev, k.sev.n)])
                        bv = k.sev.n
        k.store(aTo.rearrange("(h p) n -> p h n", h=2, p=128),
                a_sb[:].rearrange("p (h n) -> p h n", h=2),
                waits=[(k.sev, av)])
        k.store(bTo.rearrange("(h p) n -> p h n", h=2, p=128),
                b_sb[:].rearrange("p (h n) -> p h n", h=2),
                waits=[(k.sev, bv)])

        scv = 0
        for slot in range(2):
            for half in range(2):
                for b in range(B):
                    t0 = half * TOK + b * 512 + slot * 256
                    colA = ((slot * 2 + 0) * 2 + half) * 4 + b
                    colB = ((slot * 2 + 1) * 2 + half) * 4 + b
                    k.op("vector", lambda v, t0=t0: v.tensor_tensor_scan(
                        out=scr[:, 0:256], data0=a_sb[:, t0:t0 + 256],
                        data1=a_sb[:, t0:t0 + 256], initial=1.0,
                        op0=ALU.mult, op1=ALU.bypass),
                        waits=[(k.sev, av), (k.svv, scv)])
                    k.op("vector", lambda v, t0=t0: v.tensor_tensor_scan(
                        out=scr[:, 256:512], data0=a_sb[:, t0:t0 + 256],
                        data1=b_sb[:, t0:t0 + 256], initial=0.0,
                        op0=ALU.mult, op1=ALU.add),
                        waits=[(k.sev, bv)])
                    k.op("vector", lambda v, colA=colA: v.tensor_copy(
                        out=csb[:, colA:colA + 1], in_=scr[:, 255:256]))
                    k.op("vector", lambda v, colB=colB: v.tensor_copy(
                        out=csb[:, colB:colB + 1], in_=scr[:, 511:512]),
                        incs=[(k.svv, 1)])
                    scv = k.svv.n
        for slot in range(2):
            for ab in range(2):
                for half in range(2):
                    c0 = ((slot * 2 + ab) * 2 + half) * 4
                    off = slot * 2048 + ab * 1024 + half * 512
                    k.dma_op("sync", lambda s, c0=c0, off=off: s.dma_start(
                        out=summ[0:1, off:off + 512]
                        .rearrange("one (p b) -> p (one b)", p=128),
                        in_=csb[:, c0:c0 + 4],
                        allow_slow_non_contiguous=True),
                        waits=[(k.svv, scv)])
        k.wait("sync", k.dma_all())
        k.wait("gpsimd", k.dma_all())
        sB.__exit__(None, None, None)
        k.barrier()


def build_C():
    nc = _new_nc()
    DI = lambda n, s, d=F32: nc.dram_tensor(n, list(s), d, kind="ExternalInput").ap()
    DO = lambda n, s, d=F32: nc.dram_tensor(n, list(s), d, kind="ExternalOutput").ap()
    io = {}
    io["x1T"] = DI("x1T", (D, TOK))
    io["xn1T"] = DI("xn1T", (D, TOK))
    io["aTo"] = DI("aTi", (CH, TOK))
    io["bTo"] = DI("bTi", (CH, TOK))
    io["summ_all"] = DI("summ_all", (NCORE, 4096))
    io["logitsT"] = DI("logitsT", (TOK, 8))
    for n_, sh in (("sel2h", (17, 2)), ("wc1T", (CH, D)), ("wdT", (D, D)),
                   ("w1", (8, D, 4096)), ("b1h", (128, 256)),
                   ("w2", (8, 4096, D)), ("b2", (8, D)), ("Esel", (8, 1024)),
                   ("id16", (128, 128))):
        io[n_] = DI(n_, sh)
    io["outT"] = DO("outT", (D, TOK))
    with ExitStack() as top:
        k = Kb(nc, top)
        block = top.enter_context(nc.Block())
        emit_C(k, top, io)
        k.flush(block)
    return nc


def emit_C(k, top, io):
    nc = k.nc
    x1T, xn1T, aTi, bTi, summ_all, logitsT = (
        io["x1T"], io["xn1T"], io["aTo"], io["bTo"], io["summ_all"],
        io["logitsT"])
    sel2h, wc1T, wdT, w1, b1h, w2, b2, Esel, id16 = (
        io[n_] for n_ in C_INPUTS)
    outT = io["outT"]
    summ_wait = io.get("summ_wait", [])
    import os
    _dbg = os.environ.get("KDBG") == "1" and io.get("dbg_ok", True)
    if _dbg:
        DO2 = lambda n, s, d=F32: nc.dram_tensor(
            n, list(s), d, kind="ExternalOutput").ap()
        accdbg = DO2("accdbg", (128, 8 * TOK))
        seldbg = DO2("seldbg", (8, TOK))
        pdbg = DO2("pdbg", (1, TOK))
        ydbg = DO2("ydbg", (D, TOK))
        hdbg = DO2("hdbg", (128, 2 * TOK))
        ldbg = DO2("ldbg", (128, 16 * 8))
        hindbg = DO2("hindbg", (128, 16))
        spdbg = DO2("spdbg", (128, 16 * 8))
        salldbg = DO2("salldbg", (128, 17 * 8))
        saltdbg = DO2("saltdbg", (17, 1024))
        h2dbg = DO2("h2dbg", (2, 1024))
        asdbg = DO2("asdbg", (128, 16 * 8))

    S_scr = nc.dram_tensor("S_scr", [17, 1024], F32).ap()
    H_scr = nc.dram_tensor("H_scr", [2, 1024], F32).ap()
    y_dram = nc.dram_tensor("y_dram", [D, TOK], F32).ap()
    if True:
        sC = ExitStack()
        sC.__enter__()
        xn1 = k.sb(sC, "xn1", (128, NKT * TOK), BF16)
        acc = k.sb(sC, "acc", (128, 8 * TOK), BF16)
        h_sb = k.sb(sC, "h_sb", (128, 2 * TOK), BF16)
        selT = k.sb(sC, "selT", (8, TOK), BF16)
        pT = k.sb(sC, "pT", (1, TOK), BF16)
        ones128 = k.sb(sC, "ones128", (128, 1), BF16)
        onesrow = k.sb(sC, "onesrow", (1, 128), BF16)
        id_sb = k.sb(sC, "id_sb", (128, 128), BF16)
        e_sb = k.sb(sC, "e_sb", (8, 1024), BF16)
        b1_sb = k.sb(sC, "b1_sb", (128, 256), F32)
        b2_sb = k.sb(sC, "b2_sb", (8, 1024), BF16)
        Hini = k.sb(sC, "Hini", (128, 16), F32)
        psP0 = k.ps(sC, "psP0", (128, 512))
        psP1 = k.ps(sC, "psP1", (128, 512))
        psQ0 = k.ps(sC, "psQ0", (128, 512))
        psQ1 = k.ps(sC, "psQ1", (128, 512))
        psT9 = k.ps(sC, "psT9", (9, 128), BF16)
        psSel = k.ps(sC, "psSel", (128, 512))
        psX = k.ps(sC, "psX", (128, 512))

        k.op("gpsimd", lambda g: g.memset(ones128[:], 1.0))
        k.op("gpsimd", lambda g: g.memset(onesrow[:], 1.0), incs=[(k.sgp, 1)])
        ones_rdy = [(k.sgp, k.sgp.n)]
        k.load(id_sb[:], id16, "gpsimd")
        k.load(e_sb[:], Esel, "gpsimd")
        k.load(b2_sb[:], b2, "gpsimd")
        k.load(b1_sb[:], b1h)
        k.load(xn1[:].rearrange("p (kt n) -> p kt n", kt=8),
               xn1T.rearrange("(kt p) n -> p kt n", kt=8, p=128), "gpsimd")
        base_ld = k.dma_all()

        with ExitStack() as s1:
            As = k.sb(s1, "As", (128, 16 * 8), F32)
            Bs = k.sb(s1, "Bs", (128, 16 * 8), F32)
            Scur = k.sb(s1, "Scur", (128, 8), F32)
            Sall = k.sb(s1, "Sall", (128, 17 * 8), F32)
            for i in range(16):
                r = i if i < 8 else 15 - i
                slot = 0 if i < 8 else 1
                base = slot * 2048
                k.load(As[:, i * 8:(i + 1) * 8]
                       .rearrange("p (half b) -> p half b", half=2),
                       summ_all[r, base:base + 1024]
                       .rearrange("(half p b) -> p half b", half=2, p=128),
                       slow=True, waits=summ_wait)
                k.load(Bs[:, i * 8:(i + 1) * 8]
                       .rearrange("p (half b) -> p half b", half=2),
                       summ_all[r, base + 1024:base + 2048]
                       .rearrange("(half p b) -> p half b", half=2, p=128),
                       slow=True)
            sld = k.dma_all("sync")
            k.op("vector", lambda v: v.memset(Sall[:, 16 * 8:17 * 8], 0.0))
            k.op("vector", lambda v: v.tensor_copy(out=Scur[:], in_=Bs[:, 0:8]),
                 waits=sld, incs=[(k.svv, 1)])
            k.op("vector", lambda v: v.tensor_copy(out=Sall[:, 0:8], in_=Scur[:]),
                 waits=[(k.svv, k.svv.n)], incs=[(k.svv, 1)])
            for i in range(1, 16):
                k.op("vector", lambda v, i=i: v.tensor_tensor(
                    out=Scur[:], in0=As[:, i * 8:(i + 1) * 8], in1=Scur[:],
                    op=ALU.mult), waits=[(k.svv, k.svv.n)], incs=[(k.svv, 1)])
                k.op("vector", lambda v, i=i: v.tensor_tensor(
                    out=Scur[:], in0=Scur[:], in1=Bs[:, i * 8:(i + 1) * 8],
                    op=ALU.add), waits=[(k.svv, k.svv.n)], incs=[(k.svv, 1)])
                k.op("vector", lambda v, i=i: v.tensor_copy(
                    out=Sall[:, i * 8:(i + 1) * 8], in_=Scur[:]),
                    waits=[(k.svv, k.svv.n)], incs=[(k.svv, 1)])
            cmv = k.svv.n
            stok = []
            for i in range(17):
                stok += k.dma_op("sync", lambda sy, i=i: sy.dma_start(
                    out=S_scr[i:i + 1, :]
                    .rearrange("one (half p b) -> p (one half) b",
                               half=2, p=128),
                    in_=Sall[:, i * 8:(i + 1) * 8]
                    .rearrange("p (half b) -> p half b", half=2),
                    allow_slow_non_contiguous=True),
                    waits=[(k.svv, cmv)] if i == 0 else ())
            sel2_sb = k.sb(s1, "sel2_sb", (17, 2), F32)
            sallT = k.sb(s1, "sallT", (17, 1024), F32)
            h2_sb = k.sb(s1, "h2_sb", (2, 1024), F32)
            seltok = k.load(sel2_sb[:], sel2h)
            k.load(sallT[:], S_scr, waits=stok)
            saltok = k.dma_all("sync")
            h2ev = 0
            for ng in range(2):
                k.op("tensor", lambda t, ng=ng: t.matmul(
                    psSel[0:2, :],
                    sel2_sb[:], sallT[:, ng * 512:(ng + 1) * 512],
                    start=True, stop=True),
                    waits=((saltok + seltok) if ng == 0
                           else [(k.svv, h2ev)]),
                    incs=[(k.smm, 1)])
                mmh = k.smm.n
                k.op("vector", lambda v, ng=ng: v.tensor_copy(
                    out=h2_sb[:, ng * 512:(ng + 1) * 512], in_=psSel[0:2, :]),
                    waits=[(k.smm, mmh)], incs=[(k.svv, 1)])
                h2ev = k.svv.n
            h2v = k.svv.n
            if _dbg:
                k.store(salldbg, Sall[:], waits=[(k.svv, cmv)])
                k.store(asdbg, As[:], waits=[(k.svv, cmv)])
                k.store(saltdbg, sallT[:], waits=[(k.svv, h2v)])
                k.store(h2dbg, h2_sb[:], waits=[(k.svv, h2v)])
            htok = k.dma_op("sync", lambda sy: sy.dma_start(
                out=H_scr, in_=h2_sb[:]), waits=[(k.svv, h2v)])
            hini_ld = []
            for slot in range(2):
                for half in range(2):
                    hini_ld += k.load(
                        Hini[:, (slot * 2 + half) * 4:
                             (slot * 2 + half) * 4 + 4],
                        H_scr[slot, half * 512: half * 512 + 512]
                        .rearrange("(p b) -> p b", p=128),
                        "gpsimd", waits=htok, slow=True)
            if _dbg:
                k.store(hindbg, Hini[:], waits=hini_ld)
            k.barrier()

        with ExitStack() as s2:
            a_sb = k.sb(s2, "a_sb", (128, 2 * TOK), F32)
            b_sb = k.sb(s2, "b_sb", (128, 2 * TOK), F32)
            k.load(a_sb[:].rearrange("p (h n) -> p h n", h=2),
                   aTi.rearrange("(h p) n -> p h n", h=2, p=128))
            k.load(b_sb[:].rearrange("p (h n) -> p h n", h=2),
                   bTi.rearrange("(h p) n -> p h n", h=2, p=128))
            abl = k.dma_all("sync")
            scv = 0
            for slot in range(2):
                for half in range(2):
                    for b in range(B):
                        t0 = half * TOK + b * 512 + slot * 256
                        col = (slot * 2 + half) * 4 + b
                        k.op("vector", lambda v, t0=t0, col=col:
                             v.tensor_tensor_scan(
                                 out=h_sb[:, t0:t0 + 256],
                                 data0=a_sb[:, t0:t0 + 256],
                                 data1=b_sb[:, t0:t0 + 256],
                                 initial=Hini[:, col:col + 1],
                                 op0=ALU.mult, op1=ALU.add),
                             waits=(abl + hini_ld) if scv == 0 else (),
                             incs=[(k.svv, 1)])
                        scv = k.svv.n
            hv = scv

            wc_sb = k.sb(s2, "wc_sb", (128, 2 * 8 * 128), BF16)
            wdr = k.sb(s2, "wdr", (128, 2 * 1024), BF16)
            ystg = k.sb(s2, "ystg", (128, 2 * 512), F32)
            k.load(wc_sb[:].rearrange("p (kt m) -> p kt m", kt=2),
                   wc1T.rearrange("(kt p) m -> p kt m", kt=2, p=128), "gpsimd")
            wcl = k.dma_all("gpsimd")
            accY = Acc([psQ0, psQ1])
            wd_war = [[], []]
            ystg_war = [[], []]
            ytoks = []
            for mt in range(8):
                wslot = mt % 2
                ldd = k.load(wdr[:, wslot * 1024:(wslot + 1) * 1024]
                             .rearrange("p (kt m) -> p kt m", kt=8),
                             wdT.rearrange("(kt p) m -> p kt m", kt=8, p=128)
                             [:, :, mt * 128:(mt + 1) * 128], "gpsimd",
                             waits=wd_war[wslot])
                for tg in range(4):
                    bank, war2, bj = accY.next()
                    for kt in range(2):
                        k.op("tensor", lambda t, kt=kt, mt=mt, tg=tg, bank=bank:
                             t.matmul(bank[:],
                                      wc_sb[:, (kt * 8 + mt) * 128:
                                            (kt * 8 + mt) * 128 + 128],
                                      h_sb[:, kt * TOK + tg * 512:
                                           kt * TOK + tg * 512 + 512],
                                      start=(kt == 0), stop=False),
                             waits=(war2 + wcl + ldd + [(k.svv, hv)] + base_ld)
                             if kt == 0 else ())
                    for kt in range(NKT):
                        k.op("tensor", lambda t, kt=kt, wslot=wslot, tg=tg,
                             bank=bank: t.matmul(
                                 bank[:],
                                 wdr[:, wslot * 1024 + kt * 128:
                                     wslot * 1024 + kt * 128 + 128],
                                 xn1[:, kt * TOK + tg * 512:
                                     kt * TOK + tg * 512 + 512],
                                 start=False, stop=(kt == NKT - 1)),
                             incs=[(k.smm, 1)] if kt == NKT - 1 else ())
                    mmy = k.smm.n
                    if tg == 3:
                        wd_war[wslot] = [(k.smm, mmy)]
                    sg = ystg[:, (tg % 2) * 512:(tg % 2) * 512 + 512]
                    k.op("vector", lambda v, sg=sg, bank=bank: v.tensor_copy(
                        out=sg, in_=bank[:]),
                        waits=[(k.smm, mmy)] + ystg_war[tg % 2],
                        incs=[(k.svv, 1)])
                    accY.done(bj, [(k.svv, k.svv.n)])
                    vvy = k.svv.n
                    ystg_war[tg % 2] = k.store(
                        y_dram[mt * 128:(mt + 1) * 128,
                               tg * 512:(tg + 1) * 512], sg,
                        waits=[(k.svv, vvy)])
                    if _dbg:
                        ystg_war[tg % 2] = ystg_war[tg % 2] + k.store(
                            ydbg[mt * 128:(mt + 1) * 128,
                                 tg * 512:(tg + 1) * 512], sg,
                            waits=[(k.svv, vvy)], queue="gpsimd")
                    ytoks += ystg_war[tg % 2]
            ydone = list(ytoks)
            k.barrier()

        s45 = ExitStack()
        s45.__enter__()
        h1T = k.sb(s45, "h1T", (128, 32 * 1024), BF16)
        w1r = k.sb(s45, "w1r", (128, 4 * 1024), BF16)
        w2r = k.sb(s45, "w2r", (128, 2 * 4096), BF16)
        selBu = k.sb(s45, "selBu", (128, TOK), U8)
        l_sb = k.sb(s45, "l_sb", (128, 16 * 8), F32)
        with ExitStack() as s4:
            spd = k.sb(s4, "spd", (128, 16 * 8), F32) if _dbg else None
            nm = k.sb(s4, "nm", (128, 1), F32)
            eqs = k.sb(s4, "eqs", (128, 8), F32)
            prf = k.sb(s4, "prf", (128, 8), F32)
            esm = k.sb(s4, "esm", (128, 8), F32)
            srw = k.sb(s4, "srw", (128, 1), F32)
            prw = k.sb(s4, "prw", (128, 1), F32)
            rem = k.sb(s4, "rem", (128, 1), F32)
            eq = k.sb(s4, "eq", (128, 1), F32)
            selc = k.sb(s4, "selc", (128, 1), F32)
            sp16 = k.sb(s4, "sp16", (128, 8), BF16)
            p16 = k.sb(s4, "p16", (128, 1), BF16)
            lgl = k.load(l_sb[:].rearrange("p (tt e) -> p tt e", tt=16),
                         logitsT.rearrange("(tt p) e -> p tt e", tt=16, p=128))
            tr_war = []
            tr_done = 0
            for tt in range(16):
                lsl = l_sb[:, tt * 8:(tt + 1) * 8]
                k.op("vector", lambda v, lsl=lsl: v.reduce_max(
                    out=nm[:], in_=lsl, axis=AX, negate=True),
                    waits=lgl + [(k.smm, tr_done)], incs=[(k.svv, 1)])
                nmv = k.svv.n
                k.op("scalar", lambda s, lsl=lsl: s.activation(
                    out=esm[:], in_=lsl, func=AF.Exp, bias=nm[:], scale=1.0),
                    waits=[(k.svv, nmv)], incs=[(k.sev, 1)])
                esv = k.sev.n
                se = lambda: [(k.svv, k.svv.n)]
                k.op("vector", lambda v: v.reduce_sum(
                    out=srw[:], in_=esm[:], axis=AX), waits=[(k.sev, esv)],
                    incs=[(k.svv, 1)])
                k.op("vector", lambda v: v.reciprocal(out=prw[:], in_=srw[:]),
                     waits=se(), incs=[(k.svv, 1)])
                k.op("vector", lambda v: v.tensor_copy(out=p16[:], in_=prw[:]),
                     waits=se(), incs=[(k.svv, 1)])
                k.op("vector", lambda v, lsl=lsl: v.tensor_scalar(
                    out=eqs[:], in0=lsl, scalar1=nm[:], scalar2=0.0,
                    op0=ALU.add, op1=ALU.is_equal), waits=se(),
                    incs=[(k.svv, 1)])
                k.op("vector", lambda v: v.tensor_tensor_scan(
                    out=prf[:], data0=eqs[:], data1=eqs[:], initial=0.0,
                    op0=ALU.add, op1=ALU.bypass), waits=se(),
                    incs=[(k.svv, 1)])
                k.op("vector", lambda v: v.tensor_scalar(
                    out=prf[:], in0=prf[:], scalar1=1.0, scalar2=None,
                    op0=ALU.is_equal), waits=se(), incs=[(k.svv, 1)])
                k.op("vector", lambda v: v.tensor_tensor(
                    out=eqs[:], in0=eqs[:], in1=prf[:], op=ALU.mult),
                    waits=se(), incs=[(k.svv, 1)])
                k.op("vector", lambda v: v.tensor_copy(out=sp16[:], in_=eqs[:]),
                     waits=se(), incs=[(k.svv, 1)])
                spv = k.svv.n
                if _dbg:
                    k.op("vector", lambda v, tt=tt: v.tensor_copy(
                        out=spd[:, tt * 8:(tt + 1) * 8], in_=sp16[:]),
                        incs=[(k.svv, 1)])
                    spv = k.svv.n
                k.op("tensor", lambda t: t.transpose(
                    psT9[0:8, :], sp16[:], id_sb[:]),
                    waits=[(k.svv, spv)] + tr_war, incs=[(k.smm, 1)])
                mmt1 = k.smm.n
                k.op("vector", lambda v, tt=tt: v.tensor_copy(
                    out=selT[0:8, tt * 128:(tt + 1) * 128], in_=psT9[0:8, :]),
                    waits=[(k.smm, mmt1)], incs=[(k.svv, 1)])
                selv = k.svv.n
                k.op("tensor", lambda t: t.transpose(
                    psT9[0:1, :], p16[:], id_sb[:]),
                    waits=[(k.svv, selv)], incs=[(k.smm, 1)])
                mmt2 = k.smm.n
                k.op("vector", lambda v, tt=tt: v.tensor_copy(
                    out=pT[0:1, tt * 128:(tt + 1) * 128], in_=psT9[0:1, :]),
                    waits=[(k.smm, mmt2)], incs=[(k.svv, 1)])
                tr_war = [(k.svv, k.svv.n)]
                tr_done = mmt2
            selv_all = k.svv.n
            if False and _dbg:
                k.store(ldbg, l_sb[:], queue="gpsimd",
                        waits=[(k.svv, selv_all)])
                k.store(spdbg, spd[:], queue="gpsimd",
                        waits=[(k.svv, selv_all)])

        with ExitStack() as s5:
            accM = Acc([psP0, psP1])
            accO = Acc([psQ0, psQ1], init_war=[(k.svv, k.svv.n)])
            w1_war = [[] for _ in range(4)]
            w2_war = [[], []]
            selB_war = []
            h1_war = {}
            for e in range(8):
                sbv = 0
                for tg in range(4):
                    k.op("tensor", lambda t, e=e, tg=tg: t.matmul(
                        psSel[:], e_sb[0:8, e * 128:(e + 1) * 128],
                        selT[0:8, tg * 512:(tg + 1) * 512],
                        start=True, stop=True),
                        waits=(([(k.svv, selv_all)] + base_ld) if tg == 0
                               else []) + selB_war, incs=[(k.smm, 1)])
                    mms = k.smm.n
                    k.op("vector", lambda v, tg=tg: v.tensor_copy(
                        out=selBu[:, tg * 512:(tg + 1) * 512], in_=psSel[:]),
                        waits=[(k.smm, mms)], incs=[(k.svv, 1)])
                    sbv = k.svv.n
                    selB_war = [(k.svv, sbv)]
                for th in range(2):
                    for hm in range(32):
                        ws = hm % 4
                        ld1 = k.load(
                            w1r[:, ws * 1024:(ws + 1) * 1024]
                            .rearrange("p (kt m) -> p kt m", kt=8),
                            w1[e].rearrange("(kt p) m -> p kt m", kt=8, p=128)
                            [:, :, hm * 128:(hm + 1) * 128], "gpsimd",
                            waits=w1_war[ws])
                        for ng in range(2):
                            bank, war2, bj = accM.next()
                            for kt in range(NKT):
                                k.op("tensor", lambda t, kt=kt, ws=ws, th=th,
                                     ng=ng, bank=bank: t.matmul(
                                         bank[:],
                                         w1r[:, ws * 1024 + kt * 128:
                                             ws * 1024 + kt * 128 + 128],
                                         xn1[:, kt * TOK + th * 1024 +
                                             ng * 512: kt * TOK + th * 1024 +
                                             ng * 512 + 512],
                                         start=(kt == 0),
                                         stop=(kt == NKT - 1)),
                                     waits=(ld1 + war2) if kt == 0 else (),
                                     incs=[(k.smm, 1)] if kt == NKT - 1 else ())
                            mm1 = k.smm.n
                            if ng == 1:
                                w1_war[ws] = [(k.smm, mm1)]
                            w = [(k.smm, mm1)] + h1_war.get((hm, ng), base_ld)
                            k.op("scalar", lambda s, e=e, hm=hm, ng=ng,
                                 bank=bank: s.activation(
                                     out=h1T[:, hm * 1024 + ng * 512:
                                             hm * 1024 + ng * 512 + 512],
                                     in_=bank[:], func=AF.Relu,
                                     bias=b1_sb[:, e * 32 + hm:
                                                e * 32 + hm + 1],
                                     scale=1.0),
                                 waits=w, incs=[(k.sev, 1)])
                            accM.done(bj, [(k.sev, k.sev.n)])
                    h1v = k.sev.n
                    for dm in range(8):
                        ws = dm % 2
                        ld2 = k.load(
                            w2r[:, ws * 4096:(ws + 1) * 4096]
                            .rearrange("p (kt m) -> p kt m", kt=32),
                            w2[e].rearrange("(kt p) m -> p kt m", kt=32, p=128)
                            [:, :, dm * 128:(dm + 1) * 128], "gpsimd",
                            waits=w2_war[ws])
                        for ng in range(2):
                            bank, war2, bj = accO.next()
                            for kt in range(32):
                                k.op("tensor", lambda t, kt=kt, ws=ws, ng=ng,
                                     bank=bank: t.matmul(
                                         bank[:],
                                         w2r[:, ws * 4096 + kt * 128:
                                             ws * 4096 + kt * 128 + 128],
                                         h1T[:, kt * 1024 + ng * 512:
                                             kt * 1024 + ng * 512 + 512],
                                         start=(kt == 0), stop=(kt == 31)),
                                     waits=(ld2 + war2 + [(k.sev, h1v)])
                                     if kt == 0 else (),
                                     incs=[(k.smm, 1)] if kt == 31 else ())
                            mm2 = k.smm.n
                            if ng == 1:
                                w2_war[ws] = [(k.smm, mm2)]
                            col = th * 1024 + ng * 512
                            k.op("vector", lambda v, dm=dm, col=col, bank=bank:
                                 v.copy_predicated(
                                     out=acc[:, dm * TOK + col:
                                             dm * TOK + col + 512],
                                     mask=selBu[:, col:col + 512],
                                     data=bank[:]),
                                 waits=[(k.smm, mm2), (k.svv, sbv)],
                                 incs=[(k.svv, 1)])
                            accO.done(bj, [(k.svv, k.svv.n)])
                    for hm in range(32):
                        for ng in range(2):
                            h1_war[(hm, ng)] = [(k.smm, k.smm.n)]
            accv = k.svv.n
        s45.__exit__(None, None, None)
        if True:
            k.barrier()

        with ExitStack() as s6:
            x1r = k.sb(s6, "x1r", (128, 2 * 512), F32)
            yr = k.sb(s6, "yr", (128, 2 * 512), F32)
            t1 = k.sb(s6, "t1", (128, 512), F32)
            t2 = k.sb(s6, "t2", (128, 512), F32)
            t3 = k.sb(s6, "t3", (128, 512), F32)
            ost = k.sb(s6, "ost", (128, 2 * 512), F32)
            xy_war = [[], []]
            ost_war = [[], []]
            accB2 = Acc([psSel, psX], init_war=[(k.svv, accv)])
            for dm in range(8):
                for tg in range(4):
                    sl = tg % 2
                    ldx = k.load(x1r[:, sl * 512: sl * 512 + 512],
                                 x1T[dm * 128:(dm + 1) * 128,
                                     tg * 512:(tg + 1) * 512],
                                 waits=xy_war[sl])
                    ldy = k.load(yr[:, sl * 512: sl * 512 + 512],
                                 y_dram[dm * 128:(dm + 1) * 128,
                                        tg * 512:(tg + 1) * 512],
                                 waits=ydone)
                    bank, war2, bj = accB2.next()
                    k.op("tensor", lambda t, dm=dm, tg=tg, bank=bank: t.matmul(
                        bank[:], b2_sb[0:8, dm * 128:(dm + 1) * 128],
                        selT[0:8, tg * 512:(tg + 1) * 512],
                        start=True, stop=True),
                        waits=war2 + [(k.svv, accv)], incs=[(k.smm, 1)])
                    mmb2 = k.smm.n
                    k.op("vector", lambda v, dm=dm, tg=tg, bank=bank:
                         v.tensor_tensor(
                             out=t1[:], in0=acc[:, dm * TOK + tg * 512:
                                               dm * TOK + tg * 512 + 512],
                             in1=bank[:], op=ALU.add),
                         waits=[(k.smm, mmb2)], incs=[(k.svv, 1)])
                    accB2.done(bj, [(k.svv, k.svv.n)])
                    k.op("tensor", lambda t, tg=tg: t.matmul(
                        psQ0[:], onesrow[:], pT[0:1, tg * 512:(tg + 1) * 512],
                        start=True, stop=True),
                        waits=[(k.svv, k.svv.n)], incs=[(k.smm, 1)])
                    mmpb = k.smm.n
                    k.op("vector", lambda v: v.tensor_tensor(
                        out=t2[:], in0=t1[:], in1=psQ0[:], op=ALU.mult),
                        waits=[(k.smm, mmpb)], incs=[(k.svv, 1)])
                    vt2 = k.svv.n
                    k.op("scalar", lambda s: s.activation(
                        out=t3[:], in_=t2[:], func=AF.Sigmoid),
                        waits=[(k.svv, vt2)], incs=[(k.sev, 1)])
                    evs = k.sev.n
                    k.op("vector", lambda v: v.tensor_tensor(
                        out=t2[:], in0=t2[:], in1=t3[:], op=ALU.mult),
                        waits=[(k.sev, evs)])
                    k.op("vector", lambda v, sl=sl: v.tensor_tensor(
                        out=t2[:], in0=t2[:], in1=yr[:, sl * 512: sl * 512 + 512],
                        op=ALU.mult), waits=ldy)
                    k.op("vector", lambda v, sl=sl: v.tensor_tensor(
                        out=ost[:, sl * 512: sl * 512 + 512],
                        in0=t2[:], in1=x1r[:, sl * 512: sl * 512 + 512],
                        op=ALU.add),
                        waits=ldx + ost_war[sl], incs=[(k.svv, 1)])
                    xy_war[sl] = [(k.svv, k.svv.n)]
                    vvo = k.svv.n
                    ost_war[sl] = k.store(
                        outT[dm * 128:(dm + 1) * 128, tg * 512:(tg + 1) * 512],
                        ost[:, sl * 512: sl * 512 + 512], waits=[(k.svv, vvo)])
            if _dbg:
                k.store(accdbg, acc[:], queue="gpsimd")
                k.store(seldbg, selT[:], queue="gpsimd")
                k.store(pdbg, pT[:], queue="gpsimd")
                k.store(hdbg, h_sb[:], queue="gpsimd")
                pass
            k.wait("sync", k.dma_all())
            k.wait("gpsimd", k.dma_all())
        sC.__exit__(None, None, None)


def build_F():
    nc = _new_nc()
    DI = lambda n, s, d=F32: nc.dram_tensor(n, list(s), d, kind="ExternalInput").ap()
    DO = lambda n, s, d=F32: nc.dram_tensor(n, list(s), d, kind="ExternalOutput").ap()
    NI = lambda n, s, d=F32: nc.dram_tensor(n, list(s), d).ap()
    io = {}
    for n_, sh in (("xbatT", (D, L)), ("xownT", (D, TOK)), ("wqT", (D, 512)),
                   ("wkT", (D, 512)), ("wvT", (D, 512)), ("bqh", (128, 4)),
                   ("bkh", (128, 4)), ("bvrow", (1, 512)), ("woT", (512, D)),
                   ("scaleh", (128, 1)), ("cosT", (128, L)),
                   ("sinST", (128, L)), ("Psw", (128, 128)),
                   ("band", (4 * 128, 512)), ("w3T", (D, 3072)),
                   ("u3T", (D, 3072)), ("b3h", (128, 24)), ("wo0T", (D, D)),
                   ("uo0T", (D, D)), ("bo0h", (128, 8)), ("inch", (128, 8)),
                   ("gateT", (D, 8)), ("boh", (128, 8)), ("waT", (D, CH)),
                   ("wabh", (128, 2)), ("wbT", (D, CH)), ("sel2h", (17, 2)),
                   ("wc1T", (CH, D)), ("wdT", (D, D)), ("w1", (8, D, 4096)),
                   ("b1h", (128, 256)), ("w2", (8, 4096, D)), ("b2", (8, D)),
                   ("Esel", (8, 1024)), ("id16", (128, 128))):
        io[n_] = DI(n_, sh)
    io["outT"] = DO("outT", (D, TOK))
    io["a2a_in"] = NI("fa2a_in", (NCORE, D, 512))
    a2a_out = NI("fa2a_out", (NCORE, D, 512))
    io["gT"] = NI("fgT", (D, TOK))
    io["x1T"] = NI("fx1T", (D, TOK))
    io["xn1T"] = NI("fxn1T", (D, TOK))
    io["aTo"] = NI("faT", (CH, TOK))
    io["bTo"] = NI("fbT", (CH, TOK))
    io["summ"] = NI("fsumm", (1, 4096))
    summ_all = NI("fsumm_all", (NCORE, 4096))
    io["summ_all"] = summ_all
    io["logitsT"] = NI("flogitsT", (TOK, 8))
    io["recv"] = a2a_out
    io["partialT"] = None
    io["dbg_ok"] = False

    with ExitStack() as top:
        k = Kb(nc, top)
        ccA = Sem(top, nc, "ccA")
        ccS = Sem(top, nc, "ccS")
        block = top.enter_context(nc.Block())
        emit_A(k, top, io)
        k.op("gpsimd", lambda g: g.collective_compute(
            "AllToAll", ALU.bypass, replica_groups=[list(range(NCORE))],
            ins=[io["a2a_in"]], outs=[a2a_out]),
            waits=k.dma_all(), incs=[(ccA, 1)])
        io["recv_wait"] = [(ccA, 1)]
        emit_B(k, top, io)
        k.op("gpsimd", lambda g: g.collective_compute(
            "AllGather", ALU.bypass, replica_groups=[list(range(NCORE))],
            ins=[io["summ"]], outs=[summ_all]),
            waits=k.dma_all(), incs=[(ccS, 1)])
        io["summ_wait"] = [(ccS, 1)]
        emit_C(k, top, io)
        with allow_ici_all_to_all():
            k.flush(block)
    return nc


_cache = {}


def _perm64():
    p = []
    for h in range(16):
        p += [64 * h + 2 * i for i in range(32)]
        p += [64 * h + 2 * i + 1 for i in range(32)]
    return np.array(p)


def _rope_tables():
    freqs = 10000.0 ** (-np.arange(0, 64, 2, dtype=np.float64) / 64.0)
    ang = freqs[:, None] * np.arange(L, dtype=np.float64)[None, :]
    c32 = np.cos(ang)
    s32 = np.sin(ang)
    cos128 = np.empty((128, L), np.float32)
    sinS = np.empty((128, L), np.float32)
    for a in range(2):
        cos128[64 * a:64 * a + 32] = c32
        cos128[64 * a + 32:64 * a + 64] = c32
        sinS[64 * a:64 * a + 32] = -s32
        sinS[64 * a + 32:64 * a + 64] = s32
    return cos128, sinS


def _psw():
    P = np.zeros((128, 128), np.float32)
    for a in range(2):
        for i in range(32):
            P[64 * a + i, 64 * a + 32 + i] = 1.0
            P[64 * a + 32 + i, 64 * a + i] = 1.0
    return P


def _band():
    m = np.zeros((4, 128, 512), np.float32)
    for o in range(4):
        for pk in range(128):
            m[o, pk, o * 128 + pk:] = 1.0
    return m.reshape(4 * 128, 512)


def _col128(v):
    return np.ascontiguousarray(v.reshape(-1, 128).T.astype(np.float32))


def _prep_A_inputs(inputs):
    x = np.asarray(inputs["L0_attn_x"], np.float32)
    f32 = lambda a: np.ascontiguousarray(np.asarray(a, np.float32))

    perm = _perm64()
    Wq_p = f32(inputs["L0_attn_Wq"])[perm]
    Wk_p = f32(inputs["L0_attn_Wk"])[perm]
    bq_p = f32(inputs["L0_attn_bq"])[perm]
    bk_p = f32(inputs["L0_attn_bk"])[perm]
    Wv = f32(inputs["L0_attn_Wv"])
    bv = f32(inputs["L0_attn_bv"])
    Wo = f32(inputs["L0_attn_Wo"])
    bo = f32(inputs["L0_attn_bo"])
    scaleh = np.ascontiguousarray(
        np.broadcast_to(f32(inputs["L0_attn_scale"]).reshape(1, 1), (128, 1)))
    cos128, sinS = _rope_tables()
    band = _band()
    Psw = _psw()

    w3 = np.concatenate([f32(inputs["L0_ffn_wf"]), f32(inputs["L0_ffn_wi"]),
                         f32(inputs["L0_ffn_wc"])], 0)
    u3 = np.concatenate([f32(inputs["L0_ffn_uf"]), f32(inputs["L0_ffn_ui"]),
                         f32(inputs["L0_ffn_uc"])], 0)
    b3 = np.concatenate([f32(inputs["L0_ffn_bf"]), f32(inputs["L0_ffn_bi"]),
                         f32(inputs["L0_ffn_bc"])], 0)
    wo0 = f32(inputs["L0_ffn_wo"])
    uo0 = f32(inputs["L0_ffn_uo"])
    bo0 = f32(inputs["L0_ffn_bo"])
    in_c = f32(inputs["L0_ffn_in_c"])

    Wa = f32(inputs["L1_attn_Wa_w"])
    Wab = f32(inputs["L1_attn_Wa_b"])
    Bw = f32(inputs["L1_attn_B_w"])
    Cw = f32(inputs["L1_attn_C_w"])
    Dw = f32(inputs["L1_attn_D_w"])
    gate_w = f32(inputs["L1_ffn_gate_w"])
    w1 = f32(inputs["L1_ffn_w1"])
    b1 = f32(inputs["L1_ffn_b1"])
    w2 = f32(inputs["L1_ffn_w2"])
    b2 = f32(inputs["L1_ffn_b2"])

    Esel = np.zeros((8, 1024), np.float32)
    for e in range(8):
        Esel[e, e * 128:(e + 1) * 128] = 1.0
    id16 = np.eye(128, dtype=np.float32)

    chunks = [(c, 15 - c) for c in range(NCORE)]

    w3T = f32(w3.T)
    u3T = f32(u3.T)
    wo0T = f32(wo0.T)
    uo0T = f32(uo0.T)
    WqT = f32(Wq_p.T)
    WkT = f32(Wk_p.T)
    WvT = f32(Wv.T)
    WoT = f32(Wo.T)
    inA = []
    for c in range(NCORE):
        b0, hh = c // 2, c % 2
        xbatT = f32(x[b0].T)
        xownT = np.empty((D, TOK), np.float32)
        for bb in range(B):
            for slot, ck in enumerate(chunks[c]):
                xownT[:, bb * 512 + slot * 256: bb * 512 + (slot + 1) * 256] = \
                    x[bb, ck * 256:(ck + 1) * 256, :].T
        inA.append(dict(
            xbatT=xbatT, xownT=xownT,
            wqT=f32(WqT[:, 512 * hh:512 * (hh + 1)]),
            wkT=f32(WkT[:, 512 * hh:512 * (hh + 1)]),
            wvT=f32(WvT[:, 512 * hh:512 * (hh + 1)]),
            bqh=_col128(bq_p[512 * hh:512 * (hh + 1)]),
            bkh=_col128(bk_p[512 * hh:512 * (hh + 1)]),
            bvrow=f32(bv[512 * hh:512 * (hh + 1)]).reshape(1, 512),
            woT=f32(WoT[512 * hh:512 * (hh + 1), :]),
            scaleh=scaleh, cosT=cos128, sinST=sinS, Psw=Psw, band=band,
            w3T=w3T, u3T=u3T, b3h=_col128(b3), wo0T=wo0T, uo0T=uo0T,
            bo0h=_col128(bo0), inch=_col128(in_c)))
    return inA, locals()


def _run_A_only(inputs):
    inA, _ = _prep_A_inputs(inputs)
    ncA = _cache.get("A")
    if ncA is None:
        ncA = _cache["A"] = build_A()
    resA = run_bass_kernel_spmd(ncA, inA, core_ids=list(range(NCORE)))
    return resA.results


def kernel(**inputs):
    inA, env = _prep_A_inputs(inputs)
    f32 = env["f32"]
    bo = env["bo"]
    Wa, Wab, Bw, Cw, Dw = env["Wa"], env["Wab"], env["Bw"], env["Cw"], env["Dw"]
    gate_w, w1, b1, w2, b2 = (env["gate_w"], env["w1"], env["b1"], env["w2"],
                              env["b2"])
    Esel, id16, chunks = env["Esel"], env["id16"], env["chunks"]
    WaT = f32(Wa.T)
    WbT = f32(Bw.T)
    wc1T = f32(Cw.T)
    wdT = f32(Dw.T)
    gateTT = f32(gate_w.T)
    b1h = np.ascontiguousarray(
        b1.reshape(8, 32, 128).transpose(2, 0, 1).reshape(128, 256)
        .astype(np.float32))
    boh = _col128(bo)
    wabh = _col128(Wab)
    inF = []
    for c in range(NCORE):
        sel2 = np.zeros((17, 2), np.float32)
        sel2[c - 1 if c >= 1 else 16, 0] = 1.0
        sel2[14 - c, 1] = 1.0
        m = dict(inA[c])
        m.update(gateT=gateTT, boh=boh, waT=WaT, wabh=wabh, wbT=WbT,
                 sel2h=sel2, wc1T=wc1T, wdT=wdT, w1=w1, b1h=b1h, w2=w2,
                 b2=b2, Esel=Esel, id16=id16)
        inF.append(m)
    ncF = _cache.get("F")
    if ncF is None:
        ncF = _cache["F"] = build_F()
    resF = run_bass_kernel_spmd(ncF, inF, core_ids=list(range(NCORE)))

    out = np.empty((B, L, D), np.float32)
    for c in range(NCORE):
        oT = resF.results[c]["outT"]
        for bb in range(B):
            for slot, ck in enumerate(chunks[c]):
                out[bb, ck * 256:(ck + 1) * 256, :] = \
                    oT[:, bb * 512 + slot * 256: bb * 512 + (slot + 1) * 256].T
    return out


def kernel3(**inputs):
    inA, env = _prep_A_inputs(inputs)
    x = env["x"]
    f32 = env["f32"]
    bo = env["bo"]
    Wa, Wab, Bw, Cw, Dw = env["Wa"], env["Wab"], env["Bw"], env["Cw"], env["Dw"]
    gate_w, w1, b1, w2, b2 = (env["gate_w"], env["w1"], env["b1"], env["w2"],
                              env["b2"])
    Esel, id16, chunks = env["Esel"], env["id16"], env["chunks"]
    ncA = _cache.get("A")
    if ncA is None:
        ncA = _cache["A"] = build_A()
    resA = run_bass_kernel_spmd(ncA, inA, core_ids=list(range(NCORE)))
    partial = [resA.results[c]["partialT"] for c in range(NCORE)]
    gouts = [resA.results[c]["gT"] for c in range(NCORE)]

    WaT = f32(Wa.T)
    WbT = f32(Bw.T)
    inB = []
    for c in range(NCORE):
        cols = np.concatenate([np.arange(ck * 256, (ck + 1) * 256)
                               for ck in chunks[c]])
        recv = np.stack([np.ascontiguousarray(partial[i][:, cols])
                         for i in range(NCORE)])
        inB.append(dict(
            xownT=inA[c]["xownT"], gT=gouts[c], recv=recv, boh=_col128(bo),
            waT=WaT, wabh=_col128(Wab), wbT=WbT, gateT=f32(gate_w.T)))
    ncB = _cache.get("B")
    if ncB is None:
        ncB = _cache["B"] = build_B()
    resB = run_bass_kernel_spmd(ncB, inB, core_ids=list(range(NCORE)))
    summ_all = np.stack([resB.results[c]["summ"][0] for c in range(NCORE)])

    wc1T = f32(Cw.T)
    wdT = f32(Dw.T)
    b1h = np.ascontiguousarray(
        b1.reshape(8, 32, 128).transpose(2, 0, 1).reshape(128, 256)
        .astype(np.float32))
    inC = []
    for c in range(NCORE):
        sel2 = np.zeros((17, 2), np.float32)
        sel2[c - 1 if c >= 1 else 16, 0] = 1.0
        sel2[14 - c, 1] = 1.0
        inC.append(dict(
            x1T=resB.results[c]["x1T"], xn1T=resB.results[c]["xn1T"],
            aTi=resB.results[c]["aTo"], bTi=resB.results[c]["bTo"],
            summ_all=summ_all, sel2h=sel2, wc1T=wc1T, wdT=wdT,
            logitsT=resB.results[c]["logitsT"],
            w1=w1, b1h=b1h, w2=w2, b2=b2, Esel=Esel, id16=id16))
    ncC = _cache.get("C")
    if ncC is None:
        ncC = _cache["C"] = build_C()
    resC = run_bass_kernel_spmd(ncC, inC, core_ids=list(range(NCORE)))
    _cache["lastC"] = resC
    _cache["lastB"] = resB

    out = np.empty((B, L, D), np.float32)
    for c in range(NCORE):
        oT = resC.results[c]["outT"]
        for bb in range(B):
            for slot, ck in enumerate(chunks[c]):
                out[bb, ck * 256:(ck + 1) * 256, :] = \
                    oT[:, bb * 512 + slot * 256: bb * 512 + (slot + 1) * 256].T
    return out


# version 05
# speedup vs baseline: 1.0599x; 1.0599x over previous
import numpy as np
import ml_dtypes
from contextlib import ExitStack

import concourse.bass as bass
import concourse.mybir as mybir
from concourse.bass_utils import run_bass_kernel_spmd
from concourse.aot_env import allow_ici_all_to_all

F32 = mybir.dt.float32
BF16 = mybir.dt.bfloat16
U8 = mybir.dt.uint8
I32 = mybir.dt.int32
ALU = mybir.AluOpType
AF = mybir.ActivationFunctionType
AX = mybir.AxisListType.X

NCORE = 8
B = 4
L = 4096
D = 1024
CH = 256
TOK = 2048
NKT = 8
EPS = 1e-5
BF = ml_dtypes.bfloat16


class Sem:
    def __init__(self, stack, nc, name):
        self.h = stack.enter_context(nc.semaphore(name))
        self.n = 0


class DmaRing:

    def __init__(self, stack, nc, name, n=10):
        self.sems = [Sem(stack, nc, f"{name}{i}") for i in range(n)]
        self.i = 0

    def take(self):
        s = self.sems[self.i % len(self.sems)]
        self.i += 1
        gate = [(s, s.n)]
        return s, gate

    def all(self):
        return [(s, s.n) for s in self.sems if s.n > 0]


class Kb:
    ENGS = ("tensor", "vector", "scalar", "sync", "gpsimd")

    def __init__(self, nc, stack):
        self.nc = nc
        self.q = {e: [] for e in self.ENGS}
        self.rings = {"sync": DmaRing(stack, nc, "rs"),
                      "gpsimd": DmaRing(stack, nc, "rg")}
        self.smm = Sem(stack, nc, "smm")
        self.sev = Sem(stack, nc, "sev")
        self.svv = Sem(stack, nc, "svv")
        self.sgp = Sem(stack, nc, "sgp")

    _uid = [0]

    def sb(self, st, name, shape, dt):
        Kb._uid[0] += 1
        return st.enter_context(self.nc.sbuf_tensor(
            f"{name}_{Kb._uid[0]}", list(shape), dt))

    def ps(self, st, name, shape, dt=F32):
        Kb._uid[0] += 1
        return st.enter_context(self.nc.psum_tensor(
            f"{name}_{Kb._uid[0]}", list(shape), dt))

    def op(self, eng, build, waits=(), incs=()):
        wv = [(s.h, v) for (s, v) in waits if v > 0]
        iv = []
        for s, kk in incs:
            s.n += kk
            iv.append((s.h, kk))

        def run(e, wv=wv, iv=iv, build=build):
            for h, v in wv:
                e.wait_ge(h, v)
            ins = build(e)
            for h, kk in iv:
                ins.then_inc(h, kk)
        self.q[eng].append(run)

    def wait(self, eng, waits):
        wv = [(s.h, v) for (s, v) in waits if v > 0]
        if not wv:
            return

        def run(e, wv=wv):
            for h, v in wv:
                e.wait_ge(h, v)
        self.q[eng].append(run)

    def barrier(self):
        w = [(s, s.n) for s in (self.smm, self.sev, self.svv, self.sgp)]
        w += self.rings["sync"].all() + self.rings["gpsimd"].all()
        for e in self.ENGS:
            self.wait(e, w)

    def dma_all(self, queue=None):
        if queue:
            return self.rings[queue].all()
        return self.rings["sync"].all() + self.rings["gpsimd"].all()

    def dma_op(self, queue, build, waits=()):
        s, gate = self.rings[queue].take()
        self.op(queue, build, waits=list(waits) + gate, incs=[(s, 16)])
        return [(s, s.n)]

    def flush(self, block):
        for name in self.ENGS:
            fns = self.q[name]
            if not fns:
                continue

            def body(e, fns=fns):
                for f in fns:
                    f(e)
            getattr(block, name)(body)

    def load(self, dst, src, queue="sync", waits=(), slow=False):
        if slow:
            return self.dma_op(queue, lambda e, d=dst, sr=src: e.dma_start(
                out=d, in_=sr, allow_slow_non_contiguous=True), waits=waits)
        return self.dma_op(queue, lambda e, d=dst, sr=src: e.dma_start(
            out=d, in_=sr), waits=waits)

    def store(self, dst, src, waits=(), queue="sync"):
        return self.dma_op(queue, lambda e, d=dst, sr=src: e.dma_start(
            out=d, in_=sr), waits=waits)


class Acc:
    def __init__(self, banks, init_war=None):
        self.banks = banks
        self.war = [list(init_war or []) for _ in banks]
        self.i = 0

    def next(self):
        j = self.i % len(self.banks)
        self.i += 1
        return self.banks[j], list(self.war[j]), j

    def done(self, j, waits):
        self.war[j] = list(waits)


_LN_N = [0]


def ln_env(k, st, ones128, onesrow, psS, psBr, psBm, pre=None, dt=BF16):
    _LN_N[0] += 1
    tag = str(_LN_N[0])
    return dict(ones128=ones128, onesrow=onesrow, psS=psS, psBr=psBr, psBm=psBm,
                srow=k.sb(st, "ln_srow" + tag, (1, 4 * 512), F32),
                st16=k.sb(st, "ln_st16" + tag, (1, 2 * 512), dt),
                sq0=k.sb(st, "ln_sq0" + tag, (128, 512), dt),
                sq1=k.sb(st, "ln_sq1" + tag, (128, 512), dt),
                stats_war=[], sq_war=[[], []], bc_war=[], pre=pre or [])


def emit_ln_group(k, X, colstride, g, env):
    ones128, onesrow = env["ones128"], env["onesrow"]
    psS, psBr, psBm = env["psS"], env["psBr"], env["psBm"]
    srow, st16 = env["srow"], env["st16"]
    sqs = (env["sq0"], env["sq1"])

    def Xs(kt):
        return X[:, kt * colstride + g * 512: kt * colstride + g * 512 + 512]

    for kt in range(NKT):
        k.op("tensor", lambda t, kt=kt: t.matmul(
            psS[0:1, :], ones128[:, 0:1], Xs(kt),
            start=(kt == 0), stop=(kt == NKT - 1)),
            waits=(env["pre"] + env["stats_war"]) if kt == 0 else ())
    for kt in range(NKT):
        sq = sqs[kt % 2]
        k.op("scalar", lambda s, kt=kt, sq=sq: s.activation(
            out=sq[:], in_=Xs(kt), func=AF.Square),
            waits=(env["pre"] if kt == 0 else []) + env["sq_war"][kt % 2],
            incs=[(k.sev, 1)])
        evq = k.sev.n
        k.op("tensor", lambda t, kt=kt, sq=sq: t.matmul(
            psBr[0:1, :], ones128[:, 0:1], sq[:],
            start=(kt == 0), stop=(kt == NKT - 1)),
            waits=[(k.sev, evq)] + (env["bc_war"] if kt == 0 else []),
            incs=[(k.smm, 1)])
        env["sq_war"][kt % 2] = [(k.smm, k.smm.n)]
    mmq = k.smm.n
    mrow = srow[0:1, 0:512]
    vrow = srow[0:1, 512:1024]
    rrow = srow[0:1, 1024:1536]
    trow = srow[0:1, 1536:2048]
    k.op("vector", lambda v: v.tensor_scalar(
        out=mrow, in0=psS[0:1, :], scalar1=1.0 / D, scalar2=None, op0=ALU.mult),
        waits=[(k.smm, mmq)])
    k.op("vector", lambda v: v.tensor_scalar(
        out=vrow, in0=psBr[0:1, :], scalar1=1.0 / D, scalar2=None, op0=ALU.mult))
    k.op("vector", lambda v: v.tensor_tensor(out=trow, in0=mrow, in1=mrow,
                                             op=ALU.mult))
    k.op("vector", lambda v: v.tensor_tensor(out=vrow, in0=vrow, in1=trow,
                                             op=ALU.subtract), incs=[(k.svv, 1)])
    env["stats_war"] = [(k.svv, k.svv.n)]
    vvq = k.svv.n
    k.op("vector", lambda v: v.tensor_scalar(
        out=vrow, in0=vrow, scalar1=EPS, scalar2=None, op0=ALU.add),
        incs=[(k.svv, 1)])
    vve = k.svv.n
    k.op("scalar", lambda s: s.activation(out=rrow, in_=vrow, func=AF.Ln),
         waits=[(k.svv, vve)], incs=[(k.sev, 1)])
    evl = k.sev.n
    k.op("vector", lambda v: v.tensor_scalar(
        out=rrow, in0=rrow, scalar1=-0.5, scalar2=None, op0=ALU.mult),
        waits=[(k.sev, evl)], incs=[(k.svv, 1)])
    vvh = k.svv.n
    k.op("scalar", lambda s: s.activation(out=rrow, in_=rrow, func=AF.Exp),
         waits=[(k.svv, vvh)],
         incs=[(k.sev, 1)])
    evr = k.sev.n
    k.op("vector", lambda v: v.tensor_tensor(out=trow, in0=mrow, in1=rrow,
                                             op=ALU.mult), waits=[(k.sev, evr)])
    k.op("vector", lambda v: v.tensor_copy(out=st16[0:1, 0:512], in_=rrow))
    k.op("vector", lambda v: v.tensor_copy(out=st16[0:1, 512:1024], in_=trow),
         incs=[(k.svv, 1)])
    vvr = k.svv.n
    k.op("tensor", lambda t: t.matmul(psBr[:], onesrow[:], st16[0:1, 0:512],
                                      start=True, stop=True),
         waits=[(k.svv, vvr)] + env["bc_war"])
    k.op("tensor", lambda t: t.matmul(psBm[:], onesrow[:], st16[0:1, 512:1024],
                                      start=True, stop=True), incs=[(k.smm, 1)])
    mmb = k.smm.n
    for kt in range(NKT):
        k.op("vector", lambda v, kt=kt: v.tensor_tensor(
            out=Xs(kt), in0=Xs(kt), in1=psBr[:], op=ALU.mult),
            waits=[(k.smm, mmb)] if kt == 0 else ())
        k.op("vector", lambda v, kt=kt: v.tensor_tensor(
            out=Xs(kt), in0=Xs(kt), in1=psBm[:], op=ALU.subtract),
            incs=[(k.svv, 1)])
    env["bc_war"] = [(k.svv, k.svv.n)]
    return k.svv.n


A_INPUTS = ("xbatT", "xownT", "wqT", "wkT", "wvT", "bqh", "bkh", "bvrow",
            "woT", "scaleh", "cosT", "sinST", "Psw", "band", "w3T", "u3T",
            "b3h", "wo0T", "uo0T", "bo0h", "inch")
B_INPUTS = ("xownT", "gateT", "boh", "waT", "wabh", "wbT")
C_INPUTS = ("sel2h", "wc1T", "wdT", "w1", "b1h", "w2", "b2", "Esel", "id16")


def _new_nc():
    return bass.Bass("TRN2", target_bir_lowering=False, debug=False,
                     detect_race_conditions=False)


def build_A():
    nc = _new_nc()
    DI = lambda n, s, d=F32: nc.dram_tensor(n, list(s), d, kind="ExternalInput").ap()
    DO = lambda n, s, d=F32: nc.dram_tensor(n, list(s), d, kind="ExternalOutput").ap()
    io = {}
    io["xbatT"] = DI("xbatT", (D, L))
    io["xownT"] = DI("xownT", (D, TOK))
    for n_, sh in (("wqT", (D, 512)), ("wkT", (D, 512)), ("wvT", (D, 512)),
                   ("bqh", (128, 4)), ("bkh", (128, 4)), ("bvrow", (1, 512)),
                   ("woT", (512, D)), ("scaleh", (128, 1)), ("cosT", (128, L)),
                   ("sinST", (128, L)), ("Psw", (128, 128)),
                   ("band", (4 * 128, 512)), ("w3T", (D, 3072)),
                   ("u3T", (D, 3072)), ("b3h", (128, 24)), ("wo0T", (D, D)),
                   ("uo0T", (D, D)), ("bo0h", (128, 8)), ("inch", (128, 8))):
        io[n_] = DI(n_, sh)
    io["partialT"] = DO("partialT", (D, L))
    io["gT"] = DO("gT", (D, TOK))
    with ExitStack() as top:
        k = Kb(nc, top)
        block = top.enter_context(nc.Block())
        emit_A(k, top, io)
        k.flush(block)
    return nc


def emit_A(k, top, io):
    nc = k.nc
    (xbatT, xownT, wqT, wkT, wvT, bqh, bkh, bvrow, woT, scaleh, cosT, sinST,
     Psw, band, w3T, u3T, b3h, wo0T, uo0T, bo0h, inch) = (
        io[n_] for n_ in A_INPUTS)
    gT = io["gT"]
    partialT = io.get("partialT")
    a2a_in = io.get("a2a_in")
    _dbg = False
    if True:
        sA = ExitStack()
        sA.__enter__()
        xring = k.sb(sA, "xring", (128, 2 * NKT * 512), BF16)
        qT = k.sb(sA, "qT", (128, 4 * L), BF16)
        kTt = k.sb(sA, "kTt", (128, 4 * L), BF16)
        vS = k.sb(sA, "vS", (128, 32 * 520), BF16)
        oT = k.sb(sA, "oT", (128, 4 * L), BF16)
        rrow = k.sb(sA, "rrow", (1, 512), F32)
        r16row = k.sb(sA, "r16row", (1, 512), BF16)
        ones128 = k.sb(sA, "ones128", (128, 1), BF16)
        onesrow = k.sb(sA, "onesrow", (1, 128), BF16)

        psST0 = k.ps(sA, "psST0", (128, 512))
        psST1 = k.ps(sA, "psST1", (128, 512))
        psO = k.ps(sA, "psO", (65, 512))
        psC = k.ps(sA, "psC", (128, 512))
        psD = k.ps(sA, "psD", (128, 512))
        psE = k.ps(sA, "psE", (128, 512))
        psF = k.ps(sA, "psF", (128, 512))
        psG = k.ps(sA, "psG", (128, 512))

        k.op("gpsimd", lambda g: g.memset(ones128[:], 1.0))
        k.op("gpsimd", lambda g: g.memset(onesrow[:], 1.0), incs=[(k.sgp, 1)])
        ones_rdy = [(k.sgp, k.sgp.n)]

        with ExitStack() as sW:
            wq_sb = k.sb(sW, "wq_sb", (128, 8 * 512), BF16)
            wk_sb = k.sb(sW, "wk_sb", (128, 8 * 512), BF16)
            wv_sb = k.sb(sW, "wv_sb", (128, 8 * 512), BF16)
            bq_sb = k.sb(sW, "bq_sb", (128, 4), F32)
            bk_sb = k.sb(sW, "bk_sb", (128, 4), F32)
            bv_sb = k.sb(sW, "bv_sb", (1, 512), BF16)
            cos_sb = k.sb(sW, "cos_sb", (128, L), BF16)
            sin_sb = k.sb(sW, "sin_sb", (128, L), BF16)
            psw_sb = k.sb(sW, "psw_sb", (128, 128), BF16)
            scol = k.sb(sW, "scol", (128, 1), F32)
            rt1 = k.sb(sW, "rt1", (128, 512), BF16)
            rt2 = k.sb(sW, "rt2", (128, 512), BF16)

            for wsb, wdr in ((wq_sb, wqT), (wk_sb, wkT), (wv_sb, wvT)):
                k.load(wsb[:].rearrange("p (kt m) -> p kt m", kt=8),
                       wdr.rearrange("(kt p) m -> p kt m", kt=8, p=128), "gpsimd")
            k.load(bv_sb[:], bvrow, "gpsimd")
            k.load(psw_sb[:], Psw, "gpsimd")
            k.load(scol[:], scaleh)
            k.load(bq_sb[:], bqh)
            k.load(bk_sb[:], bkh)
            k.load(cos_sb[:], cosT, "gpsimd")
            k.load(sin_sb[:], sinST, "gpsimd")
            wload = k.dma_all()

            env = ln_env(k, sW, ones128, onesrow, psE, psF, psG, pre=ones_rdy)
            accQK = Acc([psC, psD])
            qkv_done = []
            for g8 in range(8):
                slot = g8 % 2
                Xv = xring[:, slot * NKT * 512:(slot + 1) * NKT * 512]

                def Xb(kt, Xv=Xv):
                    return Xv[:, kt * 512:(kt + 1) * 512]

                war = [(k.svv, qkv_done[g8 - 2])] if g8 >= 2 else []
                ld = k.load(Xv.rearrange("p (kt n) -> p kt n", kt=8),
                            xbatT.rearrange("(kt p) l -> p kt l", kt=8, p=128)
                            [:, :, g8 * 512:(g8 + 1) * 512], "gpsimd", waits=war)
                env["pre"] = ld + ones_rdy
                lnv = emit_ln_group(k, Xv, 512, 0, env)
                for (wsb, bsb, dstT) in ((wq_sb, bq_sb, qT), (wk_sb, bk_sb, kTt)):
                    for mt in range(4):
                        bank, war2, bj = accQK.next()
                        for kt in range(NKT):
                            k.op("tensor", lambda t, kt=kt, mt=mt, w=wsb,
                                 bank=bank, Xb=Xb: t.matmul(
                                     bank[:],
                                     w[:, kt * 512 + mt * 128:
                                       kt * 512 + mt * 128 + 128],
                                     Xb(kt), start=(kt == 0),
                                     stop=(kt == NKT - 1)),
                                 waits=([(k.svv, lnv)] + war2 + wload)
                                 if kt == 0 else (),
                                 incs=[(k.smm, 1)] if kt == NKT - 1 else ())
                        mmq = k.smm.n
                        k.op("vector", lambda v, mt=mt, b=bsb, d=dstT, g8=g8,
                             bank=bank: v.tensor_scalar(
                                 out=d[:, mt * L + g8 * 512:
                                       mt * L + g8 * 512 + 512],
                                 in0=bank[:], scalar1=b[:, mt:mt + 1],
                                 scalar2=None, op0=ALU.add),
                             waits=[(k.smm, mmq)], incs=[(k.svv, 1)])
                        accQK.done(bj, [(k.svv, k.svv.n)])
                for tmt in range(4):
                    bank, war2, bj = accQK.next()
                    ktile = g8 * 4 + tmt
                    for kt in range(NKT):
                        k.op("tensor", lambda t, kt=kt, tmt=tmt, bank=bank,
                             Xb=Xb: t.matmul(
                                 bank[:], Xb(kt)[:, tmt * 128:(tmt + 1) * 128],
                                 wv_sb[:, kt * 512:(kt + 1) * 512],
                                 start=(kt == 0), stop=False),
                             waits=(war2 + [(k.svv, lnv)] + wload)
                             if kt == 0 else ())
                    k.op("tensor", lambda t, bank=bank: t.matmul(
                        bank[:], onesrow[:], bv_sb[:], start=False, stop=True),
                        incs=[(k.smm, 1)])
                    mmv = k.smm.n
                    for h in range(8):
                        k.op("vector", lambda v, h=h, kt2=ktile, bank=bank:
                             v.tensor_copy(
                                 out=vS[:, kt2 * 520 + h * 65:
                                        kt2 * 520 + h * 65 + 64],
                                 in_=bank[:, h * 64: h * 64 + 64]),
                             waits=[(k.smm, mmv)] if h == 0 else (),
                             incs=[(k.svv, 1)] if h == 7 else ())
                    accQK.done(bj, [(k.svv, k.svv.n)])
                qkv_done.append(k.svv.n)

            k.op("gpsimd", lambda g: g.memset(
                vS[:].rearrange("p (kt h s) -> p kt h s", kt=32, h=8)
                [:, :, :, 64:65], 1.0),
                waits=[(k.svv, k.svv.n)], incs=[(k.sgp, 1)])
            vones = k.sgp.n

            accR = Acc([psC, psD], init_war=[(k.svv, k.svv.n)])
            for dstT in (qT, kTt):
                for pt in range(4):
                    for ng in range(8):
                        bank, war2, bj = accR.next()
                        sl = dstT[:, pt * L + ng * 512: pt * L + ng * 512 + 512]
                        k.op("tensor", lambda t, bank=bank, sl=sl: t.matmul(
                            bank[:], psw_sb[:], sl, start=True, stop=True),
                            waits=war2, incs=[(k.smm, 1)])
                        mmr = k.smm.n
                        csl = cos_sb[:, ng * 512:(ng + 1) * 512]
                        ssl = sin_sb[:, ng * 512:(ng + 1) * 512]
                        k.op("vector", lambda v, sl=sl, csl=csl: v.tensor_tensor(
                            out=rt1[:], in0=sl, in1=csl, op=ALU.mult),
                            waits=[(k.smm, mmr)])
                        k.op("vector", lambda v, bank=bank, ssl=ssl:
                             v.tensor_tensor(out=rt2[:], in0=bank[:], in1=ssl,
                                             op=ALU.mult))
                        k.op("vector", lambda v, sl=sl: v.tensor_tensor(
                            out=sl, in0=rt1[:], in1=rt2[:], op=ALU.add),
                            incs=[(k.svv, 1)])
                        accR.done(bj, [(k.svv, k.svv.n)])
            k.op("vector", lambda v: v.tensor_scalar(
                out=qT[:], in0=qT[:], scalar1=scol[:], scalar2=None,
                op0=ALU.mult), waits=wload, incs=[(k.svv, 1)])
            k.barrier()

        with ExitStack() as sAtt:
            at = [k.sb(sAtt, f"at{i}", (128, 512), BF16) for i in range(4)]
            band_sb = k.sb(sAtt, "band_sb", (128, 4 * 512), BF16)
            k.load(band_sb[:].rearrange("p (o n) -> p o n", o=4),
                   band.rearrange("(o p) n -> p o n", o=4, p=128), "gpsimd")
            bnd = k.dma_all("gpsimd")
            k.wait("vector", bnd)
            k.wait("scalar", bnd)
            k.wait("tensor", bnd + [(k.sgp, vones)])
            stb_a = (psST0, psC)
            stb_b = (psST1, psD)
            at_a = (at[0], at[2])
            at_b = (at[1], at[3])
            psO2 = psG
            exp_c = {}
            mk_c = {}
            av_c = {}
            last_exp = 0
            last_av = 0
            rr_war = 0
            accNrm = Acc([psE, psF])
            o_ev = []

            def emit_st2(j2, hp, qg, kt):
                for a, (stbk, atk, ho) in enumerate(
                        ((stb_a, at_a, 0), (stb_b, at_b, 64))):
                    h = 2 * hp + a
                    w = []
                    if kt >= 2:
                        w.append((k.sev, exp_c[(j2, kt - 2, a)]))
                    elif j2 > 0 and a == 0:
                        w.append((k.sev, last_exp))
                    k.op("tensor", lambda t, kt=kt, hp=hp, ho=ho, qg=qg,
                         bank=stbk[kt % 2]: t.matmul(
                             bank[:],
                             kTt[ho:ho + 64, hp * L + kt * 128:
                                 hp * L + kt * 128 + 128],
                             qT[ho:ho + 64, hp * L + qg * 512:
                                hp * L + qg * 512 + 512],
                             start=True, stop=True),
                         waits=w, incs=[(k.smm, 1)])
                    stc = k.smm.n
                    w = [(k.smm, stc)]
                    if kt >= 2:
                        w.append((k.smm, av_c[(j2, kt - 2, a)]))
                    elif j2 > 0 and a == 0:
                        w.append((k.smm, last_av))
                    k.op("scalar", lambda sc, kt=kt, bank=stbk[kt % 2],
                         atile=atk[kt % 2]: sc.activation(
                             out=atile[:], in_=bank[:], func=AF.Exp),
                         waits=w, incs=[(k.sev, 1)])
                    exp_c[(j2, kt, a)] = k.sev.n
                    o = kt - 4 * qg
                    if o >= 0:
                        k.op("vector", lambda v, o=o, atile=atk[kt % 2]:
                             v.tensor_tensor(
                                 out=atile[:], in0=atile[:],
                                 in1=band_sb[:, o * 512:(o + 1) * 512],
                                 op=ALU.mult),
                             waits=[(k.sev, exp_c[(j2, kt, a)])],
                             incs=[(k.svv, 1)])
                        mk_c[(j2, kt, a)] = k.svv.n

            def emit_av2(j2, hp, qg, kt, K):
                o = kt - 4 * qg
                for a, (atk, ob) in enumerate(((at_a, psO), (at_b, psO2))):
                    h = 2 * hp + a
                    w = ([(k.svv, mk_c[(j2, kt, a)])] if o >= 0
                         else [(k.sev, exp_c[(j2, kt, a)])])
                    if kt == 0 and j2 > 0 and a == 0:
                        w += [(k.sev, o_ev[j2 - 1][0]),
                              (k.svv, o_ev[j2 - 1][1])]
                    k.op("tensor", lambda t, kt=kt, h=h, K=K, ob=ob,
                         atile=atk[kt % 2]: t.matmul(
                             ob[0:65, :],
                             vS[:, kt * 520 + h * 65: kt * 520 + h * 65 + 65],
                             atile[:], start=(kt == 0), stop=(kt == K - 1)),
                         waits=w, incs=[(k.smm, 1)])
                    av_c[(j2, kt, a)] = k.smm.n

            for hp in range(4):
                for qg in range(8):
                    j2 = hp * 8 + qg
                    K = 4 * qg + 4
                    for kt in range(K):
                        emit_st2(j2, hp, qg, kt)
                        if kt >= 1:
                            emit_av2(j2, hp, qg, kt - 1, K)
                    emit_av2(j2, hp, qg, K - 1, K)
                    last_exp = exp_c[(j2, K - 1, 1)]
                    last_av = av_c[(j2, K - 1, 1)]
                    evs = []
                    for a, ob in ((0, psO), (1, psO2)):
                        ho = 64 * a
                        avl = av_c[(j2, K - 1, a)]
                        osl = oT[ho:ho + 64, hp * L + qg * 512:
                                 hp * L + qg * 512 + 512]
                        k.op("scalar", lambda sc, osl=osl, ob=ob: sc.copy(
                            out=osl, in_=ob[0:64, :]), waits=[(k.smm, avl)],
                            incs=[(k.sev, 1)])
                        oev = k.sev.n
                        k.op("vector", lambda v, ob=ob: v.reciprocal(
                            out=rrow[:], in_=ob[64:65, :]),
                            waits=[(k.smm, avl), (k.svv, rr_war)],
                            incs=[(k.svv, 1)])
                        rcv = k.svv.n
                        k.op("vector", lambda v: v.tensor_copy(
                            out=r16row[:], in_=rrow[:]),
                            waits=[(k.svv, k.svv.n)], incs=[(k.svv, 1)])
                        r16v = k.svv.n
                        nbank, nwar, nbj = accNrm.next()
                        k.op("tensor", lambda t, nbank=nbank: t.matmul(
                            nbank[0:64, :], onesrow[:, 0:64], r16row[:],
                            start=True, stop=True),
                            waits=[(k.svv, r16v)] + nwar, incs=[(k.smm, 1)])
                        mmn = k.smm.n
                        k.op("vector", lambda v, osl=osl, nbank=nbank:
                             v.tensor_tensor(out=osl, in0=osl,
                                             in1=nbank[0:64, :], op=ALU.mult),
                             waits=[(k.smm, mmn), (k.sev, oev)],
                             incs=[(k.svv, 1)])
                        accNrm.done(nbj, [(k.svv, k.svv.n)])
                        rr_war = k.svv.n
                        evs.append((oev, rcv))
                    o_ev.append((max(e for e, r in evs),
                                  max(r for e, r in evs)))
            normv = k.svv.n
            k.barrier()

        with ExitStack() as sProj:
            wo_sb = k.sb(sProj, "wo_sb", (128, 4 * 8 * 128), BF16)
            stg = (k.sb(sProj, "stg0", (128, 512), F32),
                   k.sb(sProj, "stg1", (128, 512), F32))
            wl = k.load(wo_sb[:].rearrange("p (kt m) -> p kt m", kt=4),
                        woT.rearrange("(kt p) m -> p kt m", kt=4, p=128),
                        "gpsimd")
            accP = Acc([psC, psD])
            stg_war = [[], []]
            for mt in range(8):
                for ng in range(8):
                    bank, war2, bj = accP.next()
                    for kt in range(4):
                        k.op("tensor", lambda t, kt=kt, mt=mt, ng=ng, bank=bank:
                             t.matmul(bank[:],
                                      wo_sb[:, (kt * 8 + mt) * 128:
                                            (kt * 8 + mt) * 128 + 128],
                                      oT[:, kt * L + ng * 512:
                                         kt * L + ng * 512 + 512],
                                      start=(kt == 0), stop=(kt == 3)),
                             waits=(war2 + wl + [(k.svv, normv)])
                             if kt == 0 else (),
                             incs=[(k.smm, 1)] if kt == 3 else ())
                    mmp = k.smm.n
                    sg = stg[ng % 2]
                    k.op("vector", lambda v, sg=sg, bank=bank: v.tensor_copy(
                        out=sg[:], in_=bank[:]),
                        waits=[(k.smm, mmp)] + stg_war[ng % 2],
                        incs=[(k.svv, 1)])
                    accP.done(bj, [(k.svv, k.svv.n)])
                    vvp = k.svv.n
                    if a2a_in is None:
                        stg_war[ng % 2] = k.store(
                            partialT[mt * 128:(mt + 1) * 128,
                                     ng * 512:(ng + 1) * 512], sg[:],
                            waits=[(k.svv, vvp)])
                    else:
                        toks = []
                        for half in (0, 1):
                            ck = 2 * ng + half
                            r_ = ck if ck < 8 else 15 - ck
                            pos = 0 if ck < 8 else 1
                            toks += k.store(
                                a2a_in[r_, mt * 128:(mt + 1) * 128,
                                       pos * 256:(pos + 1) * 256],
                                sg[:, half * 256:(half + 1) * 256],
                                waits=[(k.svv, vvp)])
                        stg_war[ng % 2] = toks
            if _dbg:
                k.store(qdbg, qT[:], queue="gpsimd")
                k.store(kdbg, kTt[:], queue="gpsimd")
                k.store(odbg, oT[:], queue="gpsimd")
                k.store(vdbg, vS[:], queue="gpsimd")
            k.barrier()

        sA.close()

        sF = ExitStack()
        sF.__enter__()
        ones128b = k.sb(sF, "ones128b", (128, 1), BF16)
        onesrowb = k.sb(sF, "onesrowb", (1, 128), BF16)
        xn = k.sb(sF, "xn", (128, NKT * TOK), BF16)
        cT = k.sb(sF, "cT", (128, 8 * 512), F32)
        c16 = k.sb(sF, "c16", (128, 8 * 512), BF16)
        uT = k.sb(sF, "uT", (128, 8 * 512), F32)
        gsb = k.sb(sF, "gsb", (128, NKT * TOK), BF16)
        wring = k.sb(sF, "wring", (128, 8 * 1024), BF16)
        gate_f = k.sb(sF, "gate_f", (128, 3 * 512), F32)
        tmp_a = k.sb(sF, "tmp_a", (128, 512), F32)
        tmp_b = k.sb(sF, "tmp_b", (128, 512), F32)
        tmp_c = k.sb(sF, "tmp_c", (128, 512), F32)
        b3_sb = k.sb(sF, "b3_sb", (128, 24), F32)
        bo0_sb = k.sb(sF, "bo0_sb", (128, 8), F32)
        inc_sb = k.sb(sF, "inc_sb", (128, 8), F32)
        onesf = k.sb(sF, "onesf", (128, 512), F32)
        psH0 = k.ps(sF, "psH0", (128, 512))
        psH1 = k.ps(sF, "psH1", (128, 512))
        psH2 = k.ps(sF, "psH2", (128, 512))
        psB0 = k.ps(sF, "psB0", (128, 512))
        psB1 = k.ps(sF, "psB1", (128, 512))
        psB2 = k.ps(sF, "psB2", (128, 512))

        k.op("gpsimd", lambda g: g.memset(ones128b[:], 1.0))
        k.op("gpsimd", lambda g: g.memset(onesrowb[:], 1.0), incs=[(k.sgp, 1)])
        k.op("vector", lambda v: v.memset(onesf[:], 1.0), incs=[(k.svv, 1)])
        ones2 = [(k.sgp, k.sgp.n), (k.svv, k.svv.n)]
        k.load(xn[:].rearrange("p (kt n) -> p kt n", kt=8),
               xownT.rearrange("(kt p) n -> p kt n", kt=8, p=128), "gpsimd")
        k.load(b3_sb[:], b3h)
        k.load(bo0_sb[:], bo0h)
        k.load(inc_sb[:], inch)
        ldf = k.dma_all()

        env2 = ln_env(k, sF, ones128b, onesrowb, psB0, psB1, psB2,
                      pre=ldf + ones2)
        lnv2 = 0
        for g in range(4):
            lnv2 = emit_ln_group(k, xn, TOK, g, env2)

        for kt in range(NKT):
            k.op("vector", lambda v, kt=kt: v.tensor_scalar(
                out=cT[:, kt * 512:(kt + 1) * 512], in0=onesf[:],
                scalar1=inc_sb[:, kt:kt + 1], scalar2=None, op0=ALU.mult),
                waits=ldf if kt == 0 else (), incs=[(k.svv, 1)])

        gatebanks = (psH0, psH1, psH2)
        gb_war = [[], [], []]
        wslot_war = [[] for _ in range(8)]
        for t_ in range(B):
            for kt in range(NKT):
                k.op("scalar", lambda s, kt=kt: s.copy(
                    out=c16[:, kt * 512:(kt + 1) * 512],
                    in_=cT[:, kt * 512:(kt + 1) * 512]),
                    waits=[(k.svv, k.svv.n)] if kt == 0 else (),
                    incs=[(k.sev, 1)])
            cv = k.sev.n
            for dm in range(8):
                gd = []
                for gi in range(3):
                    mcol = gi * 1024 + dm * 128
                    bank = gatebanks[gi]
                    s_w = (dm * 3 + gi) % 4
                    s_u = 4 + (dm * 3 + gi) % 4
                    ldw = k.load(wring[:, s_w * 1024:(s_w + 1) * 1024]
                                 .rearrange("p (kt m) -> p kt m", kt=8),
                                 w3T.rearrange("(kt p) m -> p kt m", kt=8, p=128)
                                 [:, :, mcol:mcol + 128], "gpsimd",
                                 waits=wslot_war[s_w])
                    ldw = ldw + k.load(
                        wring[:, s_u * 1024:(s_u + 1) * 1024]
                        .rearrange("p (kt m) -> p kt m", kt=8),
                        u3T.rearrange("(kt p) m -> p kt m", kt=8, p=128)
                        [:, :, mcol:mcol + 128], "gpsimd",
                        waits=wslot_war[s_u])
                    for kt in range(NKT):
                        k.op("tensor", lambda t, kt=kt, s_w=s_w, t_=t_,
                             bank=bank: t.matmul(
                                 bank[:],
                                 wring[:, s_w * 1024 + kt * 128:
                                       s_w * 1024 + kt * 128 + 128],
                                 xn[:, kt * TOK + t_ * 512:
                                    kt * TOK + t_ * 512 + 512],
                                 start=(kt == 0), stop=False),
                             waits=(ldw + [(k.sev, cv), (k.svv, lnv2)]
                                    + gb_war[gi]) if kt == 0 else ())
                    for kt in range(NKT):
                        k.op("tensor", lambda t, kt=kt, s_u=s_u, bank=bank:
                             t.matmul(
                                 bank[:],
                                 wring[:, s_u * 1024 + kt * 128:
                                       s_u * 1024 + kt * 128 + 128],
                                 c16[:, kt * 512:(kt + 1) * 512],
                                 start=False, stop=(kt == NKT - 1)),
                             incs=[(k.smm, 1)] if kt == NKT - 1 else ())
                    wslot_war[s_w] = [(k.smm, k.smm.n)]
                    wslot_war[s_u] = [(k.smm, k.smm.n)]
                    mmg = k.smm.n
                    k.op("scalar", lambda s, gi=gi, dm=dm, bank=bank:
                         s.activation(
                             out=gate_f[:, gi * 512:(gi + 1) * 512],
                             in_=bank[:],
                             func=(AF.Sigmoid if gi < 2 else AF.Tanh),
                             bias=b3_sb[:, gi * 8 + dm: gi * 8 + dm + 1],
                             scale=1.0),
                         waits=[(k.smm, mmg)], incs=[(k.sev, 1)])
                    gb_war[gi] = [(k.sev, k.sev.n)]
                    gd.append(k.sev.n)
                k.op("vector", lambda v, dm=dm: v.tensor_tensor(
                    out=tmp_a[:], in0=gate_f[:, 0:512],
                    in1=cT[:, dm * 512:(dm + 1) * 512], op=ALU.mult),
                    waits=[(k.sev, gd[-1])])
                k.op("vector", lambda v: v.tensor_tensor(
                    out=tmp_b[:], in0=gate_f[:, 512:1024],
                    in1=gate_f[:, 1024:1536], op=ALU.mult))
                k.op("vector", lambda v, dm=dm: v.tensor_tensor(
                    out=cT[:, dm * 512:(dm + 1) * 512], in0=tmp_a[:],
                    in1=tmp_b[:], op=ALU.add), incs=[(k.svv, 1)])
                gb_war[0] = gb_war[1] = gb_war[2] = [(k.svv, k.svv.n)]

        for kt in range(NKT):
            k.op("scalar", lambda s, kt=kt: s.copy(
                out=c16[:, kt * 512:(kt + 1) * 512],
                in_=cT[:, kt * 512:(kt + 1) * 512]),
                waits=[(k.svv, k.svv.n)] if kt == 0 else (), incs=[(k.sev, 1)])
        cfv = k.sev.n
        accU = Acc([psH0, psH1], init_war=[(k.sev, k.sev.n)])
        for mt in range(8):
            slot = mt % 4
            ldu = k.load(wring[:, slot * 1024:(slot + 1) * 1024]
                         .rearrange("p (kt m) -> p kt m", kt=8),
                         uo0T.rearrange("(kt p) m -> p kt m", kt=8, p=128)
                         [:, :, mt * 128:(mt + 1) * 128], "gpsimd",
                         waits=wslot_war[slot])
            bank, war2, bj = accU.next()
            for kt in range(NKT):
                k.op("tensor", lambda t, kt=kt, slot=slot, bank=bank: t.matmul(
                    bank[:], wring[:, slot * 1024 + kt * 128:
                                   slot * 1024 + kt * 128 + 128],
                    c16[:, kt * 512:(kt + 1) * 512],
                    start=(kt == 0), stop=(kt == NKT - 1)),
                    waits=(ldu + [(k.sev, cfv)] + war2) if kt == 0 else (),
                    incs=[(k.smm, 1)] if kt == NKT - 1 else ())
            wslot_war[slot] = [(k.smm, k.smm.n)]
            mmu = k.smm.n
            k.op("vector", lambda v, mt=mt, bank=bank: v.tensor_copy(
                out=uT[:, mt * 512:(mt + 1) * 512], in_=bank[:]),
                waits=[(k.smm, mmu)], incs=[(k.svv, 1)])
            accU.done(bj, [(k.svv, k.svv.n)])
        uv = k.svv.n

        accX = Acc([psH0, psH1], init_war=[(k.svv, uv)])
        gv = 0
        for mt in range(8):
            slot = mt % 4
            ldx = k.load(wring[:, slot * 1024:(slot + 1) * 1024]
                         .rearrange("p (kt m) -> p kt m", kt=8),
                         wo0T.rearrange("(kt p) m -> p kt m", kt=8, p=128)
                         [:, :, mt * 128:(mt + 1) * 128], "gpsimd",
                         waits=wslot_war[slot])
            for tg in range(4):
                bank, war2, bj = accX.next()
                for kt in range(NKT):
                    k.op("tensor", lambda t, kt=kt, slot=slot, tg=tg, bank=bank:
                         t.matmul(bank[:],
                                  wring[:, slot * 1024 + kt * 128:
                                        slot * 1024 + kt * 128 + 128],
                                  xn[:, kt * TOK + tg * 512:
                                     kt * TOK + tg * 512 + 512],
                                  start=(kt == 0), stop=(kt == NKT - 1)),
                         waits=(ldx + war2) if kt == 0 else (),
                         incs=[(k.smm, 1)] if kt == NKT - 1 else ())
                mmx = k.smm.n
                if tg == 3:
                    wslot_war[slot] = [(k.smm, mmx)]
                k.op("vector", lambda v, mt=mt, bank=bank: v.tensor_tensor(
                    out=tmp_a[:], in0=uT[:, mt * 512:(mt + 1) * 512],
                    in1=bank[:], op=ALU.add),
                    waits=[(k.smm, mmx), (k.sev, k.sev.n)], incs=[(k.svv, 1)])
                accX.done(bj, [(k.svv, k.svv.n)])
                vvx = k.svv.n
                k.op("scalar", lambda s, mt=mt: s.activation(
                    out=tmp_b[:], in_=tmp_a[:], func=AF.Tanh,
                    bias=bo0_sb[:, mt:mt + 1], scale=1.0),
                    waits=[(k.svv, vvx)])
                k.op("scalar", lambda s: s.activation(
                    out=tmp_c[:], in_=tmp_b[:], func=AF.Sigmoid),
                    incs=[(k.sev, 1)])
                evx = k.sev.n
                k.op("vector", lambda v, mt=mt, tg=tg: v.tensor_tensor(
                    out=gsb[:, mt * TOK + tg * 512: mt * TOK + tg * 512 + 512],
                    in0=tmp_b[:], in1=tmp_c[:], op=ALU.mult),
                    waits=[(k.sev, evx)], incs=[(k.svv, 1)])
                gv = k.svv.n
        k.store(gT.rearrange("(kt p) n -> p kt n", kt=8, p=128),
                gsb[:].rearrange("p (kt n) -> p kt n", kt=8),
                waits=[(k.svv, gv)], queue="gpsimd")
        k.wait("sync", k.dma_all())
        k.wait("gpsimd", k.dma_all())
        sF.close()
        k.barrier()


def build_B():
    nc = _new_nc()
    DI = lambda n, s, d=F32: nc.dram_tensor(n, list(s), d, kind="ExternalInput").ap()
    DO = lambda n, s, d=F32: nc.dram_tensor(n, list(s), d, kind="ExternalOutput").ap()
    io = {}
    io["xownT"] = DI("xownT", (D, TOK))
    io["gT"] = DI("gT", (D, TOK))
    io["recv"] = DI("recv", (NCORE, D, 512))
    io["boh"] = DI("boh", (128, 8))
    io["waT"] = DI("waT", (D, CH))
    io["wabh"] = DI("wabh", (128, 2))
    io["wbT"] = DI("wbT", (D, CH))
    io["gateT"] = DI("gateT", (D, 8))
    io["x1T"] = DO("x1T", (D, TOK))
    io["xn1T"] = DO("xn1T", (D, TOK))
    io["aTo"] = DO("aTo", (CH, TOK))
    io["bTo"] = DO("bTo", (CH, TOK))
    io["summ"] = DO("summ", (1, 4096))
    io["logitsT"] = DO("logitsT", (TOK, 8))
    with ExitStack() as top:
        k = Kb(nc, top)
        block = top.enter_context(nc.Block())
        emit_B(k, top, io)
        k.flush(block)
    return nc


def emit_B(k, top, io):
    nc = k.nc
    xownT = io["xownT"]
    gT = io["gT"]
    recv = io["recv"]
    boh, waT, wabh, wbT, gateT = (io[n_] for n_ in
                                  ("boh", "waT", "wabh", "wbT", "gateT"))
    x1T, xn1T, aTo, bTo, summ, logitsT = (io[n_] for n_ in
                                          ("x1T", "xn1T", "aTo", "bTo",
                                           "summ", "logitsT"))
    recv_wait = io.get("recv_wait", [])
    if True:
        sB = ExitStack()
        sB.__enter__()
        xo = k.sb(sB, "xo", (128, NKT * TOK), F32)
        ones128 = k.sb(sB, "ones128", (128, 1), F32)
        onesrow = k.sb(sB, "onesrow", (1, 128), F32)
        psL = k.ps(sB, "psL", (128, 8))
        psS = k.ps(sB, "psS", (128, 512))
        psBr = k.ps(sB, "psBr", (128, 512))
        psBm = k.ps(sB, "psBm", (128, 512))
        psA0 = k.ps(sB, "psA0", (128, 512))
        psA1 = k.ps(sB, "psA1", (128, 512))

        k.op("gpsimd", lambda g: g.memset(ones128[:], 1.0))
        k.op("gpsimd", lambda g: g.memset(onesrow[:], 1.0), incs=[(k.sgp, 1)])
        ones_rdy = [(k.sgp, k.sgp.n)]

        k.load(xo[:].rearrange("p (kt n) -> p kt n", kt=8),
               xownT.rearrange("(kt p) n -> p kt n", kt=8, p=128))
        xol = k.dma_all("sync")

        x1v = 0
        with ExitStack() as s1:
            gsb = k.sb(s1, "gsb", (128, NKT * TOK), BF16)
            rv = k.sb(s1, "rv", (128, 2 * NKT * 512), F32)
            bo_sb = k.sb(s1, "bo_sb", (128, 8), F32)
            tmp = k.sb(s1, "tmp", (128, 512), F32)
            tmp2 = k.sb(s1, "tmp2", (128, 512), F32)
            k.load(gsb[:].rearrange("p (kt n) -> p kt n", kt=8),
                   gT.rearrange("(kt p) n -> p kt n", kt=8, p=128), "gpsimd")
            k.load(bo_sb[:], boh)
            base_ld = k.dma_all()
            rv_war = []
            for b in range(B):
                for two in range(2):
                    k.load(rv[:, two * NKT * 512:(two + 1) * NKT * 512]
                           .rearrange("p (kt n) -> p kt n", kt=8),
                           recv[2 * b + two, :, :]
                           .rearrange("(kt p) n -> p kt n", kt=8, p=128),
                           waits=(rv_war + recv_wait) if two == 0 else ())
                ldr = k.dma_all("sync") + (base_ld if b == 0 else [])
                for mt in range(8):
                    r0 = rv[:, mt * 512: mt * 512 + 512]
                    r1 = rv[:, NKT * 512 + mt * 512: NKT * 512 + mt * 512 + 512]
                    col = mt * TOK + b * 512
                    k.op("vector", lambda v, r0=r0, r1=r1, mt=mt:
                         v.scalar_tensor_tensor(
                             out=tmp[:], in0=r0, scalar=bo_sb[:, mt:mt + 1],
                             in1=r1, op0=ALU.add, op1=ALU.add),
                         waits=ldr if mt == 0 else ())
                    k.op("vector", lambda v, col=col: v.tensor_tensor(
                        out=tmp2[:], in0=tmp[:], in1=gsb[:, col:col + 512],
                        op=ALU.mult))
                    k.op("vector", lambda v, col=col: v.tensor_tensor(
                        out=xo[:, col:col + 512], in0=xo[:, col:col + 512],
                        in1=tmp2[:], op=ALU.add), incs=[(k.svv, 1)])
                    x1v = k.svv.n
                rv_war = [(k.svv, x1v)]
            x1tok = k.store(
                x1T.rearrange("(kt p) n -> p kt n", kt=8, p=128),
                xo[:].rearrange("p (kt n) -> p kt n", kt=8),
                waits=[(k.svv, x1v)])
            k.barrier()

        xn1 = k.sb(sB, "xn1", (128, NKT * TOK), BF16)
        wa_sb = k.sb(sB, "wa_sb", (128, 8 * 256), BF16)
        wb_sb = k.sb(sB, "wb_sb", (128, 8 * 256), BF16)
        wab_sb = k.sb(sB, "wab_sb", (128, 2), F32)
        a_sb = k.sb(sB, "a_sb", (128, 2 * TOK), F32)
        b_sb = k.sb(sB, "b_sb", (128, 2 * TOK), F32)
        scr = k.sb(sB, "scr", (128, 2 * 256), F32)
        csb = k.sb(sB, "csb", (128, 32), F32)
        k.load(wab_sb[:], wabh)
        k.load(wa_sb[:].rearrange("p (kt m) -> p kt m", kt=8),
               waT.rearrange("(kt p) m -> p kt m", kt=8, p=128), "gpsimd")
        k.load(wb_sb[:].rearrange("p (kt m) -> p kt m", kt=8),
               wbT.rearrange("(kt p) m -> p kt m", kt=8, p=128), "gpsimd")
        base_ld = k.dma_all()

        env = ln_env(k, sB, ones128, onesrow, psS, psBr, psBm,
                     pre=[(k.svv, x1v)] + ones_rdy + x1tok, dt=F32)
        lnv = 0
        for g in range(4):
            lnv = emit_ln_group(k, xo, TOK, g, env)
        k.store(xn1T.rearrange("(kt p) n -> p kt n", kt=8, p=128),
                xo[:].rearrange("p (kt n) -> p kt n", kt=8),
                waits=[(k.svv, lnv)])
        for kt in range(NKT):
            k.op("scalar", lambda s, kt=kt: s.copy(
                out=xn1[:, kt * TOK:(kt + 1) * TOK],
                in_=xo[:, kt * TOK:(kt + 1) * TOK]),
                waits=[(k.svv, lnv)] if kt == 0 else (), incs=[(k.sev, 1)])
        cast_done = k.sev.n
        gwf = k.sb(sB, "gwf", (128, 8 * 8), F32)
        lg = k.sb(sB, "lg", (128, 16 * 8), F32)
        gwl = k.load(gwf[:].rearrange("p (kt e) -> p kt e", kt=8),
                     gateT.rearrange("(kt p) e -> p kt e", kt=8, p=128))
        lg_done = 0
        for tt in range(16):
            for kt in range(NKT):
                k.op("tensor", lambda t, kt=kt, tt=tt: t.matmul(
                    psL[:], xo[:, kt * TOK + tt * 128: kt * TOK + tt * 128 + 128],
                    gwf[:, kt * 8:(kt + 1) * 8],
                    start=(kt == 0), stop=(kt == NKT - 1)),
                    waits=(gwl + [(k.svv, lnv), (k.svv, lg_done)])
                    if kt == 0 else (),
                    incs=[(k.smm, 1)] if kt == NKT - 1 else ())
            mml = k.smm.n
            k.op("vector", lambda v, tt=tt: v.tensor_copy(
                out=lg[:, tt * 8:(tt + 1) * 8], in_=psL[:]),
                waits=[(k.smm, mml)], incs=[(k.svv, 1)])
            lg_done = k.svv.n
        k.store(logitsT.rearrange("(tt p) e -> p tt e", tt=16, p=128),
                lg[:].rearrange("p (tt e) -> p tt e", tt=16),
                waits=[(k.svv, lg_done)])

        accA = Acc([psA0, psA1])
        av = bv = 0
        for (wsb, dst, bias, isa) in ((wa_sb, a_sb, wab_sb, True),
                                      (wb_sb, b_sb, None, False)):
            for mt in range(2):
                for tg in range(4):
                    bank, war2, bj = accA.next()
                    for kt in range(NKT):
                        k.op("tensor", lambda t, kt=kt, mt=mt, tg=tg, w=wsb,
                             bank=bank: t.matmul(
                                 bank[:],
                                 w[:, kt * 256 + mt * 128:
                                   kt * 256 + mt * 128 + 128],
                                 xn1[:, kt * TOK + tg * 512:
                                     kt * TOK + tg * 512 + 512],
                                 start=(kt == 0), stop=(kt == NKT - 1)),
                             waits=(war2 + [(k.sev, cast_done)] + base_ld)
                             if kt == 0 else (),
                             incs=[(k.smm, 1)] if kt == NKT - 1 else ())
                    mma = k.smm.n
                    if isa:
                        k.op("scalar", lambda s, mt=mt, tg=tg, bank=bank:
                             s.activation(
                                 out=a_sb[:, mt * TOK + tg * 512:
                                          mt * TOK + tg * 512 + 512],
                                 in_=bank[:], func=AF.Sigmoid,
                                 bias=wab_sb[:, mt:mt + 1], scale=1.0),
                             waits=[(k.smm, mma)], incs=[(k.sev, 1)])
                        accA.done(bj, [(k.sev, k.sev.n)])
                        av = k.sev.n
                    else:
                        k.op("scalar", lambda s, mt=mt, tg=tg, bank=bank:
                             s.copy(out=b_sb[:, mt * TOK + tg * 512:
                                             mt * TOK + tg * 512 + 512],
                                    in_=bank[:]),
                             waits=[(k.smm, mma)], incs=[(k.sev, 1)])
                        accA.done(bj, [(k.sev, k.sev.n)])
                        bv = k.sev.n
        k.store(aTo.rearrange("(h p) n -> p h n", h=2, p=128),
                a_sb[:].rearrange("p (h n) -> p h n", h=2),
                waits=[(k.sev, av)])
        k.store(bTo.rearrange("(h p) n -> p h n", h=2, p=128),
                b_sb[:].rearrange("p (h n) -> p h n", h=2),
                waits=[(k.sev, bv)])

        scv = 0
        for slot in range(2):
            for half in range(2):
                for b in range(B):
                    t0 = half * TOK + b * 512 + slot * 256
                    colA = ((slot * 2 + 0) * 2 + half) * 4 + b
                    colB = ((slot * 2 + 1) * 2 + half) * 4 + b
                    k.op("vector", lambda v, t0=t0: v.tensor_tensor_scan(
                        out=scr[:, 0:256], data0=a_sb[:, t0:t0 + 256],
                        data1=a_sb[:, t0:t0 + 256], initial=1.0,
                        op0=ALU.mult, op1=ALU.bypass),
                        waits=[(k.sev, av), (k.svv, scv)])
                    k.op("vector", lambda v, t0=t0: v.tensor_tensor_scan(
                        out=scr[:, 256:512], data0=a_sb[:, t0:t0 + 256],
                        data1=b_sb[:, t0:t0 + 256], initial=0.0,
                        op0=ALU.mult, op1=ALU.add),
                        waits=[(k.sev, bv)])
                    k.op("vector", lambda v, colA=colA: v.tensor_copy(
                        out=csb[:, colA:colA + 1], in_=scr[:, 255:256]))
                    k.op("vector", lambda v, colB=colB: v.tensor_copy(
                        out=csb[:, colB:colB + 1], in_=scr[:, 511:512]),
                        incs=[(k.svv, 1)])
                    scv = k.svv.n
        for slot in range(2):
            for ab in range(2):
                for half in range(2):
                    c0 = ((slot * 2 + ab) * 2 + half) * 4
                    off = slot * 2048 + ab * 1024 + half * 512
                    k.dma_op("sync", lambda s, c0=c0, off=off: s.dma_start(
                        out=summ[0:1, off:off + 512]
                        .rearrange("one (p b) -> p (one b)", p=128),
                        in_=csb[:, c0:c0 + 4],
                        allow_slow_non_contiguous=True),
                        waits=[(k.svv, scv)])
        k.wait("sync", k.dma_all())
        k.wait("gpsimd", k.dma_all())
        sB.__exit__(None, None, None)
        k.barrier()


def build_C():
    nc = _new_nc()
    DI = lambda n, s, d=F32: nc.dram_tensor(n, list(s), d, kind="ExternalInput").ap()
    DO = lambda n, s, d=F32: nc.dram_tensor(n, list(s), d, kind="ExternalOutput").ap()
    io = {}
    io["x1T"] = DI("x1T", (D, TOK))
    io["xn1T"] = DI("xn1T", (D, TOK))
    io["aTo"] = DI("aTi", (CH, TOK))
    io["bTo"] = DI("bTi", (CH, TOK))
    io["summ_all"] = DI("summ_all", (NCORE, 4096))
    io["logitsT"] = DI("logitsT", (TOK, 8))
    for n_, sh in (("sel2h", (17, 2)), ("wc1T", (CH, D)), ("wdT", (D, D)),
                   ("w1", (8, D, 4096)), ("b1h", (128, 256)),
                   ("w2", (8, 4096, D)), ("b2", (8, D)), ("Esel", (8, 1024)),
                   ("id16", (128, 128))):
        io[n_] = DI(n_, sh)
    io["outT"] = DO("outT", (D, TOK))
    with ExitStack() as top:
        k = Kb(nc, top)
        block = top.enter_context(nc.Block())
        emit_C(k, top, io)
        k.flush(block)
    return nc


def emit_C(k, top, io):
    nc = k.nc
    x1T, xn1T, aTi, bTi, summ_all, logitsT = (
        io["x1T"], io["xn1T"], io["aTo"], io["bTo"], io["summ_all"],
        io["logitsT"])
    sel2h, wc1T, wdT, w1, b1h, w2, b2, Esel, id16 = (
        io[n_] for n_ in C_INPUTS)
    outT = io["outT"]
    summ_wait = io.get("summ_wait", [])
    import os
    _dbg = os.environ.get("KDBG") == "1" and io.get("dbg_ok", True)
    if _dbg:
        DO2 = lambda n, s, d=F32: nc.dram_tensor(
            n, list(s), d, kind="ExternalOutput").ap()
        accdbg = DO2("accdbg", (128, 8 * TOK))
        seldbg = DO2("seldbg", (8, TOK))
        pdbg = DO2("pdbg", (1, TOK))
        ydbg = DO2("ydbg", (D, TOK))
        hdbg = DO2("hdbg", (128, 2 * TOK))
        ldbg = DO2("ldbg", (128, 16 * 8))
        hindbg = DO2("hindbg", (128, 16))
        spdbg = DO2("spdbg", (128, 16 * 8))
        salldbg = DO2("salldbg", (128, 17 * 8))
        saltdbg = DO2("saltdbg", (17, 1024))
        h2dbg = DO2("h2dbg", (2, 1024))
        asdbg = DO2("asdbg", (128, 16 * 8))

    S_scr = nc.dram_tensor("S_scr", [17, 1024], F32).ap()
    H_scr = nc.dram_tensor("H_scr", [2, 1024], F32).ap()
    y_dram = nc.dram_tensor("y_dram", [D, TOK], F32).ap()
    if True:
        sC = ExitStack()
        sC.__enter__()
        xn1 = k.sb(sC, "xn1", (128, NKT * TOK), BF16)
        acc = k.sb(sC, "acc", (128, 8 * TOK), BF16)
        h_sb = k.sb(sC, "h_sb", (128, 2 * TOK), BF16)
        selT = k.sb(sC, "selT", (8, TOK), BF16)
        pT = k.sb(sC, "pT", (1, TOK), BF16)
        ones128 = k.sb(sC, "ones128", (128, 1), BF16)
        onesrow = k.sb(sC, "onesrow", (1, 128), BF16)
        id_sb = k.sb(sC, "id_sb", (128, 128), BF16)
        e_sb = k.sb(sC, "e_sb", (8, 1024), BF16)
        b1_sb = k.sb(sC, "b1_sb", (128, 256), F32)
        b2_sb = k.sb(sC, "b2_sb", (8, 1024), BF16)
        Hini = k.sb(sC, "Hini", (128, 16), F32)
        psP0 = k.ps(sC, "psP0", (128, 512))
        psP1 = k.ps(sC, "psP1", (128, 512))
        psQ0 = k.ps(sC, "psQ0", (128, 512))
        psQ1 = k.ps(sC, "psQ1", (128, 512))
        psT9 = k.ps(sC, "psT9", (9, 128), BF16)
        psSel = k.ps(sC, "psSel", (128, 512))
        psX = k.ps(sC, "psX", (128, 512))

        k.op("gpsimd", lambda g: g.memset(ones128[:], 1.0))
        k.op("gpsimd", lambda g: g.memset(onesrow[:], 1.0), incs=[(k.sgp, 1)])
        ones_rdy = [(k.sgp, k.sgp.n)]
        k.load(id_sb[:], id16, "gpsimd")
        k.load(e_sb[:], Esel, "gpsimd")
        k.load(b2_sb[:], b2, "gpsimd")
        k.load(b1_sb[:], b1h)
        k.load(xn1[:].rearrange("p (kt n) -> p kt n", kt=8),
               xn1T.rearrange("(kt p) n -> p kt n", kt=8, p=128), "gpsimd")
        base_ld = k.dma_all()

        with ExitStack() as s1:
            As = k.sb(s1, "As", (128, 16 * 8), F32)
            Bs = k.sb(s1, "Bs", (128, 16 * 8), F32)
            Scur = k.sb(s1, "Scur", (128, 8), F32)
            Sall = k.sb(s1, "Sall", (128, 17 * 8), F32)
            for i in range(16):
                r = i if i < 8 else 15 - i
                slot = 0 if i < 8 else 1
                base = slot * 2048
                k.load(As[:, i * 8:(i + 1) * 8]
                       .rearrange("p (half b) -> p half b", half=2),
                       summ_all[r, base:base + 1024]
                       .rearrange("(half p b) -> p half b", half=2, p=128),
                       slow=True, waits=summ_wait)
                k.load(Bs[:, i * 8:(i + 1) * 8]
                       .rearrange("p (half b) -> p half b", half=2),
                       summ_all[r, base + 1024:base + 2048]
                       .rearrange("(half p b) -> p half b", half=2, p=128),
                       slow=True)
            sld = k.dma_all("sync")
            k.op("vector", lambda v: v.memset(Sall[:, 16 * 8:17 * 8], 0.0))
            k.op("vector", lambda v: v.tensor_copy(out=Scur[:], in_=Bs[:, 0:8]),
                 waits=sld, incs=[(k.svv, 1)])
            k.op("vector", lambda v: v.tensor_copy(out=Sall[:, 0:8], in_=Scur[:]),
                 waits=[(k.svv, k.svv.n)], incs=[(k.svv, 1)])
            for i in range(1, 16):
                k.op("vector", lambda v, i=i: v.tensor_tensor(
                    out=Scur[:], in0=As[:, i * 8:(i + 1) * 8], in1=Scur[:],
                    op=ALU.mult), waits=[(k.svv, k.svv.n)], incs=[(k.svv, 1)])
                k.op("vector", lambda v, i=i: v.tensor_tensor(
                    out=Scur[:], in0=Scur[:], in1=Bs[:, i * 8:(i + 1) * 8],
                    op=ALU.add), waits=[(k.svv, k.svv.n)], incs=[(k.svv, 1)])
                k.op("vector", lambda v, i=i: v.tensor_copy(
                    out=Sall[:, i * 8:(i + 1) * 8], in_=Scur[:]),
                    waits=[(k.svv, k.svv.n)], incs=[(k.svv, 1)])
            cmv = k.svv.n
            stok = []
            for i in range(17):
                stok += k.dma_op("sync", lambda sy, i=i: sy.dma_start(
                    out=S_scr[i:i + 1, :]
                    .rearrange("one (half p b) -> p (one half) b",
                               half=2, p=128),
                    in_=Sall[:, i * 8:(i + 1) * 8]
                    .rearrange("p (half b) -> p half b", half=2),
                    allow_slow_non_contiguous=True),
                    waits=[(k.svv, cmv)] if i == 0 else ())
            sel2_sb = k.sb(s1, "sel2_sb", (17, 2), F32)
            sallT = k.sb(s1, "sallT", (17, 1024), F32)
            h2_sb = k.sb(s1, "h2_sb", (2, 1024), F32)
            seltok = k.load(sel2_sb[:], sel2h)
            k.load(sallT[:], S_scr, waits=stok)
            saltok = k.dma_all("sync")
            h2ev = 0
            for ng in range(2):
                k.op("tensor", lambda t, ng=ng: t.matmul(
                    psSel[0:2, :],
                    sel2_sb[:], sallT[:, ng * 512:(ng + 1) * 512],
                    start=True, stop=True),
                    waits=((saltok + seltok) if ng == 0
                           else [(k.svv, h2ev)]),
                    incs=[(k.smm, 1)])
                mmh = k.smm.n
                k.op("vector", lambda v, ng=ng: v.tensor_copy(
                    out=h2_sb[:, ng * 512:(ng + 1) * 512], in_=psSel[0:2, :]),
                    waits=[(k.smm, mmh)], incs=[(k.svv, 1)])
                h2ev = k.svv.n
            h2v = k.svv.n
            if _dbg:
                k.store(salldbg, Sall[:], waits=[(k.svv, cmv)])
                k.store(asdbg, As[:], waits=[(k.svv, cmv)])
                k.store(saltdbg, sallT[:], waits=[(k.svv, h2v)])
                k.store(h2dbg, h2_sb[:], waits=[(k.svv, h2v)])
            htok = k.dma_op("sync", lambda sy: sy.dma_start(
                out=H_scr, in_=h2_sb[:]), waits=[(k.svv, h2v)])
            hini_ld = []
            for slot in range(2):
                for half in range(2):
                    hini_ld += k.load(
                        Hini[:, (slot * 2 + half) * 4:
                             (slot * 2 + half) * 4 + 4],
                        H_scr[slot, half * 512: half * 512 + 512]
                        .rearrange("(p b) -> p b", p=128),
                        "gpsimd", waits=htok, slow=True)
            if _dbg:
                k.store(hindbg, Hini[:], waits=hini_ld)
            k.barrier()

        with ExitStack() as s2:
            a_sb = k.sb(s2, "a_sb", (128, 2 * TOK), F32)
            b_sb = k.sb(s2, "b_sb", (128, 2 * TOK), F32)
            k.load(a_sb[:].rearrange("p (h n) -> p h n", h=2),
                   aTi.rearrange("(h p) n -> p h n", h=2, p=128))
            k.load(b_sb[:].rearrange("p (h n) -> p h n", h=2),
                   bTi.rearrange("(h p) n -> p h n", h=2, p=128))
            abl = k.dma_all("sync")
            scv = 0
            for slot in range(2):
                for half in range(2):
                    for b in range(B):
                        t0 = half * TOK + b * 512 + slot * 256
                        col = (slot * 2 + half) * 4 + b
                        k.op("vector", lambda v, t0=t0, col=col:
                             v.tensor_tensor_scan(
                                 out=h_sb[:, t0:t0 + 256],
                                 data0=a_sb[:, t0:t0 + 256],
                                 data1=b_sb[:, t0:t0 + 256],
                                 initial=Hini[:, col:col + 1],
                                 op0=ALU.mult, op1=ALU.add),
                             waits=(abl + hini_ld) if scv == 0 else (),
                             incs=[(k.svv, 1)])
                        scv = k.svv.n
            hv = scv

            wc_sb = k.sb(s2, "wc_sb", (128, 2 * 8 * 128), BF16)
            wdr = k.sb(s2, "wdr", (128, 2 * 1024), BF16)
            ystg = k.sb(s2, "ystg", (128, 2 * 512), F32)
            k.load(wc_sb[:].rearrange("p (kt m) -> p kt m", kt=2),
                   wc1T.rearrange("(kt p) m -> p kt m", kt=2, p=128), "gpsimd")
            wcl = k.dma_all("gpsimd")
            accY = Acc([psQ0, psQ1])
            wd_war = [[], []]
            ystg_war = [[], []]
            ytoks = []
            for mt in range(8):
                wslot = mt % 2
                ldd = k.load(wdr[:, wslot * 1024:(wslot + 1) * 1024]
                             .rearrange("p (kt m) -> p kt m", kt=8),
                             wdT.rearrange("(kt p) m -> p kt m", kt=8, p=128)
                             [:, :, mt * 128:(mt + 1) * 128], "gpsimd",
                             waits=wd_war[wslot])
                for tg in range(4):
                    bank, war2, bj = accY.next()
                    for kt in range(2):
                        k.op("tensor", lambda t, kt=kt, mt=mt, tg=tg, bank=bank:
                             t.matmul(bank[:],
                                      wc_sb[:, (kt * 8 + mt) * 128:
                                            (kt * 8 + mt) * 128 + 128],
                                      h_sb[:, kt * TOK + tg * 512:
                                           kt * TOK + tg * 512 + 512],
                                      start=(kt == 0), stop=False),
                             waits=(war2 + wcl + ldd + [(k.svv, hv)] + base_ld)
                             if kt == 0 else ())
                    for kt in range(NKT):
                        k.op("tensor", lambda t, kt=kt, wslot=wslot, tg=tg,
                             bank=bank: t.matmul(
                                 bank[:],
                                 wdr[:, wslot * 1024 + kt * 128:
                                     wslot * 1024 + kt * 128 + 128],
                                 xn1[:, kt * TOK + tg * 512:
                                     kt * TOK + tg * 512 + 512],
                                 start=False, stop=(kt == NKT - 1)),
                             incs=[(k.smm, 1)] if kt == NKT - 1 else ())
                    mmy = k.smm.n
                    if tg == 3:
                        wd_war[wslot] = [(k.smm, mmy)]
                    sg = ystg[:, (tg % 2) * 512:(tg % 2) * 512 + 512]
                    k.op("vector", lambda v, sg=sg, bank=bank: v.tensor_copy(
                        out=sg, in_=bank[:]),
                        waits=[(k.smm, mmy)] + ystg_war[tg % 2],
                        incs=[(k.svv, 1)])
                    accY.done(bj, [(k.svv, k.svv.n)])
                    vvy = k.svv.n
                    ystg_war[tg % 2] = k.store(
                        y_dram[mt * 128:(mt + 1) * 128,
                               tg * 512:(tg + 1) * 512], sg,
                        waits=[(k.svv, vvy)])
                    if _dbg:
                        ystg_war[tg % 2] = ystg_war[tg % 2] + k.store(
                            ydbg[mt * 128:(mt + 1) * 128,
                                 tg * 512:(tg + 1) * 512], sg,
                            waits=[(k.svv, vvy)], queue="gpsimd")
                    ytoks += ystg_war[tg % 2]
            ydone = list(ytoks)
            k.barrier()

        s45 = ExitStack()
        s45.__enter__()
        h1T = k.sb(s45, "h1T", (128, 32 * 1024), BF16)
        w1r = k.sb(s45, "w1r", (128, 4 * 1024), BF16)
        w2r = k.sb(s45, "w2r", (128, 2 * 4096), BF16)
        selBu = k.sb(s45, "selBu", (128, TOK), U8)
        l_sb = k.sb(s45, "l_sb", (128, 16 * 8), F32)
        with ExitStack() as s4:
            spd = k.sb(s4, "spd", (128, 16 * 8), F32) if _dbg else None
            nm = k.sb(s4, "nm", (128, 1), F32)
            eqs = k.sb(s4, "eqs", (128, 8), F32)
            prf = k.sb(s4, "prf", (128, 8), F32)
            esm = k.sb(s4, "esm", (128, 8), F32)
            srw = k.sb(s4, "srw", (128, 1), F32)
            prw = k.sb(s4, "prw", (128, 1), F32)
            rem = k.sb(s4, "rem", (128, 1), F32)
            eq = k.sb(s4, "eq", (128, 1), F32)
            selc = k.sb(s4, "selc", (128, 1), F32)
            sp16 = k.sb(s4, "sp16", (128, 8), BF16)
            p16 = k.sb(s4, "p16", (128, 1), BF16)
            lgl = k.load(l_sb[:].rearrange("p (tt e) -> p tt e", tt=16),
                         logitsT.rearrange("(tt p) e -> p tt e", tt=16, p=128))
            tr_war = []
            tr_done = 0
            for tt in range(16):
                lsl = l_sb[:, tt * 8:(tt + 1) * 8]
                k.op("vector", lambda v, lsl=lsl: v.reduce_max(
                    out=nm[:], in_=lsl, axis=AX, negate=True),
                    waits=lgl + [(k.smm, tr_done)], incs=[(k.svv, 1)])
                nmv = k.svv.n
                k.op("scalar", lambda s, lsl=lsl: s.activation(
                    out=esm[:], in_=lsl, func=AF.Exp, bias=nm[:], scale=1.0),
                    waits=[(k.svv, nmv)], incs=[(k.sev, 1)])
                esv = k.sev.n
                se = lambda: [(k.svv, k.svv.n)]
                k.op("vector", lambda v: v.reduce_sum(
                    out=srw[:], in_=esm[:], axis=AX), waits=[(k.sev, esv)],
                    incs=[(k.svv, 1)])
                k.op("vector", lambda v: v.reciprocal(out=prw[:], in_=srw[:]),
                     waits=se(), incs=[(k.svv, 1)])
                k.op("vector", lambda v: v.tensor_copy(out=p16[:], in_=prw[:]),
                     waits=se(), incs=[(k.svv, 1)])
                k.op("vector", lambda v, lsl=lsl: v.tensor_scalar(
                    out=eqs[:], in0=lsl, scalar1=nm[:], scalar2=0.0,
                    op0=ALU.add, op1=ALU.is_equal), waits=se(),
                    incs=[(k.svv, 1)])
                k.op("vector", lambda v: v.tensor_tensor_scan(
                    out=prf[:], data0=eqs[:], data1=eqs[:], initial=0.0,
                    op0=ALU.add, op1=ALU.bypass), waits=se(),
                    incs=[(k.svv, 1)])
                k.op("vector", lambda v: v.tensor_scalar(
                    out=prf[:], in0=prf[:], scalar1=1.0, scalar2=None,
                    op0=ALU.is_equal), waits=se(), incs=[(k.svv, 1)])
                k.op("vector", lambda v: v.tensor_tensor(
                    out=eqs[:], in0=eqs[:], in1=prf[:], op=ALU.mult),
                    waits=se(), incs=[(k.svv, 1)])
                k.op("vector", lambda v: v.tensor_copy(out=sp16[:], in_=eqs[:]),
                     waits=se(), incs=[(k.svv, 1)])
                spv = k.svv.n
                if _dbg:
                    k.op("vector", lambda v, tt=tt: v.tensor_copy(
                        out=spd[:, tt * 8:(tt + 1) * 8], in_=sp16[:]),
                        incs=[(k.svv, 1)])
                    spv = k.svv.n
                k.op("tensor", lambda t: t.transpose(
                    psT9[0:8, :], sp16[:], id_sb[:]),
                    waits=[(k.svv, spv)] + tr_war, incs=[(k.smm, 1)])
                mmt1 = k.smm.n
                k.op("vector", lambda v, tt=tt: v.tensor_copy(
                    out=selT[0:8, tt * 128:(tt + 1) * 128], in_=psT9[0:8, :]),
                    waits=[(k.smm, mmt1)], incs=[(k.svv, 1)])
                selv = k.svv.n
                k.op("tensor", lambda t: t.transpose(
                    psT9[0:1, :], p16[:], id_sb[:]),
                    waits=[(k.svv, selv)], incs=[(k.smm, 1)])
                mmt2 = k.smm.n
                k.op("vector", lambda v, tt=tt: v.tensor_copy(
                    out=pT[0:1, tt * 128:(tt + 1) * 128], in_=psT9[0:1, :]),
                    waits=[(k.smm, mmt2)], incs=[(k.svv, 1)])
                tr_war = [(k.svv, k.svv.n)]
                tr_done = mmt2
            selv_all = k.svv.n
            if False and _dbg:
                k.store(ldbg, l_sb[:], queue="gpsimd",
                        waits=[(k.svv, selv_all)])
                k.store(spdbg, spd[:], queue="gpsimd",
                        waits=[(k.svv, selv_all)])

        with ExitStack() as s5:
            accM = Acc([psP0, psP1])
            accO = Acc([psQ0, psQ1], init_war=[(k.svv, k.svv.n)])
            w1_war = [[] for _ in range(4)]
            w2_war = [[], []]
            selB_war = []
            h1_war = {}
            for e in range(8):
                sbv = 0
                for tg in range(4):
                    k.op("tensor", lambda t, e=e, tg=tg: t.matmul(
                        psSel[:], e_sb[0:8, e * 128:(e + 1) * 128],
                        selT[0:8, tg * 512:(tg + 1) * 512],
                        start=True, stop=True),
                        waits=(([(k.svv, selv_all)] + base_ld) if tg == 0
                               else []) + selB_war, incs=[(k.smm, 1)])
                    mms = k.smm.n
                    k.op("vector", lambda v, tg=tg: v.tensor_copy(
                        out=selBu[:, tg * 512:(tg + 1) * 512], in_=psSel[:]),
                        waits=[(k.smm, mms)], incs=[(k.svv, 1)])
                    sbv = k.svv.n
                    selB_war = [(k.svv, sbv)]
                for th in range(2):
                    for hm in range(32):
                        ws = hm % 4
                        ld1 = k.load(
                            w1r[:, ws * 1024:(ws + 1) * 1024]
                            .rearrange("p (kt m) -> p kt m", kt=8),
                            w1[e].rearrange("(kt p) m -> p kt m", kt=8, p=128)
                            [:, :, hm * 128:(hm + 1) * 128], "gpsimd",
                            waits=w1_war[ws])
                        for ng in range(2):
                            bank, war2, bj = accM.next()
                            for kt in range(NKT):
                                k.op("tensor", lambda t, kt=kt, ws=ws, th=th,
                                     ng=ng, bank=bank: t.matmul(
                                         bank[:],
                                         w1r[:, ws * 1024 + kt * 128:
                                             ws * 1024 + kt * 128 + 128],
                                         xn1[:, kt * TOK + th * 1024 +
                                             ng * 512: kt * TOK + th * 1024 +
                                             ng * 512 + 512],
                                         start=(kt == 0),
                                         stop=(kt == NKT - 1)),
                                     waits=(ld1 + war2) if kt == 0 else (),
                                     incs=[(k.smm, 1)] if kt == NKT - 1 else ())
                            mm1 = k.smm.n
                            if ng == 1:
                                w1_war[ws] = [(k.smm, mm1)]
                            w = [(k.smm, mm1)] + h1_war.get((hm, ng), base_ld)
                            k.op("scalar", lambda s, e=e, hm=hm, ng=ng,
                                 bank=bank: s.activation(
                                     out=h1T[:, hm * 1024 + ng * 512:
                                             hm * 1024 + ng * 512 + 512],
                                     in_=bank[:], func=AF.Relu,
                                     bias=b1_sb[:, e * 32 + hm:
                                                e * 32 + hm + 1],
                                     scale=1.0),
                                 waits=w, incs=[(k.sev, 1)])
                            accM.done(bj, [(k.sev, k.sev.n)])
                    h1v = k.sev.n
                    for dm in range(8):
                        ws = dm % 2
                        ld2 = k.load(
                            w2r[:, ws * 4096:(ws + 1) * 4096]
                            .rearrange("p (kt m) -> p kt m", kt=32),
                            w2[e].rearrange("(kt p) m -> p kt m", kt=32, p=128)
                            [:, :, dm * 128:(dm + 1) * 128], "gpsimd",
                            waits=w2_war[ws])
                        for ng in range(2):
                            bank, war2, bj = accO.next()
                            for kt in range(32):
                                k.op("tensor", lambda t, kt=kt, ws=ws, ng=ng,
                                     bank=bank: t.matmul(
                                         bank[:],
                                         w2r[:, ws * 4096 + kt * 128:
                                             ws * 4096 + kt * 128 + 128],
                                         h1T[:, kt * 1024 + ng * 512:
                                             kt * 1024 + ng * 512 + 512],
                                         start=(kt == 0), stop=(kt == 31)),
                                     waits=(ld2 + war2 + [(k.sev, h1v)])
                                     if kt == 0 else (),
                                     incs=[(k.smm, 1)] if kt == 31 else ())
                            mm2 = k.smm.n
                            if ng == 1:
                                w2_war[ws] = [(k.smm, mm2)]
                            col = th * 1024 + ng * 512
                            k.op("vector", lambda v, dm=dm, col=col, bank=bank:
                                 v.copy_predicated(
                                     out=acc[:, dm * TOK + col:
                                             dm * TOK + col + 512],
                                     mask=selBu[:, col:col + 512],
                                     data=bank[:]),
                                 waits=[(k.smm, mm2), (k.svv, sbv)],
                                 incs=[(k.svv, 1)])
                            accO.done(bj, [(k.svv, k.svv.n)])
                    for hm in range(32):
                        for ng in range(2):
                            h1_war[(hm, ng)] = [(k.smm, k.smm.n)]
            accv = k.svv.n
        s45.__exit__(None, None, None)
        if True:
            k.barrier()

        with ExitStack() as s6:
            x1r = k.sb(s6, "x1r", (128, 2 * 512), F32)
            yr = k.sb(s6, "yr", (128, 2 * 512), F32)
            t1 = k.sb(s6, "t1", (128, 512), F32)
            t2 = k.sb(s6, "t2", (128, 512), F32)
            t3 = k.sb(s6, "t3", (128, 512), F32)
            ost = k.sb(s6, "ost", (128, 2 * 512), F32)
            xy_war = [[], []]
            ost_war = [[], []]
            accB2 = Acc([psSel, psX], init_war=[(k.svv, accv)])
            for dm in range(8):
                for tg in range(4):
                    sl = tg % 2
                    ldx = k.load(x1r[:, sl * 512: sl * 512 + 512],
                                 x1T[dm * 128:(dm + 1) * 128,
                                     tg * 512:(tg + 1) * 512],
                                 waits=xy_war[sl])
                    ldy = k.load(yr[:, sl * 512: sl * 512 + 512],
                                 y_dram[dm * 128:(dm + 1) * 128,
                                        tg * 512:(tg + 1) * 512],
                                 waits=ydone)
                    bank, war2, bj = accB2.next()
                    k.op("tensor", lambda t, dm=dm, tg=tg, bank=bank: t.matmul(
                        bank[:], b2_sb[0:8, dm * 128:(dm + 1) * 128],
                        selT[0:8, tg * 512:(tg + 1) * 512],
                        start=True, stop=True),
                        waits=war2 + [(k.svv, accv)], incs=[(k.smm, 1)])
                    mmb2 = k.smm.n
                    k.op("vector", lambda v, dm=dm, tg=tg, bank=bank:
                         v.tensor_tensor(
                             out=t1[:], in0=acc[:, dm * TOK + tg * 512:
                                               dm * TOK + tg * 512 + 512],
                             in1=bank[:], op=ALU.add),
                         waits=[(k.smm, mmb2)], incs=[(k.svv, 1)])
                    accB2.done(bj, [(k.svv, k.svv.n)])
                    k.op("tensor", lambda t, tg=tg: t.matmul(
                        psQ0[:], onesrow[:], pT[0:1, tg * 512:(tg + 1) * 512],
                        start=True, stop=True),
                        waits=[(k.svv, k.svv.n)], incs=[(k.smm, 1)])
                    mmpb = k.smm.n
                    k.op("vector", lambda v: v.tensor_tensor(
                        out=t2[:], in0=t1[:], in1=psQ0[:], op=ALU.mult),
                        waits=[(k.smm, mmpb)], incs=[(k.svv, 1)])
                    vt2 = k.svv.n
                    k.op("scalar", lambda s: s.activation(
                        out=t3[:], in_=t2[:], func=AF.Sigmoid),
                        waits=[(k.svv, vt2)], incs=[(k.sev, 1)])
                    evs = k.sev.n
                    k.op("vector", lambda v: v.tensor_tensor(
                        out=t2[:], in0=t2[:], in1=t3[:], op=ALU.mult),
                        waits=[(k.sev, evs)])
                    k.op("vector", lambda v, sl=sl: v.tensor_tensor(
                        out=t2[:], in0=t2[:], in1=yr[:, sl * 512: sl * 512 + 512],
                        op=ALU.mult), waits=ldy)
                    k.op("vector", lambda v, sl=sl: v.tensor_tensor(
                        out=ost[:, sl * 512: sl * 512 + 512],
                        in0=t2[:], in1=x1r[:, sl * 512: sl * 512 + 512],
                        op=ALU.add),
                        waits=ldx + ost_war[sl], incs=[(k.svv, 1)])
                    xy_war[sl] = [(k.svv, k.svv.n)]
                    vvo = k.svv.n
                    ost_war[sl] = k.store(
                        outT[dm * 128:(dm + 1) * 128, tg * 512:(tg + 1) * 512],
                        ost[:, sl * 512: sl * 512 + 512], waits=[(k.svv, vvo)])
            if _dbg:
                k.store(accdbg, acc[:], queue="gpsimd")
                k.store(seldbg, selT[:], queue="gpsimd")
                k.store(pdbg, pT[:], queue="gpsimd")
                k.store(hdbg, h_sb[:], queue="gpsimd")
                pass
            k.wait("sync", k.dma_all())
            k.wait("gpsimd", k.dma_all())
        sC.__exit__(None, None, None)


def build_F():
    nc = _new_nc()
    DI = lambda n, s, d=F32: nc.dram_tensor(n, list(s), d, kind="ExternalInput").ap()
    DO = lambda n, s, d=F32: nc.dram_tensor(n, list(s), d, kind="ExternalOutput").ap()
    NI = lambda n, s, d=F32: nc.dram_tensor(n, list(s), d).ap()
    io = {}
    for n_, sh in (("xbatT", (D, L)), ("xownT", (D, TOK)), ("wqT", (D, 512)),
                   ("wkT", (D, 512)), ("wvT", (D, 512)), ("bqh", (128, 4)),
                   ("bkh", (128, 4)), ("bvrow", (1, 512)), ("woT", (512, D)),
                   ("scaleh", (128, 1)), ("cosT", (128, L)),
                   ("sinST", (128, L)), ("Psw", (128, 128)),
                   ("band", (4 * 128, 512)), ("w3T", (D, 3072)),
                   ("u3T", (D, 3072)), ("b3h", (128, 24)), ("wo0T", (D, D)),
                   ("uo0T", (D, D)), ("bo0h", (128, 8)), ("inch", (128, 8)),
                   ("gateT", (D, 8)), ("boh", (128, 8)), ("waT", (D, CH)),
                   ("wabh", (128, 2)), ("wbT", (D, CH)), ("sel2h", (17, 2)),
                   ("wc1T", (CH, D)), ("wdT", (D, D)), ("w1", (8, D, 4096)),
                   ("b1h", (128, 256)), ("w2", (8, 4096, D)), ("b2", (8, D)),
                   ("Esel", (8, 1024)), ("id16", (128, 128))):
        io[n_] = DI(n_, sh)
    io["outT"] = DO("outT", (D, TOK))
    io["a2a_in"] = NI("fa2a_in", (NCORE, D, 512))
    a2a_out = NI("fa2a_out", (NCORE, D, 512))
    io["gT"] = NI("fgT", (D, TOK))
    io["x1T"] = NI("fx1T", (D, TOK))
    io["xn1T"] = NI("fxn1T", (D, TOK))
    io["aTo"] = NI("faT", (CH, TOK))
    io["bTo"] = NI("fbT", (CH, TOK))
    io["summ"] = NI("fsumm", (1, 4096))
    summ_all = NI("fsumm_all", (NCORE, 4096))
    io["summ_all"] = summ_all
    io["logitsT"] = NI("flogitsT", (TOK, 8))
    io["recv"] = a2a_out
    io["partialT"] = None
    io["dbg_ok"] = False

    with ExitStack() as top:
        k = Kb(nc, top)
        ccA = Sem(top, nc, "ccA")
        ccS = Sem(top, nc, "ccS")
        block = top.enter_context(nc.Block())
        emit_A(k, top, io)
        k.op("gpsimd", lambda g: g.collective_compute(
            "AllToAll", ALU.bypass, replica_groups=[list(range(NCORE))],
            ins=[io["a2a_in"]], outs=[a2a_out]),
            waits=k.dma_all(), incs=[(ccA, 1)])
        io["recv_wait"] = [(ccA, 1)]
        emit_B(k, top, io)
        k.op("gpsimd", lambda g: g.collective_compute(
            "AllGather", ALU.bypass, replica_groups=[list(range(NCORE))],
            ins=[io["summ"]], outs=[summ_all]),
            waits=k.dma_all(), incs=[(ccS, 1)])
        io["summ_wait"] = [(ccS, 1)]
        emit_C(k, top, io)
        with allow_ici_all_to_all():
            k.flush(block)
    return nc


_cache = {}


def _perm64():
    p = []
    for h in range(16):
        p += [64 * h + 2 * i for i in range(32)]
        p += [64 * h + 2 * i + 1 for i in range(32)]
    return np.array(p)


def _rope_tables():
    freqs = 10000.0 ** (-np.arange(0, 64, 2, dtype=np.float64) / 64.0)
    ang = freqs[:, None] * np.arange(L, dtype=np.float64)[None, :]
    c32 = np.cos(ang)
    s32 = np.sin(ang)
    cos128 = np.empty((128, L), np.float32)
    sinS = np.empty((128, L), np.float32)
    for a in range(2):
        cos128[64 * a:64 * a + 32] = c32
        cos128[64 * a + 32:64 * a + 64] = c32
        sinS[64 * a:64 * a + 32] = -s32
        sinS[64 * a + 32:64 * a + 64] = s32
    return cos128, sinS


def _psw():
    P = np.zeros((128, 128), np.float32)
    for a in range(2):
        for i in range(32):
            P[64 * a + i, 64 * a + 32 + i] = 1.0
            P[64 * a + 32 + i, 64 * a + i] = 1.0
    return P


def _band():
    m = np.zeros((4, 128, 512), np.float32)
    for o in range(4):
        for pk in range(128):
            m[o, pk, o * 128 + pk:] = 1.0
    return m.reshape(4 * 128, 512)


def _col128(v):
    return np.ascontiguousarray(v.reshape(-1, 128).T.astype(np.float32))


def _prep_A_inputs(inputs):
    x = np.asarray(inputs["L0_attn_x"], np.float32)
    f32 = lambda a: np.ascontiguousarray(np.asarray(a, np.float32))

    perm = _perm64()
    Wq_p = f32(inputs["L0_attn_Wq"])[perm]
    Wk_p = f32(inputs["L0_attn_Wk"])[perm]
    bq_p = f32(inputs["L0_attn_bq"])[perm]
    bk_p = f32(inputs["L0_attn_bk"])[perm]
    Wv = f32(inputs["L0_attn_Wv"])
    bv = f32(inputs["L0_attn_bv"])
    Wo = f32(inputs["L0_attn_Wo"])
    bo = f32(inputs["L0_attn_bo"])
    scaleh = np.ascontiguousarray(
        np.broadcast_to(f32(inputs["L0_attn_scale"]).reshape(1, 1), (128, 1)))
    cos128, sinS = _rope_tables()
    band = _band()
    Psw = _psw()

    w3 = np.concatenate([f32(inputs["L0_ffn_wf"]), f32(inputs["L0_ffn_wi"]),
                         f32(inputs["L0_ffn_wc"])], 0)
    u3 = np.concatenate([f32(inputs["L0_ffn_uf"]), f32(inputs["L0_ffn_ui"]),
                         f32(inputs["L0_ffn_uc"])], 0)
    b3 = np.concatenate([f32(inputs["L0_ffn_bf"]), f32(inputs["L0_ffn_bi"]),
                         f32(inputs["L0_ffn_bc"])], 0)
    wo0 = f32(inputs["L0_ffn_wo"])
    uo0 = f32(inputs["L0_ffn_uo"])
    bo0 = f32(inputs["L0_ffn_bo"])
    in_c = f32(inputs["L0_ffn_in_c"])

    Wa = f32(inputs["L1_attn_Wa_w"])
    Wab = f32(inputs["L1_attn_Wa_b"])
    Bw = f32(inputs["L1_attn_B_w"])
    Cw = f32(inputs["L1_attn_C_w"])
    Dw = f32(inputs["L1_attn_D_w"])
    gate_w = f32(inputs["L1_ffn_gate_w"])
    w1 = f32(inputs["L1_ffn_w1"])
    b1 = f32(inputs["L1_ffn_b1"])
    w2 = f32(inputs["L1_ffn_w2"])
    b2 = f32(inputs["L1_ffn_b2"])

    Esel = np.zeros((8, 1024), np.float32)
    for e in range(8):
        Esel[e, e * 128:(e + 1) * 128] = 1.0
    id16 = np.eye(128, dtype=np.float32)

    chunks = [(c, 15 - c) for c in range(NCORE)]

    w3T = f32(w3.T)
    u3T = f32(u3.T)
    wo0T = f32(wo0.T)
    uo0T = f32(uo0.T)
    WqT = f32(Wq_p.T)
    WkT = f32(Wk_p.T)
    WvT = f32(Wv.T)
    WoT = f32(Wo.T)
    inA = []
    for c in range(NCORE):
        b0, hh = c // 2, c % 2
        xbatT = f32(x[b0].T)
        xownT = np.empty((D, TOK), np.float32)
        for bb in range(B):
            for slot, ck in enumerate(chunks[c]):
                xownT[:, bb * 512 + slot * 256: bb * 512 + (slot + 1) * 256] = \
                    x[bb, ck * 256:(ck + 1) * 256, :].T
        inA.append(dict(
            xbatT=xbatT, xownT=xownT,
            wqT=f32(WqT[:, 512 * hh:512 * (hh + 1)]),
            wkT=f32(WkT[:, 512 * hh:512 * (hh + 1)]),
            wvT=f32(WvT[:, 512 * hh:512 * (hh + 1)]),
            bqh=_col128(bq_p[512 * hh:512 * (hh + 1)]),
            bkh=_col128(bk_p[512 * hh:512 * (hh + 1)]),
            bvrow=f32(bv[512 * hh:512 * (hh + 1)]).reshape(1, 512),
            woT=f32(WoT[512 * hh:512 * (hh + 1), :]),
            scaleh=scaleh, cosT=cos128, sinST=sinS, Psw=Psw, band=band,
            w3T=w3T, u3T=u3T, b3h=_col128(b3), wo0T=wo0T, uo0T=uo0T,
            bo0h=_col128(bo0), inch=_col128(in_c)))
    return inA, locals()


def _run_A_only(inputs):
    inA, _ = _prep_A_inputs(inputs)
    ncA = _cache.get("A")
    if ncA is None:
        ncA = _cache["A"] = build_A()
    resA = run_bass_kernel_spmd(ncA, inA, core_ids=list(range(NCORE)))
    return resA.results


def kernel(**inputs):
    inA, env = _prep_A_inputs(inputs)
    f32 = env["f32"]
    bo = env["bo"]
    Wa, Wab, Bw, Cw, Dw = env["Wa"], env["Wab"], env["Bw"], env["Cw"], env["Dw"]
    gate_w, w1, b1, w2, b2 = (env["gate_w"], env["w1"], env["b1"], env["w2"],
                              env["b2"])
    Esel, id16, chunks = env["Esel"], env["id16"], env["chunks"]
    WaT = f32(Wa.T)
    WbT = f32(Bw.T)
    wc1T = f32(Cw.T)
    wdT = f32(Dw.T)
    gateTT = f32(gate_w.T)
    b1h = np.ascontiguousarray(
        b1.reshape(8, 32, 128).transpose(2, 0, 1).reshape(128, 256)
        .astype(np.float32))
    boh = _col128(bo)
    wabh = _col128(Wab)
    inF = []
    for c in range(NCORE):
        sel2 = np.zeros((17, 2), np.float32)
        sel2[c - 1 if c >= 1 else 16, 0] = 1.0
        sel2[14 - c, 1] = 1.0
        m = dict(inA[c])
        m.update(gateT=gateTT, boh=boh, waT=WaT, wabh=wabh, wbT=WbT,
                 sel2h=sel2, wc1T=wc1T, wdT=wdT, w1=w1, b1h=b1h, w2=w2,
                 b2=b2, Esel=Esel, id16=id16)
        inF.append(m)
    ncF = _cache.get("F")
    if ncF is None:
        ncF = _cache["F"] = build_F()
    resF = run_bass_kernel_spmd(ncF, inF, core_ids=list(range(NCORE)))

    out = np.empty((B, L, D), np.float32)
    for c in range(NCORE):
        oT = resF.results[c]["outT"]
        for bb in range(B):
            for slot, ck in enumerate(chunks[c]):
                out[bb, ck * 256:(ck + 1) * 256, :] = \
                    oT[:, bb * 512 + slot * 256: bb * 512 + (slot + 1) * 256].T
    return out


def kernel3(**inputs):
    inA, env = _prep_A_inputs(inputs)
    x = env["x"]
    f32 = env["f32"]
    bo = env["bo"]
    Wa, Wab, Bw, Cw, Dw = env["Wa"], env["Wab"], env["Bw"], env["Cw"], env["Dw"]
    gate_w, w1, b1, w2, b2 = (env["gate_w"], env["w1"], env["b1"], env["w2"],
                              env["b2"])
    Esel, id16, chunks = env["Esel"], env["id16"], env["chunks"]
    ncA = _cache.get("A")
    if ncA is None:
        ncA = _cache["A"] = build_A()
    resA = run_bass_kernel_spmd(ncA, inA, core_ids=list(range(NCORE)))
    partial = [resA.results[c]["partialT"] for c in range(NCORE)]
    gouts = [resA.results[c]["gT"] for c in range(NCORE)]

    WaT = f32(Wa.T)
    WbT = f32(Bw.T)
    inB = []
    for c in range(NCORE):
        cols = np.concatenate([np.arange(ck * 256, (ck + 1) * 256)
                               for ck in chunks[c]])
        recv = np.stack([np.ascontiguousarray(partial[i][:, cols])
                         for i in range(NCORE)])
        inB.append(dict(
            xownT=inA[c]["xownT"], gT=gouts[c], recv=recv, boh=_col128(bo),
            waT=WaT, wabh=_col128(Wab), wbT=WbT, gateT=f32(gate_w.T)))
    ncB = _cache.get("B")
    if ncB is None:
        ncB = _cache["B"] = build_B()
    resB = run_bass_kernel_spmd(ncB, inB, core_ids=list(range(NCORE)))
    summ_all = np.stack([resB.results[c]["summ"][0] for c in range(NCORE)])

    wc1T = f32(Cw.T)
    wdT = f32(Dw.T)
    b1h = np.ascontiguousarray(
        b1.reshape(8, 32, 128).transpose(2, 0, 1).reshape(128, 256)
        .astype(np.float32))
    inC = []
    for c in range(NCORE):
        sel2 = np.zeros((17, 2), np.float32)
        sel2[c - 1 if c >= 1 else 16, 0] = 1.0
        sel2[14 - c, 1] = 1.0
        inC.append(dict(
            x1T=resB.results[c]["x1T"], xn1T=resB.results[c]["xn1T"],
            aTi=resB.results[c]["aTo"], bTi=resB.results[c]["bTo"],
            summ_all=summ_all, sel2h=sel2, wc1T=wc1T, wdT=wdT,
            logitsT=resB.results[c]["logitsT"],
            w1=w1, b1h=b1h, w2=w2, b2=b2, Esel=Esel, id16=id16))
    ncC = _cache.get("C")
    if ncC is None:
        ncC = _cache["C"] = build_C()
    resC = run_bass_kernel_spmd(ncC, inC, core_ids=list(range(NCORE)))
    _cache["lastC"] = resC
    _cache["lastB"] = resB

    out = np.empty((B, L, D), np.float32)
    for c in range(NCORE):
        oT = resC.results[c]["outT"]
        for bb in range(B):
            for slot, ck in enumerate(chunks[c]):
                out[bb, ck * 256:(ck + 1) * 256, :] = \
                    oT[:, bb * 512 + slot * 256: bb * 512 + (slot + 1) * 256].T
    return out


# version 06
# speedup vs baseline: 1.0636x; 1.0035x over previous
import numpy as np
import ml_dtypes
from contextlib import ExitStack

import concourse.bass as bass
import concourse.mybir as mybir
from concourse.bass_utils import run_bass_kernel_spmd
from concourse.aot_env import allow_ici_all_to_all

F32 = mybir.dt.float32
BF16 = mybir.dt.bfloat16
U8 = mybir.dt.uint8
I32 = mybir.dt.int32
ALU = mybir.AluOpType
AF = mybir.ActivationFunctionType
AX = mybir.AxisListType.X

NCORE = 8
B = 4
L = 4096
D = 1024
CH = 256
TOK = 2048
NKT = 8
EPS = 1e-5
BF = ml_dtypes.bfloat16


class Sem:
    def __init__(self, stack, nc, name):
        self.h = stack.enter_context(nc.semaphore(name))
        self.n = 0


class DmaRing:

    def __init__(self, stack, nc, name, n=10):
        self.sems = [Sem(stack, nc, f"{name}{i}") for i in range(n)]
        self.i = 0

    def take(self):
        s = self.sems[self.i % len(self.sems)]
        self.i += 1
        gate = [(s, s.n)]
        return s, gate

    def all(self):
        return [(s, s.n) for s in self.sems if s.n > 0]


class Kb:
    ENGS = ("tensor", "vector", "scalar", "sync", "gpsimd")

    def __init__(self, nc, stack):
        self.nc = nc
        self.q = {e: [] for e in self.ENGS}
        self.rings = {"sync": DmaRing(stack, nc, "rs"),
                      "gpsimd": DmaRing(stack, nc, "rg")}
        self.smm = Sem(stack, nc, "smm")
        self.sev = Sem(stack, nc, "sev")
        self.svv = Sem(stack, nc, "svv")
        self.sgp = Sem(stack, nc, "sgp")

    _uid = [0]

    def sb(self, st, name, shape, dt):
        Kb._uid[0] += 1
        return st.enter_context(self.nc.sbuf_tensor(
            f"{name}_{Kb._uid[0]}", list(shape), dt))

    def ps(self, st, name, shape, dt=F32):
        Kb._uid[0] += 1
        return st.enter_context(self.nc.psum_tensor(
            f"{name}_{Kb._uid[0]}", list(shape), dt))

    def op(self, eng, build, waits=(), incs=()):
        wv = [(s.h, v) for (s, v) in waits if v > 0]
        iv = []
        for s, kk in incs:
            s.n += kk
            iv.append((s.h, kk))

        def run(e, wv=wv, iv=iv, build=build):
            for h, v in wv:
                e.wait_ge(h, v)
            ins = build(e)
            for h, kk in iv:
                ins.then_inc(h, kk)
        self.q[eng].append(run)

    def wait(self, eng, waits):
        wv = [(s.h, v) for (s, v) in waits if v > 0]
        if not wv:
            return

        def run(e, wv=wv):
            for h, v in wv:
                e.wait_ge(h, v)
        self.q[eng].append(run)

    def barrier(self):
        w = [(s, s.n) for s in (self.smm, self.sev, self.svv, self.sgp)]
        w += self.rings["sync"].all() + self.rings["gpsimd"].all()
        for e in self.ENGS:
            self.wait(e, w)

    def dma_all(self, queue=None):
        if queue:
            return self.rings[queue].all()
        return self.rings["sync"].all() + self.rings["gpsimd"].all()

    def dma_op(self, queue, build, waits=()):
        s, gate = self.rings[queue].take()
        self.op(queue, build, waits=list(waits) + gate, incs=[(s, 16)])
        return [(s, s.n)]

    def flush(self, block):
        for name in self.ENGS:
            fns = self.q[name]
            if not fns:
                continue

            def body(e, fns=fns):
                for f in fns:
                    f(e)
            getattr(block, name)(body)

    def load(self, dst, src, queue="sync", waits=(), slow=False):
        if slow:
            return self.dma_op(queue, lambda e, d=dst, sr=src: e.dma_start(
                out=d, in_=sr, allow_slow_non_contiguous=True), waits=waits)
        return self.dma_op(queue, lambda e, d=dst, sr=src: e.dma_start(
            out=d, in_=sr), waits=waits)

    def store(self, dst, src, waits=(), queue="sync"):
        return self.dma_op(queue, lambda e, d=dst, sr=src: e.dma_start(
            out=d, in_=sr), waits=waits)


class Acc:
    def __init__(self, banks, init_war=None):
        self.banks = banks
        self.war = [list(init_war or []) for _ in banks]
        self.i = 0

    def next(self):
        j = self.i % len(self.banks)
        self.i += 1
        return self.banks[j], list(self.war[j]), j

    def done(self, j, waits):
        self.war[j] = list(waits)


_LN_N = [0]


def ln_env(k, st, ones128, onesrow, psS, psBr, psBm, pre=None, dt=BF16):
    _LN_N[0] += 1
    tag = str(_LN_N[0])
    return dict(ones128=ones128, onesrow=onesrow, psS=psS, psBr=psBr, psBm=psBm,
                srow=k.sb(st, "ln_srow" + tag, (1, 4 * 512), F32),
                st16=k.sb(st, "ln_st16" + tag, (1, 2 * 512), dt),
                sq0=k.sb(st, "ln_sq0" + tag, (128, 512), dt),
                sq1=k.sb(st, "ln_sq1" + tag, (128, 512), dt),
                stats_war=[], sq_war=[[], []], bc_war=[], pre=pre or [])


def emit_ln_group(k, X, colstride, g, env):
    ones128, onesrow = env["ones128"], env["onesrow"]
    psS, psBr, psBm = env["psS"], env["psBr"], env["psBm"]
    srow, st16 = env["srow"], env["st16"]
    sqs = (env["sq0"], env["sq1"])

    def Xs(kt):
        return X[:, kt * colstride + g * 512: kt * colstride + g * 512 + 512]

    for kt in range(NKT):
        k.op("tensor", lambda t, kt=kt: t.matmul(
            psS[0:1, :], ones128[:, 0:1], Xs(kt),
            start=(kt == 0), stop=(kt == NKT - 1)),
            waits=(env["pre"] + env["stats_war"]) if kt == 0 else ())
    for kt in range(NKT):
        sq = sqs[kt % 2]
        k.op("scalar", lambda s, kt=kt, sq=sq: s.activation(
            out=sq[:], in_=Xs(kt), func=AF.Square),
            waits=(env["pre"] if kt == 0 else []) + env["sq_war"][kt % 2],
            incs=[(k.sev, 1)])
        evq = k.sev.n
        k.op("tensor", lambda t, kt=kt, sq=sq: t.matmul(
            psBr[0:1, :], ones128[:, 0:1], sq[:],
            start=(kt == 0), stop=(kt == NKT - 1)),
            waits=[(k.sev, evq)] + (env["bc_war"] if kt == 0 else []),
            incs=[(k.smm, 1)])
        env["sq_war"][kt % 2] = [(k.smm, k.smm.n)]
    mmq = k.smm.n
    mrow = srow[0:1, 0:512]
    vrow = srow[0:1, 512:1024]
    rrow = srow[0:1, 1024:1536]
    trow = srow[0:1, 1536:2048]
    k.op("vector", lambda v: v.tensor_scalar(
        out=mrow, in0=psS[0:1, :], scalar1=1.0 / D, scalar2=None, op0=ALU.mult),
        waits=[(k.smm, mmq)])
    k.op("vector", lambda v: v.tensor_scalar(
        out=vrow, in0=psBr[0:1, :], scalar1=1.0 / D, scalar2=None, op0=ALU.mult))
    k.op("vector", lambda v: v.tensor_tensor(out=trow, in0=mrow, in1=mrow,
                                             op=ALU.mult))
    k.op("vector", lambda v: v.tensor_tensor(out=vrow, in0=vrow, in1=trow,
                                             op=ALU.subtract), incs=[(k.svv, 1)])
    env["stats_war"] = [(k.svv, k.svv.n)]
    vvq = k.svv.n
    k.op("vector", lambda v: v.tensor_scalar(
        out=vrow, in0=vrow, scalar1=EPS, scalar2=None, op0=ALU.add),
        incs=[(k.svv, 1)])
    vve = k.svv.n
    k.op("scalar", lambda s: s.activation(out=rrow, in_=vrow, func=AF.Ln),
         waits=[(k.svv, vve)], incs=[(k.sev, 1)])
    evl = k.sev.n
    k.op("vector", lambda v: v.tensor_scalar(
        out=rrow, in0=rrow, scalar1=-0.5, scalar2=None, op0=ALU.mult),
        waits=[(k.sev, evl)], incs=[(k.svv, 1)])
    vvh = k.svv.n
    k.op("scalar", lambda s: s.activation(out=rrow, in_=rrow, func=AF.Exp),
         waits=[(k.svv, vvh)],
         incs=[(k.sev, 1)])
    evr = k.sev.n
    k.op("vector", lambda v: v.tensor_tensor(out=trow, in0=mrow, in1=rrow,
                                             op=ALU.mult), waits=[(k.sev, evr)])
    k.op("vector", lambda v: v.tensor_copy(out=st16[0:1, 0:512], in_=rrow))
    k.op("vector", lambda v: v.tensor_copy(out=st16[0:1, 512:1024], in_=trow),
         incs=[(k.svv, 1)])
    vvr = k.svv.n
    k.op("tensor", lambda t: t.matmul(psBr[:], onesrow[:], st16[0:1, 0:512],
                                      start=True, stop=True),
         waits=[(k.svv, vvr)] + env["bc_war"])
    k.op("tensor", lambda t: t.matmul(psBm[:], onesrow[:], st16[0:1, 512:1024],
                                      start=True, stop=True), incs=[(k.smm, 1)])
    mmb = k.smm.n
    for kt in range(NKT):
        k.op("vector", lambda v, kt=kt: v.tensor_tensor(
            out=Xs(kt), in0=Xs(kt), in1=psBr[:], op=ALU.mult),
            waits=[(k.smm, mmb)] if kt == 0 else ())
        k.op("vector", lambda v, kt=kt: v.tensor_tensor(
            out=Xs(kt), in0=Xs(kt), in1=psBm[:], op=ALU.subtract),
            incs=[(k.svv, 1)])
    env["bc_war"] = [(k.svv, k.svv.n)]
    return k.svv.n


A_INPUTS = ("xbatT", "xownT", "wqT", "wkT", "wvT", "bqh", "bkh", "bvrow",
            "woT", "scaleh", "cosT", "sinST", "Psw", "band", "w3T", "u3T",
            "b3h", "wo0T", "uo0T", "bo0h", "inch")
B_INPUTS = ("xownT", "gateT", "boh", "waT", "wabh", "wbT")
C_INPUTS = ("sel2h", "wc1T", "wdT", "w1", "b1h", "w2", "b2", "Esel", "id16")


def _new_nc():
    return bass.Bass("TRN2", target_bir_lowering=False, debug=False,
                     detect_race_conditions=False)


def build_A():
    nc = _new_nc()
    DI = lambda n, s, d=F32: nc.dram_tensor(n, list(s), d, kind="ExternalInput").ap()
    DO = lambda n, s, d=F32: nc.dram_tensor(n, list(s), d, kind="ExternalOutput").ap()
    io = {}
    io["xbatT"] = DI("xbatT", (D, L))
    io["xownT"] = DI("xownT", (D, TOK))
    for n_, sh in (("wqT", (D, 512)), ("wkT", (D, 512)), ("wvT", (D, 512)),
                   ("bqh", (128, 4)), ("bkh", (128, 4)), ("bvrow", (1, 512)),
                   ("woT", (512, D)), ("scaleh", (128, 1)), ("cosT", (128, L)),
                   ("sinST", (128, L)), ("Psw", (128, 128)),
                   ("band", (4 * 128, 512)), ("w3T", (D, 3072)),
                   ("u3T", (D, 3072)), ("b3h", (128, 24)), ("wo0T", (D, D)),
                   ("uo0T", (D, D)), ("bo0h", (128, 8)), ("inch", (128, 8))):
        io[n_] = DI(n_, sh)
    io["partialT"] = DO("partialT", (D, L))
    io["gT"] = DO("gT", (D, TOK))
    with ExitStack() as top:
        k = Kb(nc, top)
        block = top.enter_context(nc.Block())
        emit_A(k, top, io)
        k.flush(block)
    return nc


def emit_A(k, top, io):
    nc = k.nc
    (xbatT, xownT, wqT, wkT, wvT, bqh, bkh, bvrow, woT, scaleh, cosT, sinST,
     Psw, band, w3T, u3T, b3h, wo0T, uo0T, bo0h, inch) = (
        io[n_] for n_ in A_INPUTS)
    gT = io["gT"]
    partialT = io.get("partialT")
    a2a_in = io.get("a2a_in")
    _dbg = False
    if True:
        sA = ExitStack()
        sA.__enter__()
        xring = k.sb(sA, "xring", (128, 2 * NKT * 512), BF16)
        qT = k.sb(sA, "qT", (128, 4 * L), BF16)
        kTt = k.sb(sA, "kTt", (128, 4 * L), BF16)
        vS = k.sb(sA, "vS", (128, 32 * 520), BF16)
        oT = k.sb(sA, "oT", (128, 4 * L), BF16)
        rrow = k.sb(sA, "rrow", (1, 512), F32)
        r16row = k.sb(sA, "r16row", (1, 512), BF16)
        ones128 = k.sb(sA, "ones128", (128, 1), BF16)
        onesrow = k.sb(sA, "onesrow", (1, 128), BF16)

        psST0 = k.ps(sA, "psST0", (128, 512))
        psST1 = k.ps(sA, "psST1", (128, 512))
        psO = k.ps(sA, "psO", (65, 512))
        psC = k.ps(sA, "psC", (128, 512))
        psD = k.ps(sA, "psD", (128, 512))
        psE = k.ps(sA, "psE", (128, 512))
        psF = k.ps(sA, "psF", (128, 512))
        psG = k.ps(sA, "psG", (128, 512))

        k.op("gpsimd", lambda g: g.memset(ones128[:], 1.0))
        k.op("gpsimd", lambda g: g.memset(onesrow[:], 1.0), incs=[(k.sgp, 1)])
        ones_rdy = [(k.sgp, k.sgp.n)]

        with ExitStack() as sW:
            wq_sb = k.sb(sW, "wq_sb", (128, 8 * 512), BF16)
            wk_sb = k.sb(sW, "wk_sb", (128, 8 * 512), BF16)
            wv_sb = k.sb(sW, "wv_sb", (128, 8 * 512), BF16)
            bq_sb = k.sb(sW, "bq_sb", (128, 4), F32)
            bk_sb = k.sb(sW, "bk_sb", (128, 4), F32)
            bv_sb = k.sb(sW, "bv_sb", (1, 512), BF16)
            cos_sb = k.sb(sW, "cos_sb", (128, L), BF16)
            sin_sb = k.sb(sW, "sin_sb", (128, L), BF16)
            psw_sb = k.sb(sW, "psw_sb", (128, 128), BF16)
            scol = k.sb(sW, "scol", (128, 1), F32)
            rt1 = k.sb(sW, "rt1", (128, 512), BF16)
            rt2 = k.sb(sW, "rt2", (128, 512), BF16)

            for wsb, wdr in ((wq_sb, wqT), (wk_sb, wkT), (wv_sb, wvT)):
                k.load(wsb[:].rearrange("p (kt m) -> p kt m", kt=8),
                       wdr.rearrange("(kt p) m -> p kt m", kt=8, p=128), "gpsimd")
            k.load(bv_sb[:], bvrow, "gpsimd")
            k.load(psw_sb[:], Psw, "gpsimd")
            k.load(scol[:], scaleh)
            k.load(bq_sb[:], bqh)
            k.load(bk_sb[:], bkh)
            k.load(cos_sb[:], cosT, "gpsimd")
            k.load(sin_sb[:], sinST, "gpsimd")
            wload = k.dma_all()

            env = ln_env(k, sW, ones128, onesrow, psE, psF, psG, pre=ones_rdy)
            accQK = Acc([psC, psD])
            qkv_done = []
            for g8 in range(8):
                slot = g8 % 2
                Xv = xring[:, slot * NKT * 512:(slot + 1) * NKT * 512]

                def Xb(kt, Xv=Xv):
                    return Xv[:, kt * 512:(kt + 1) * 512]

                war = [(k.svv, qkv_done[g8 - 2])] if g8 >= 2 else []
                ld = k.load(Xv.rearrange("p (kt n) -> p kt n", kt=8),
                            xbatT.rearrange("(kt p) l -> p kt l", kt=8, p=128)
                            [:, :, g8 * 512:(g8 + 1) * 512], "gpsimd", waits=war)
                env["pre"] = ld + ones_rdy
                lnv = emit_ln_group(k, Xv, 512, 0, env)
                for (wsb, bsb, dstT) in ((wq_sb, bq_sb, qT), (wk_sb, bk_sb, kTt)):
                    for mt in range(4):
                        bank, war2, bj = accQK.next()
                        for kt in range(NKT):
                            k.op("tensor", lambda t, kt=kt, mt=mt, w=wsb,
                                 bank=bank, Xb=Xb: t.matmul(
                                     bank[:],
                                     w[:, kt * 512 + mt * 128:
                                       kt * 512 + mt * 128 + 128],
                                     Xb(kt), start=(kt == 0),
                                     stop=(kt == NKT - 1)),
                                 waits=([(k.svv, lnv)] + war2 + wload)
                                 if kt == 0 else (),
                                 incs=[(k.smm, 1)] if kt == NKT - 1 else ())
                        mmq = k.smm.n
                        k.op("vector", lambda v, mt=mt, b=bsb, d=dstT, g8=g8,
                             bank=bank: v.tensor_scalar(
                                 out=d[:, mt * L + g8 * 512:
                                       mt * L + g8 * 512 + 512],
                                 in0=bank[:], scalar1=b[:, mt:mt + 1],
                                 scalar2=None, op0=ALU.add),
                             waits=[(k.smm, mmq)], incs=[(k.svv, 1)])
                        accQK.done(bj, [(k.svv, k.svv.n)])
                for tmt in range(4):
                    bank, war2, bj = accQK.next()
                    ktile = g8 * 4 + tmt
                    for kt in range(NKT):
                        k.op("tensor", lambda t, kt=kt, tmt=tmt, bank=bank,
                             Xb=Xb: t.matmul(
                                 bank[:], Xb(kt)[:, tmt * 128:(tmt + 1) * 128],
                                 wv_sb[:, kt * 512:(kt + 1) * 512],
                                 start=(kt == 0), stop=False),
                             waits=(war2 + [(k.svv, lnv)] + wload)
                             if kt == 0 else ())
                    k.op("tensor", lambda t, bank=bank: t.matmul(
                        bank[:], onesrow[:], bv_sb[:], start=False, stop=True),
                        incs=[(k.smm, 1)])
                    mmv = k.smm.n
                    for h in range(8):
                        k.op("vector", lambda v, h=h, kt2=ktile, bank=bank:
                             v.tensor_copy(
                                 out=vS[:, kt2 * 520 + h * 65:
                                        kt2 * 520 + h * 65 + 64],
                                 in_=bank[:, h * 64: h * 64 + 64]),
                             waits=[(k.smm, mmv)] if h == 0 else (),
                             incs=[(k.svv, 1)] if h == 7 else ())
                    accQK.done(bj, [(k.svv, k.svv.n)])
                qkv_done.append(k.svv.n)

            k.op("gpsimd", lambda g: g.memset(
                vS[:].rearrange("p (kt h s) -> p kt h s", kt=32, h=8)
                [:, :, :, 64:65], 1.0),
                waits=[(k.svv, k.svv.n)], incs=[(k.sgp, 1)])
            vones = k.sgp.n

            accR = Acc([psC, psD], init_war=[(k.svv, k.svv.n)])
            for dstT in (qT, kTt):
                for pt in range(4):
                    for ng in range(8):
                        bank, war2, bj = accR.next()
                        sl = dstT[:, pt * L + ng * 512: pt * L + ng * 512 + 512]
                        k.op("tensor", lambda t, bank=bank, sl=sl: t.matmul(
                            bank[:], psw_sb[:], sl, start=True, stop=True),
                            waits=war2, incs=[(k.smm, 1)])
                        mmr = k.smm.n
                        csl = cos_sb[:, ng * 512:(ng + 1) * 512]
                        ssl = sin_sb[:, ng * 512:(ng + 1) * 512]
                        k.op("vector", lambda v, sl=sl, csl=csl: v.tensor_tensor(
                            out=rt1[:], in0=sl, in1=csl, op=ALU.mult),
                            waits=[(k.smm, mmr)])
                        k.op("vector", lambda v, bank=bank, ssl=ssl:
                             v.tensor_tensor(out=rt2[:], in0=bank[:], in1=ssl,
                                             op=ALU.mult))
                        k.op("vector", lambda v, sl=sl: v.tensor_tensor(
                            out=sl, in0=rt1[:], in1=rt2[:], op=ALU.add),
                            incs=[(k.svv, 1)])
                        accR.done(bj, [(k.svv, k.svv.n)])
            k.op("vector", lambda v: v.tensor_scalar(
                out=qT[:], in0=qT[:], scalar1=scol[:], scalar2=None,
                op0=ALU.mult), waits=wload, incs=[(k.svv, 1)])
            k.barrier()

        with ExitStack() as sAtt:
            at = [k.sb(sAtt, f"at{i}", (128, 512), BF16) for i in range(4)]
            band_sb = k.sb(sAtt, "band_sb", (128, 4 * 512), BF16)
            k.load(band_sb[:].rearrange("p (o n) -> p o n", o=4),
                   band.rearrange("(o p) n -> p o n", o=4, p=128), "gpsimd")
            bnd = k.dma_all("gpsimd")
            k.wait("vector", bnd)
            k.wait("scalar", bnd)
            k.wait("tensor", bnd + [(k.sgp, vones)])
            stb_a = (psST0, psC)
            stb_b = (psST1, psD)
            at_a = (at[0], at[2])
            at_b = (at[1], at[3])
            psO2 = psG
            exp_c = {}
            mk_c = {}
            av_c = {}
            last_exp = 0
            last_av = 0
            rr_war = 0
            accNrm = Acc([psE, psF])
            o_ev = []

            def emit_st2(j2, hp, qg, kt):
                for a, (stbk, atk, ho) in enumerate(
                        ((stb_a, at_a, 0), (stb_b, at_b, 64))):
                    h = 2 * hp + a
                    w = []
                    if kt >= 2:
                        w.append((k.sev, exp_c[(j2, kt - 2, a)]))
                    elif j2 > 0 and a == 0:
                        w.append((k.sev, last_exp))
                    k.op("tensor", lambda t, kt=kt, hp=hp, ho=ho, qg=qg,
                         bank=stbk[kt % 2]: t.matmul(
                             bank[:],
                             kTt[ho:ho + 64, hp * L + kt * 128:
                                 hp * L + kt * 128 + 128],
                             qT[ho:ho + 64, hp * L + qg * 512:
                                hp * L + qg * 512 + 512],
                             start=True, stop=True),
                         waits=w, incs=[(k.smm, 1)])
                    stc = k.smm.n
                    w = [(k.smm, stc)]
                    if kt >= 2:
                        w.append((k.smm, av_c[(j2, kt - 2, a)]))
                    elif j2 > 0 and a == 0:
                        w.append((k.smm, last_av))
                    k.op("scalar", lambda sc, kt=kt, bank=stbk[kt % 2],
                         atile=atk[kt % 2]: sc.activation(
                             out=atile[:], in_=bank[:], func=AF.Exp),
                         waits=w, incs=[(k.sev, 1)])
                    exp_c[(j2, kt, a)] = k.sev.n
                    o = kt - 4 * qg
                    if o >= 0:
                        k.op("vector", lambda v, o=o, atile=atk[kt % 2]:
                             v.tensor_tensor(
                                 out=atile[:], in0=atile[:],
                                 in1=band_sb[:, o * 512:(o + 1) * 512],
                                 op=ALU.mult),
                             waits=[(k.sev, exp_c[(j2, kt, a)])],
                             incs=[(k.svv, 1)])
                        mk_c[(j2, kt, a)] = k.svv.n

            def emit_av2(j2, hp, qg, kt, K):
                o = kt - 4 * qg
                for a, (atk, ob) in enumerate(((at_a, psO), (at_b, psO2))):
                    h = 2 * hp + a
                    w = ([(k.svv, mk_c[(j2, kt, a)])] if o >= 0
                         else [(k.sev, exp_c[(j2, kt, a)])])
                    if kt == 0 and j2 > 0 and a == 0:
                        w += [(k.sev, o_ev[j2 - 1][0]),
                              (k.svv, o_ev[j2 - 1][1])]
                    k.op("tensor", lambda t, kt=kt, h=h, K=K, ob=ob,
                         atile=atk[kt % 2]: t.matmul(
                             ob[0:65, :],
                             vS[:, kt * 520 + h * 65: kt * 520 + h * 65 + 65],
                             atile[:], start=(kt == 0), stop=(kt == K - 1)),
                         waits=w, incs=[(k.smm, 1)])
                    av_c[(j2, kt, a)] = k.smm.n

            for hp in range(4):
                for qg in range(8):
                    j2 = hp * 8 + qg
                    K = 4 * qg + 4
                    for kt in range(K):
                        emit_st2(j2, hp, qg, kt)
                        if kt >= 1:
                            emit_av2(j2, hp, qg, kt - 1, K)
                    emit_av2(j2, hp, qg, K - 1, K)
                    last_exp = exp_c[(j2, K - 1, 1)]
                    last_av = av_c[(j2, K - 1, 1)]
                    evs = []
                    for a, ob in ((0, psO), (1, psO2)):
                        ho = 64 * a
                        avl = av_c[(j2, K - 1, a)]
                        osl = oT[ho:ho + 64, hp * L + qg * 512:
                                 hp * L + qg * 512 + 512]
                        k.op("scalar", lambda sc, osl=osl, ob=ob: sc.copy(
                            out=osl, in_=ob[0:64, :]), waits=[(k.smm, avl)],
                            incs=[(k.sev, 1)])
                        oev = k.sev.n
                        k.op("vector", lambda v, ob=ob: v.reciprocal(
                            out=rrow[:], in_=ob[64:65, :]),
                            waits=[(k.smm, avl), (k.svv, rr_war)],
                            incs=[(k.svv, 1)])
                        rcv = k.svv.n
                        k.op("vector", lambda v: v.tensor_copy(
                            out=r16row[:], in_=rrow[:]),
                            waits=[(k.svv, k.svv.n)], incs=[(k.svv, 1)])
                        r16v = k.svv.n
                        nbank, nwar, nbj = accNrm.next()
                        k.op("tensor", lambda t, nbank=nbank: t.matmul(
                            nbank[0:64, :], onesrow[:, 0:64], r16row[:],
                            start=True, stop=True),
                            waits=[(k.svv, r16v)] + nwar, incs=[(k.smm, 1)])
                        mmn = k.smm.n
                        k.op("vector", lambda v, osl=osl, nbank=nbank:
                             v.tensor_tensor(out=osl, in0=osl,
                                             in1=nbank[0:64, :], op=ALU.mult),
                             waits=[(k.smm, mmn), (k.sev, oev)],
                             incs=[(k.svv, 1)])
                        accNrm.done(nbj, [(k.svv, k.svv.n)])
                        rr_war = k.svv.n
                        evs.append((oev, rcv))
                    o_ev.append((max(e for e, r in evs),
                                  max(r for e, r in evs)))
            normv = k.svv.n
            k.barrier()

        with ExitStack() as sProj:
            wo_sb = k.sb(sProj, "wo_sb", (128, 4 * 8 * 128), BF16)
            stg = (k.sb(sProj, "stg0", (128, 512), F32),
                   k.sb(sProj, "stg1", (128, 512), F32))
            wl = k.load(wo_sb[:].rearrange("p (kt m) -> p kt m", kt=4),
                        woT.rearrange("(kt p) m -> p kt m", kt=4, p=128),
                        "gpsimd")
            accP = Acc([psC, psD])
            stg_war = [[], []]
            for mt in range(8):
                for ng in range(8):
                    bank, war2, bj = accP.next()
                    for kt in range(4):
                        k.op("tensor", lambda t, kt=kt, mt=mt, ng=ng, bank=bank:
                             t.matmul(bank[:],
                                      wo_sb[:, (kt * 8 + mt) * 128:
                                            (kt * 8 + mt) * 128 + 128],
                                      oT[:, kt * L + ng * 512:
                                         kt * L + ng * 512 + 512],
                                      start=(kt == 0), stop=(kt == 3)),
                             waits=(war2 + wl + [(k.svv, normv)])
                             if kt == 0 else (),
                             incs=[(k.smm, 1)] if kt == 3 else ())
                    mmp = k.smm.n
                    sg = stg[ng % 2]
                    k.op("vector", lambda v, sg=sg, bank=bank: v.tensor_copy(
                        out=sg[:], in_=bank[:]),
                        waits=[(k.smm, mmp)] + stg_war[ng % 2],
                        incs=[(k.svv, 1)])
                    accP.done(bj, [(k.svv, k.svv.n)])
                    vvp = k.svv.n
                    if a2a_in is None:
                        stg_war[ng % 2] = k.store(
                            partialT[mt * 128:(mt + 1) * 128,
                                     ng * 512:(ng + 1) * 512], sg[:],
                            waits=[(k.svv, vvp)])
                    else:
                        toks = []
                        for half in (0, 1):
                            ck = 2 * ng + half
                            r_ = ck if ck < 8 else 15 - ck
                            pos = 0 if ck < 8 else 1
                            toks += k.store(
                                a2a_in[r_, mt * 128:(mt + 1) * 128,
                                       pos * 256:(pos + 1) * 256],
                                sg[:, half * 256:(half + 1) * 256],
                                waits=[(k.svv, vvp)])
                        stg_war[ng % 2] = toks
            if _dbg:
                k.store(qdbg, qT[:], queue="gpsimd")
                k.store(kdbg, kTt[:], queue="gpsimd")
                k.store(odbg, oT[:], queue="gpsimd")
                k.store(vdbg, vS[:], queue="gpsimd")
            k.barrier()

        sA.close()

        sF = ExitStack()
        sF.__enter__()
        ones128b = k.sb(sF, "ones128b", (128, 1), BF16)
        onesrowb = k.sb(sF, "onesrowb", (1, 128), BF16)
        xn = k.sb(sF, "xn", (128, NKT * TOK), BF16)
        cT = k.sb(sF, "cT", (128, 8 * 512), F32)
        c16 = k.sb(sF, "c16", (128, 8 * 512), BF16)
        uT = k.sb(sF, "uT", (128, 8 * 512), F32)
        gsb = k.sb(sF, "gsb", (128, NKT * TOK), BF16)
        wring = k.sb(sF, "wring", (128, 8 * 1024), BF16)
        gate_f = k.sb(sF, "gate_f", (128, 3 * 512), F32)
        tmp_a = k.sb(sF, "tmp_a", (128, 512), F32)
        tmp_b = k.sb(sF, "tmp_b", (128, 512), F32)
        tmp_c = k.sb(sF, "tmp_c", (128, 512), F32)
        b3_sb = k.sb(sF, "b3_sb", (128, 24), F32)
        bo0_sb = k.sb(sF, "bo0_sb", (128, 8), F32)
        inc_sb = k.sb(sF, "inc_sb", (128, 8), F32)
        onesf = k.sb(sF, "onesf", (128, 512), F32)
        psH0 = k.ps(sF, "psH0", (128, 512))
        psH1 = k.ps(sF, "psH1", (128, 512))
        psH2 = k.ps(sF, "psH2", (128, 512))
        psB0 = k.ps(sF, "psB0", (128, 512))
        psB1 = k.ps(sF, "psB1", (128, 512))
        psB2 = k.ps(sF, "psB2", (128, 512))

        k.op("gpsimd", lambda g: g.memset(ones128b[:], 1.0))
        k.op("gpsimd", lambda g: g.memset(onesrowb[:], 1.0), incs=[(k.sgp, 1)])
        k.op("vector", lambda v: v.memset(onesf[:], 1.0), incs=[(k.svv, 1)])
        ones2 = [(k.sgp, k.sgp.n), (k.svv, k.svv.n)]
        k.load(xn[:].rearrange("p (kt n) -> p kt n", kt=8),
               xownT.rearrange("(kt p) n -> p kt n", kt=8, p=128), "gpsimd")
        k.load(b3_sb[:], b3h)
        k.load(bo0_sb[:], bo0h)
        k.load(inc_sb[:], inch)
        ldf = k.dma_all()

        env2 = ln_env(k, sF, ones128b, onesrowb, psB0, psB1, psB2,
                      pre=ldf + ones2)
        lnv2 = 0
        for g in range(4):
            lnv2 = emit_ln_group(k, xn, TOK, g, env2)

        for kt in range(NKT):
            k.op("vector", lambda v, kt=kt: v.tensor_scalar(
                out=cT[:, kt * 512:(kt + 1) * 512], in0=onesf[:],
                scalar1=inc_sb[:, kt:kt + 1], scalar2=None, op0=ALU.mult),
                waits=ldf if kt == 0 else (), incs=[(k.svv, 1)])

        gatebanks = (psH0, psH1, psH2)
        gb_war = [[], [], []]
        wslot_war = [[] for _ in range(8)]
        for t_ in range(B):
            for kt in range(NKT):
                k.op("scalar", lambda s, kt=kt: s.copy(
                    out=c16[:, kt * 512:(kt + 1) * 512],
                    in_=cT[:, kt * 512:(kt + 1) * 512]),
                    waits=[(k.svv, k.svv.n)] if kt == 0 else (),
                    incs=[(k.sev, 1)])
            cv = k.sev.n
            for dm in range(8):
                gd = []
                for gi in range(3):
                    mcol = gi * 1024 + dm * 128
                    bank = gatebanks[gi]
                    s_w = (dm * 3 + gi) % 4
                    s_u = 4 + (dm * 3 + gi) % 4
                    ldw = k.load(wring[:, s_w * 1024:(s_w + 1) * 1024]
                                 .rearrange("p (kt m) -> p kt m", kt=8),
                                 w3T.rearrange("(kt p) m -> p kt m", kt=8, p=128)
                                 [:, :, mcol:mcol + 128], "gpsimd",
                                 waits=wslot_war[s_w])
                    ldw = ldw + k.load(
                        wring[:, s_u * 1024:(s_u + 1) * 1024]
                        .rearrange("p (kt m) -> p kt m", kt=8),
                        u3T.rearrange("(kt p) m -> p kt m", kt=8, p=128)
                        [:, :, mcol:mcol + 128], "gpsimd",
                        waits=wslot_war[s_u])
                    for kt in range(NKT):
                        k.op("tensor", lambda t, kt=kt, s_w=s_w, t_=t_,
                             bank=bank: t.matmul(
                                 bank[:],
                                 wring[:, s_w * 1024 + kt * 128:
                                       s_w * 1024 + kt * 128 + 128],
                                 xn[:, kt * TOK + t_ * 512:
                                    kt * TOK + t_ * 512 + 512],
                                 start=(kt == 0), stop=False),
                             waits=(ldw + [(k.sev, cv), (k.svv, lnv2)]
                                    + gb_war[gi]) if kt == 0 else ())
                    for kt in range(NKT):
                        k.op("tensor", lambda t, kt=kt, s_u=s_u, bank=bank:
                             t.matmul(
                                 bank[:],
                                 wring[:, s_u * 1024 + kt * 128:
                                       s_u * 1024 + kt * 128 + 128],
                                 c16[:, kt * 512:(kt + 1) * 512],
                                 start=False, stop=(kt == NKT - 1)),
                             incs=[(k.smm, 1)] if kt == NKT - 1 else ())
                    wslot_war[s_w] = [(k.smm, k.smm.n)]
                    wslot_war[s_u] = [(k.smm, k.smm.n)]
                    mmg = k.smm.n
                    k.op("scalar", lambda s, gi=gi, dm=dm, bank=bank:
                         s.activation(
                             out=gate_f[:, gi * 512:(gi + 1) * 512],
                             in_=bank[:],
                             func=(AF.Sigmoid if gi < 2 else AF.Tanh),
                             bias=b3_sb[:, gi * 8 + dm: gi * 8 + dm + 1],
                             scale=1.0),
                         waits=[(k.smm, mmg)], incs=[(k.sev, 1)])
                    gb_war[gi] = [(k.sev, k.sev.n)]
                    gd.append(k.sev.n)
                k.op("vector", lambda v, dm=dm: v.tensor_tensor(
                    out=tmp_a[:], in0=gate_f[:, 0:512],
                    in1=cT[:, dm * 512:(dm + 1) * 512], op=ALU.mult),
                    waits=[(k.sev, gd[-1])])
                k.op("vector", lambda v: v.tensor_tensor(
                    out=tmp_b[:], in0=gate_f[:, 512:1024],
                    in1=gate_f[:, 1024:1536], op=ALU.mult))
                k.op("vector", lambda v, dm=dm: v.tensor_tensor(
                    out=cT[:, dm * 512:(dm + 1) * 512], in0=tmp_a[:],
                    in1=tmp_b[:], op=ALU.add), incs=[(k.svv, 1)])
                gb_war[0] = gb_war[1] = gb_war[2] = [(k.svv, k.svv.n)]

        for kt in range(NKT):
            k.op("scalar", lambda s, kt=kt: s.copy(
                out=c16[:, kt * 512:(kt + 1) * 512],
                in_=cT[:, kt * 512:(kt + 1) * 512]),
                waits=[(k.svv, k.svv.n)] if kt == 0 else (), incs=[(k.sev, 1)])
        cfv = k.sev.n
        accU = Acc([psH0, psH1], init_war=[(k.sev, k.sev.n)])
        for mt in range(8):
            slot = mt % 4
            ldu = k.load(wring[:, slot * 1024:(slot + 1) * 1024]
                         .rearrange("p (kt m) -> p kt m", kt=8),
                         uo0T.rearrange("(kt p) m -> p kt m", kt=8, p=128)
                         [:, :, mt * 128:(mt + 1) * 128], "gpsimd",
                         waits=wslot_war[slot])
            bank, war2, bj = accU.next()
            for kt in range(NKT):
                k.op("tensor", lambda t, kt=kt, slot=slot, bank=bank: t.matmul(
                    bank[:], wring[:, slot * 1024 + kt * 128:
                                   slot * 1024 + kt * 128 + 128],
                    c16[:, kt * 512:(kt + 1) * 512],
                    start=(kt == 0), stop=(kt == NKT - 1)),
                    waits=(ldu + [(k.sev, cfv)] + war2) if kt == 0 else (),
                    incs=[(k.smm, 1)] if kt == NKT - 1 else ())
            wslot_war[slot] = [(k.smm, k.smm.n)]
            mmu = k.smm.n
            k.op("vector", lambda v, mt=mt, bank=bank: v.tensor_copy(
                out=uT[:, mt * 512:(mt + 1) * 512], in_=bank[:]),
                waits=[(k.smm, mmu)], incs=[(k.svv, 1)])
            accU.done(bj, [(k.svv, k.svv.n)])
        uv = k.svv.n

        accX = Acc([psH0, psH1], init_war=[(k.svv, uv)])
        gv = 0
        for mt in range(8):
            slot = mt % 4
            ldx = k.load(wring[:, slot * 1024:(slot + 1) * 1024]
                         .rearrange("p (kt m) -> p kt m", kt=8),
                         wo0T.rearrange("(kt p) m -> p kt m", kt=8, p=128)
                         [:, :, mt * 128:(mt + 1) * 128], "gpsimd",
                         waits=wslot_war[slot])
            for tg in range(4):
                bank, war2, bj = accX.next()
                for kt in range(NKT):
                    k.op("tensor", lambda t, kt=kt, slot=slot, tg=tg, bank=bank:
                         t.matmul(bank[:],
                                  wring[:, slot * 1024 + kt * 128:
                                        slot * 1024 + kt * 128 + 128],
                                  xn[:, kt * TOK + tg * 512:
                                     kt * TOK + tg * 512 + 512],
                                  start=(kt == 0), stop=(kt == NKT - 1)),
                         waits=(ldx + war2) if kt == 0 else (),
                         incs=[(k.smm, 1)] if kt == NKT - 1 else ())
                mmx = k.smm.n
                if tg == 3:
                    wslot_war[slot] = [(k.smm, mmx)]
                k.op("vector", lambda v, mt=mt, bank=bank: v.tensor_tensor(
                    out=tmp_a[:], in0=uT[:, mt * 512:(mt + 1) * 512],
                    in1=bank[:], op=ALU.add),
                    waits=[(k.smm, mmx), (k.sev, k.sev.n)], incs=[(k.svv, 1)])
                accX.done(bj, [(k.svv, k.svv.n)])
                vvx = k.svv.n
                k.op("scalar", lambda s, mt=mt: s.activation(
                    out=tmp_b[:], in_=tmp_a[:], func=AF.Tanh,
                    bias=bo0_sb[:, mt:mt + 1], scale=1.0),
                    waits=[(k.svv, vvx)])
                k.op("scalar", lambda s: s.activation(
                    out=tmp_c[:], in_=tmp_b[:], func=AF.Sigmoid),
                    incs=[(k.sev, 1)])
                evx = k.sev.n
                k.op("vector", lambda v, mt=mt, tg=tg: v.tensor_tensor(
                    out=gsb[:, mt * TOK + tg * 512: mt * TOK + tg * 512 + 512],
                    in0=tmp_b[:], in1=tmp_c[:], op=ALU.mult),
                    waits=[(k.sev, evx)], incs=[(k.svv, 1)])
                gv = k.svv.n
        k.store(gT.rearrange("(kt p) n -> p kt n", kt=8, p=128),
                gsb[:].rearrange("p (kt n) -> p kt n", kt=8),
                waits=[(k.svv, gv)], queue="gpsimd")
        k.wait("sync", k.dma_all())
        k.wait("gpsimd", k.dma_all())
        sF.close()
        k.barrier()


def build_B():
    nc = _new_nc()
    DI = lambda n, s, d=F32: nc.dram_tensor(n, list(s), d, kind="ExternalInput").ap()
    DO = lambda n, s, d=F32: nc.dram_tensor(n, list(s), d, kind="ExternalOutput").ap()
    io = {}
    io["xownT"] = DI("xownT", (D, TOK))
    io["gT"] = DI("gT", (D, TOK))
    io["recv"] = DI("recv", (NCORE, D, 512))
    io["boh"] = DI("boh", (128, 8))
    io["waT"] = DI("waT", (D, CH))
    io["wabh"] = DI("wabh", (128, 2))
    io["wbT"] = DI("wbT", (D, CH))
    io["gateT"] = DI("gateT", (D, 8))
    io["x1T"] = DO("x1T", (D, TOK))
    io["xn1T"] = DO("xn1T", (D, TOK))
    io["aTo"] = DO("aTo", (CH, TOK))
    io["bTo"] = DO("bTo", (CH, TOK))
    io["summ"] = DO("summ", (1, 4096))
    io["logitsT"] = DO("logitsT", (TOK, 8))
    with ExitStack() as top:
        k = Kb(nc, top)
        block = top.enter_context(nc.Block())
        emit_B(k, top, io)
        k.flush(block)
    return nc


def emit_B(k, top, io):
    nc = k.nc
    xownT = io["xownT"]
    gT = io["gT"]
    recv = io["recv"]
    boh, waT, wabh, wbT, gateT = (io[n_] for n_ in
                                  ("boh", "waT", "wabh", "wbT", "gateT"))
    x1T, xn1T, aTo, bTo, summ, logitsT = (io[n_] for n_ in
                                          ("x1T", "xn1T", "aTo", "bTo",
                                           "summ", "logitsT"))
    recv_wait = io.get("recv_wait", [])
    if True:
        sB = ExitStack()
        sB.__enter__()
        xo = k.sb(sB, "xo", (128, NKT * TOK), F32)
        ones128 = k.sb(sB, "ones128", (128, 1), F32)
        onesrow = k.sb(sB, "onesrow", (1, 128), F32)
        psL = k.ps(sB, "psL", (128, 8))
        psS = k.ps(sB, "psS", (128, 512))
        psBr = k.ps(sB, "psBr", (128, 512))
        psBm = k.ps(sB, "psBm", (128, 512))
        psA0 = k.ps(sB, "psA0", (128, 512))
        psA1 = k.ps(sB, "psA1", (128, 512))

        k.op("gpsimd", lambda g: g.memset(ones128[:], 1.0))
        k.op("gpsimd", lambda g: g.memset(onesrow[:], 1.0), incs=[(k.sgp, 1)])
        ones_rdy = [(k.sgp, k.sgp.n)]

        k.load(xo[:].rearrange("p (kt n) -> p kt n", kt=8),
               xownT.rearrange("(kt p) n -> p kt n", kt=8, p=128))
        xol = k.dma_all("sync")

        x1v = 0
        with ExitStack() as s1:
            gsb = k.sb(s1, "gsb", (128, NKT * TOK), BF16)
            rv = k.sb(s1, "rv", (128, 2 * NKT * 512), F32)
            rv2 = k.sb(s1, "rv2", (128, 2 * NKT * 512), F32)
            bo_sb = k.sb(s1, "bo_sb", (128, 8), F32)
            tmp = k.sb(s1, "tmp", (128, 512), F32)
            tmp2 = k.sb(s1, "tmp2", (128, 512), F32)
            k.load(gsb[:].rearrange("p (kt n) -> p kt n", kt=8),
                   gT.rearrange("(kt p) n -> p kt n", kt=8, p=128), "gpsimd")
            k.load(bo_sb[:], boh)
            base_ld = k.dma_all()
            def load_pair(b, waits):
                buf = (rv, rv2)[b % 2]
                for two in range(2):
                    k.load(buf[:, two * NKT * 512:(two + 1) * NKT * 512]
                           .rearrange("p (kt n) -> p kt n", kt=8),
                           recv[2 * b + two, :, :]
                           .rearrange("(kt p) n -> p kt n", kt=8, p=128),
                           waits=(waits + recv_wait) if two == 0 else ())
                return k.dma_all("sync")

            ldr_b = {0: load_pair(0, []) + base_ld, 1: load_pair(1, [])}
            for b in range(B):
                buf = (rv, rv2)[b % 2]
                ldr = ldr_b[b]
                for mt in range(8):
                    r0 = buf[:, mt * 512: mt * 512 + 512]
                    r1 = buf[:, NKT * 512 + mt * 512: NKT * 512 + mt * 512 + 512]
                    col = mt * TOK + b * 512
                    k.op("vector", lambda v, r0=r0, r1=r1, mt=mt:
                         v.scalar_tensor_tensor(
                             out=tmp[:], in0=r0, scalar=bo_sb[:, mt:mt + 1],
                             in1=r1, op0=ALU.add, op1=ALU.add),
                         waits=ldr if mt == 0 else ())
                    k.op("vector", lambda v, col=col: v.tensor_tensor(
                        out=tmp2[:], in0=tmp[:], in1=gsb[:, col:col + 512],
                        op=ALU.mult))
                    k.op("vector", lambda v, col=col: v.tensor_tensor(
                        out=xo[:, col:col + 512], in0=xo[:, col:col + 512],
                        in1=tmp2[:], op=ALU.add), incs=[(k.svv, 1)])
                    x1v = k.svv.n
                if b + 2 < B:
                    ldr_b[b + 2] = load_pair(b + 2, [(k.svv, x1v)])
            x1tok = k.store(
                x1T.rearrange("(kt p) n -> p kt n", kt=8, p=128),
                xo[:].rearrange("p (kt n) -> p kt n", kt=8),
                waits=[(k.svv, x1v)])
            k.barrier()

        xn1 = k.sb(sB, "xn1", (128, NKT * TOK), BF16)
        wa_sb = k.sb(sB, "wa_sb", (128, 8 * 256), BF16)
        wb_sb = k.sb(sB, "wb_sb", (128, 8 * 256), BF16)
        wab_sb = k.sb(sB, "wab_sb", (128, 2), F32)
        a_sb = k.sb(sB, "a_sb", (128, 2 * TOK), F32)
        b_sb = k.sb(sB, "b_sb", (128, 2 * TOK), F32)
        scr = k.sb(sB, "scr", (128, 2 * 256), F32)
        csb = k.sb(sB, "csb", (128, 32), F32)
        k.load(wab_sb[:], wabh)
        k.load(wa_sb[:].rearrange("p (kt m) -> p kt m", kt=8),
               waT.rearrange("(kt p) m -> p kt m", kt=8, p=128), "gpsimd")
        k.load(wb_sb[:].rearrange("p (kt m) -> p kt m", kt=8),
               wbT.rearrange("(kt p) m -> p kt m", kt=8, p=128), "gpsimd")
        base_ld = k.dma_all()

        env = ln_env(k, sB, ones128, onesrow, psS, psBr, psBm,
                     pre=[(k.svv, x1v)] + ones_rdy + x1tok, dt=F32)
        lnv = 0
        for g in range(4):
            lnv = emit_ln_group(k, xo, TOK, g, env)
        k.store(xn1T.rearrange("(kt p) n -> p kt n", kt=8, p=128),
                xo[:].rearrange("p (kt n) -> p kt n", kt=8),
                waits=[(k.svv, lnv)])
        for kt in range(NKT):
            k.op("scalar", lambda s, kt=kt: s.copy(
                out=xn1[:, kt * TOK:(kt + 1) * TOK],
                in_=xo[:, kt * TOK:(kt + 1) * TOK]),
                waits=[(k.svv, lnv)] if kt == 0 else (), incs=[(k.sev, 1)])
        cast_done = k.sev.n
        gwf = k.sb(sB, "gwf", (128, 8 * 8), F32)
        lg = k.sb(sB, "lg", (128, 16 * 8), F32)
        gwl = k.load(gwf[:].rearrange("p (kt e) -> p kt e", kt=8),
                     gateT.rearrange("(kt p) e -> p kt e", kt=8, p=128))
        lg_done = 0
        for tt in range(16):
            for kt in range(NKT):
                k.op("tensor", lambda t, kt=kt, tt=tt: t.matmul(
                    psL[:], xo[:, kt * TOK + tt * 128: kt * TOK + tt * 128 + 128],
                    gwf[:, kt * 8:(kt + 1) * 8],
                    start=(kt == 0), stop=(kt == NKT - 1)),
                    waits=(gwl + [(k.svv, lnv), (k.svv, lg_done)])
                    if kt == 0 else (),
                    incs=[(k.smm, 1)] if kt == NKT - 1 else ())
            mml = k.smm.n
            k.op("vector", lambda v, tt=tt: v.tensor_copy(
                out=lg[:, tt * 8:(tt + 1) * 8], in_=psL[:]),
                waits=[(k.smm, mml)], incs=[(k.svv, 1)])
            lg_done = k.svv.n
        k.store(logitsT.rearrange("(tt p) e -> p tt e", tt=16, p=128),
                lg[:].rearrange("p (tt e) -> p tt e", tt=16),
                waits=[(k.svv, lg_done)])

        accA = Acc([psA0, psA1])
        av = bv = 0
        for (wsb, dst, bias, isa) in ((wa_sb, a_sb, wab_sb, True),
                                      (wb_sb, b_sb, None, False)):
            for mt in range(2):
                for tg in range(4):
                    bank, war2, bj = accA.next()
                    for kt in range(NKT):
                        k.op("tensor", lambda t, kt=kt, mt=mt, tg=tg, w=wsb,
                             bank=bank: t.matmul(
                                 bank[:],
                                 w[:, kt * 256 + mt * 128:
                                   kt * 256 + mt * 128 + 128],
                                 xn1[:, kt * TOK + tg * 512:
                                     kt * TOK + tg * 512 + 512],
                                 start=(kt == 0), stop=(kt == NKT - 1)),
                             waits=(war2 + [(k.sev, cast_done)] + base_ld)
                             if kt == 0 else (),
                             incs=[(k.smm, 1)] if kt == NKT - 1 else ())
                    mma = k.smm.n
                    if isa:
                        k.op("scalar", lambda s, mt=mt, tg=tg, bank=bank:
                             s.activation(
                                 out=a_sb[:, mt * TOK + tg * 512:
                                          mt * TOK + tg * 512 + 512],
                                 in_=bank[:], func=AF.Sigmoid,
                                 bias=wab_sb[:, mt:mt + 1], scale=1.0),
                             waits=[(k.smm, mma)], incs=[(k.sev, 1)])
                        accA.done(bj, [(k.sev, k.sev.n)])
                        av = k.sev.n
                    else:
                        k.op("scalar", lambda s, mt=mt, tg=tg, bank=bank:
                             s.copy(out=b_sb[:, mt * TOK + tg * 512:
                                             mt * TOK + tg * 512 + 512],
                                    in_=bank[:]),
                             waits=[(k.smm, mma)], incs=[(k.sev, 1)])
                        accA.done(bj, [(k.sev, k.sev.n)])
                        bv = k.sev.n
        k.store(aTo.rearrange("(h p) n -> p h n", h=2, p=128),
                a_sb[:].rearrange("p (h n) -> p h n", h=2),
                waits=[(k.sev, av)])
        k.store(bTo.rearrange("(h p) n -> p h n", h=2, p=128),
                b_sb[:].rearrange("p (h n) -> p h n", h=2),
                waits=[(k.sev, bv)])

        scv = 0
        for slot in range(2):
            for half in range(2):
                for b in range(B):
                    t0 = half * TOK + b * 512 + slot * 256
                    colA = ((slot * 2 + 0) * 2 + half) * 4 + b
                    colB = ((slot * 2 + 1) * 2 + half) * 4 + b
                    k.op("vector", lambda v, t0=t0: v.tensor_tensor_scan(
                        out=scr[:, 0:256], data0=a_sb[:, t0:t0 + 256],
                        data1=a_sb[:, t0:t0 + 256], initial=1.0,
                        op0=ALU.mult, op1=ALU.bypass),
                        waits=[(k.sev, av), (k.svv, scv)])
                    k.op("vector", lambda v, t0=t0: v.tensor_tensor_scan(
                        out=scr[:, 256:512], data0=a_sb[:, t0:t0 + 256],
                        data1=b_sb[:, t0:t0 + 256], initial=0.0,
                        op0=ALU.mult, op1=ALU.add),
                        waits=[(k.sev, bv)])
                    k.op("vector", lambda v, colA=colA: v.tensor_copy(
                        out=csb[:, colA:colA + 1], in_=scr[:, 255:256]))
                    k.op("vector", lambda v, colB=colB: v.tensor_copy(
                        out=csb[:, colB:colB + 1], in_=scr[:, 511:512]),
                        incs=[(k.svv, 1)])
                    scv = k.svv.n
        for slot in range(2):
            for ab in range(2):
                for half in range(2):
                    c0 = ((slot * 2 + ab) * 2 + half) * 4
                    off = slot * 2048 + ab * 1024 + half * 512
                    k.dma_op("sync", lambda s, c0=c0, off=off: s.dma_start(
                        out=summ[0:1, off:off + 512]
                        .rearrange("one (p b) -> p (one b)", p=128),
                        in_=csb[:, c0:c0 + 4],
                        allow_slow_non_contiguous=True),
                        waits=[(k.svv, scv)])
        k.wait("sync", k.dma_all())
        k.wait("gpsimd", k.dma_all())
        sB.__exit__(None, None, None)
        k.barrier()


def build_C():
    nc = _new_nc()
    DI = lambda n, s, d=F32: nc.dram_tensor(n, list(s), d, kind="ExternalInput").ap()
    DO = lambda n, s, d=F32: nc.dram_tensor(n, list(s), d, kind="ExternalOutput").ap()
    io = {}
    io["x1T"] = DI("x1T", (D, TOK))
    io["xn1T"] = DI("xn1T", (D, TOK))
    io["aTo"] = DI("aTi", (CH, TOK))
    io["bTo"] = DI("bTi", (CH, TOK))
    io["summ_all"] = DI("summ_all", (NCORE, 4096))
    io["logitsT"] = DI("logitsT", (TOK, 8))
    for n_, sh in (("sel2h", (17, 2)), ("wc1T", (CH, D)), ("wdT", (D, D)),
                   ("w1", (8, D, 4096)), ("b1h", (128, 256)),
                   ("w2", (8, 4096, D)), ("b2", (8, D)), ("Esel", (8, 1024)),
                   ("id16", (128, 128))):
        io[n_] = DI(n_, sh)
    io["outT"] = DO("outT", (D, TOK))
    with ExitStack() as top:
        k = Kb(nc, top)
        block = top.enter_context(nc.Block())
        emit_C(k, top, io)
        k.flush(block)
    return nc


def emit_C(k, top, io):
    nc = k.nc
    x1T, xn1T, aTi, bTi, summ_all, logitsT = (
        io["x1T"], io["xn1T"], io["aTo"], io["bTo"], io["summ_all"],
        io["logitsT"])
    sel2h, wc1T, wdT, w1, b1h, w2, b2, Esel, id16 = (
        io[n_] for n_ in C_INPUTS)
    outT = io["outT"]
    summ_wait = io.get("summ_wait", [])
    import os
    _dbg = os.environ.get("KDBG") == "1" and io.get("dbg_ok", True)
    if _dbg:
        DO2 = lambda n, s, d=F32: nc.dram_tensor(
            n, list(s), d, kind="ExternalOutput").ap()
        accdbg = DO2("accdbg", (128, 8 * TOK))
        seldbg = DO2("seldbg", (8, TOK))
        pdbg = DO2("pdbg", (1, TOK))
        ydbg = DO2("ydbg", (D, TOK))
        hdbg = DO2("hdbg", (128, 2 * TOK))
        ldbg = DO2("ldbg", (128, 16 * 8))
        hindbg = DO2("hindbg", (128, 16))
        spdbg = DO2("spdbg", (128, 16 * 8))
        salldbg = DO2("salldbg", (128, 17 * 8))
        saltdbg = DO2("saltdbg", (17, 1024))
        h2dbg = DO2("h2dbg", (2, 1024))
        asdbg = DO2("asdbg", (128, 16 * 8))

    S_scr = nc.dram_tensor("S_scr", [17, 1024], F32).ap()
    H_scr = nc.dram_tensor("H_scr", [2, 1024], F32).ap()
    y_dram = nc.dram_tensor("y_dram", [D, TOK], F32).ap()
    if True:
        sC = ExitStack()
        sC.__enter__()
        xn1 = k.sb(sC, "xn1", (128, NKT * TOK), BF16)
        acc = k.sb(sC, "acc", (128, 8 * TOK), BF16)
        h_sb = k.sb(sC, "h_sb", (128, 2 * TOK), BF16)
        selT = k.sb(sC, "selT", (8, TOK), BF16)
        pT = k.sb(sC, "pT", (1, TOK), BF16)
        ones128 = k.sb(sC, "ones128", (128, 1), BF16)
        onesrow = k.sb(sC, "onesrow", (1, 128), BF16)
        id_sb = k.sb(sC, "id_sb", (128, 128), BF16)
        e_sb = k.sb(sC, "e_sb", (8, 1024), BF16)
        b1_sb = k.sb(sC, "b1_sb", (128, 256), F32)
        b2_sb = k.sb(sC, "b2_sb", (8, 1024), BF16)
        Hini = k.sb(sC, "Hini", (128, 16), F32)
        psP0 = k.ps(sC, "psP0", (128, 512))
        psP1 = k.ps(sC, "psP1", (128, 512))
        psQ0 = k.ps(sC, "psQ0", (128, 512))
        psQ1 = k.ps(sC, "psQ1", (128, 512))
        psT9 = k.ps(sC, "psT9", (9, 128), BF16)
        psSel = k.ps(sC, "psSel", (128, 512))
        psX = k.ps(sC, "psX", (128, 512))

        k.op("gpsimd", lambda g: g.memset(ones128[:], 1.0))
        k.op("gpsimd", lambda g: g.memset(onesrow[:], 1.0), incs=[(k.sgp, 1)])
        ones_rdy = [(k.sgp, k.sgp.n)]
        k.load(id_sb[:], id16, "gpsimd")
        k.load(e_sb[:], Esel, "gpsimd")
        k.load(b2_sb[:], b2, "gpsimd")
        k.load(b1_sb[:], b1h)
        k.load(xn1[:].rearrange("p (kt n) -> p kt n", kt=8),
               xn1T.rearrange("(kt p) n -> p kt n", kt=8, p=128), "gpsimd")
        base_ld = k.dma_all()

        with ExitStack() as s1:
            As = k.sb(s1, "As", (128, 16 * 8), F32)
            Bs = k.sb(s1, "Bs", (128, 16 * 8), F32)
            Scur = k.sb(s1, "Scur", (128, 8), F32)
            Sall = k.sb(s1, "Sall", (128, 17 * 8), F32)
            for i in range(16):
                r = i if i < 8 else 15 - i
                slot = 0 if i < 8 else 1
                base = slot * 2048
                k.load(As[:, i * 8:(i + 1) * 8]
                       .rearrange("p (half b) -> p half b", half=2),
                       summ_all[r, base:base + 1024]
                       .rearrange("(half p b) -> p half b", half=2, p=128),
                       slow=True, waits=summ_wait)
                k.load(Bs[:, i * 8:(i + 1) * 8]
                       .rearrange("p (half b) -> p half b", half=2),
                       summ_all[r, base + 1024:base + 2048]
                       .rearrange("(half p b) -> p half b", half=2, p=128),
                       slow=True)
            sld = k.dma_all("sync")
            k.op("vector", lambda v: v.memset(Sall[:, 16 * 8:17 * 8], 0.0))
            k.op("vector", lambda v: v.tensor_copy(out=Scur[:], in_=Bs[:, 0:8]),
                 waits=sld, incs=[(k.svv, 1)])
            k.op("vector", lambda v: v.tensor_copy(out=Sall[:, 0:8], in_=Scur[:]),
                 waits=[(k.svv, k.svv.n)], incs=[(k.svv, 1)])
            for i in range(1, 16):
                k.op("vector", lambda v, i=i: v.tensor_tensor(
                    out=Scur[:], in0=As[:, i * 8:(i + 1) * 8], in1=Scur[:],
                    op=ALU.mult), waits=[(k.svv, k.svv.n)], incs=[(k.svv, 1)])
                k.op("vector", lambda v, i=i: v.tensor_tensor(
                    out=Scur[:], in0=Scur[:], in1=Bs[:, i * 8:(i + 1) * 8],
                    op=ALU.add), waits=[(k.svv, k.svv.n)], incs=[(k.svv, 1)])
                k.op("vector", lambda v, i=i: v.tensor_copy(
                    out=Sall[:, i * 8:(i + 1) * 8], in_=Scur[:]),
                    waits=[(k.svv, k.svv.n)], incs=[(k.svv, 1)])
            cmv = k.svv.n
            stok = []
            for i in range(17):
                stok += k.dma_op("sync", lambda sy, i=i: sy.dma_start(
                    out=S_scr[i:i + 1, :]
                    .rearrange("one (half p b) -> p (one half) b",
                               half=2, p=128),
                    in_=Sall[:, i * 8:(i + 1) * 8]
                    .rearrange("p (half b) -> p half b", half=2),
                    allow_slow_non_contiguous=True),
                    waits=[(k.svv, cmv)] if i == 0 else ())
            sel2_sb = k.sb(s1, "sel2_sb", (17, 2), F32)
            sallT = k.sb(s1, "sallT", (17, 1024), F32)
            h2_sb = k.sb(s1, "h2_sb", (2, 1024), F32)
            seltok = k.load(sel2_sb[:], sel2h)
            k.load(sallT[:], S_scr, waits=stok)
            saltok = k.dma_all("sync")
            h2ev = 0
            for ng in range(2):
                k.op("tensor", lambda t, ng=ng: t.matmul(
                    psSel[0:2, :],
                    sel2_sb[:], sallT[:, ng * 512:(ng + 1) * 512],
                    start=True, stop=True),
                    waits=((saltok + seltok) if ng == 0
                           else [(k.svv, h2ev)]),
                    incs=[(k.smm, 1)])
                mmh = k.smm.n
                k.op("vector", lambda v, ng=ng: v.tensor_copy(
                    out=h2_sb[:, ng * 512:(ng + 1) * 512], in_=psSel[0:2, :]),
                    waits=[(k.smm, mmh)], incs=[(k.svv, 1)])
                h2ev = k.svv.n
            h2v = k.svv.n
            if _dbg:
                k.store(salldbg, Sall[:], waits=[(k.svv, cmv)])
                k.store(asdbg, As[:], waits=[(k.svv, cmv)])
                k.store(saltdbg, sallT[:], waits=[(k.svv, h2v)])
                k.store(h2dbg, h2_sb[:], waits=[(k.svv, h2v)])
            htok = k.dma_op("sync", lambda sy: sy.dma_start(
                out=H_scr, in_=h2_sb[:]), waits=[(k.svv, h2v)])
            hini_ld = []
            for slot in range(2):
                for half in range(2):
                    hini_ld += k.load(
                        Hini[:, (slot * 2 + half) * 4:
                             (slot * 2 + half) * 4 + 4],
                        H_scr[slot, half * 512: half * 512 + 512]
                        .rearrange("(p b) -> p b", p=128),
                        "gpsimd", waits=htok, slow=True)
            if _dbg:
                k.store(hindbg, Hini[:], waits=hini_ld)
            k.barrier()

        with ExitStack() as s2:
            a_sb = k.sb(s2, "a_sb", (128, 2 * TOK), F32)
            b_sb = k.sb(s2, "b_sb", (128, 2 * TOK), F32)
            k.load(a_sb[:].rearrange("p (h n) -> p h n", h=2),
                   aTi.rearrange("(h p) n -> p h n", h=2, p=128))
            k.load(b_sb[:].rearrange("p (h n) -> p h n", h=2),
                   bTi.rearrange("(h p) n -> p h n", h=2, p=128))
            abl = k.dma_all("sync")
            scv = 0
            for slot in range(2):
                for half in range(2):
                    for b in range(B):
                        t0 = half * TOK + b * 512 + slot * 256
                        col = (slot * 2 + half) * 4 + b
                        k.op("vector", lambda v, t0=t0, col=col:
                             v.tensor_tensor_scan(
                                 out=h_sb[:, t0:t0 + 256],
                                 data0=a_sb[:, t0:t0 + 256],
                                 data1=b_sb[:, t0:t0 + 256],
                                 initial=Hini[:, col:col + 1],
                                 op0=ALU.mult, op1=ALU.add),
                             waits=(abl + hini_ld) if scv == 0 else (),
                             incs=[(k.svv, 1)])
                        scv = k.svv.n
            hv = scv

            wc_sb = k.sb(s2, "wc_sb", (128, 2 * 8 * 128), BF16)
            wdr = k.sb(s2, "wdr", (128, 2 * 1024), BF16)
            ystg = k.sb(s2, "ystg", (128, 2 * 512), F32)
            k.load(wc_sb[:].rearrange("p (kt m) -> p kt m", kt=2),
                   wc1T.rearrange("(kt p) m -> p kt m", kt=2, p=128), "gpsimd")
            wcl = k.dma_all("gpsimd")
            accY = Acc([psQ0, psQ1])
            wd_war = [[], []]
            ystg_war = [[], []]
            ytoks = []
            for mt in range(8):
                wslot = mt % 2
                ldd = k.load(wdr[:, wslot * 1024:(wslot + 1) * 1024]
                             .rearrange("p (kt m) -> p kt m", kt=8),
                             wdT.rearrange("(kt p) m -> p kt m", kt=8, p=128)
                             [:, :, mt * 128:(mt + 1) * 128], "gpsimd",
                             waits=wd_war[wslot])
                for tg in range(4):
                    bank, war2, bj = accY.next()
                    for kt in range(2):
                        k.op("tensor", lambda t, kt=kt, mt=mt, tg=tg, bank=bank:
                             t.matmul(bank[:],
                                      wc_sb[:, (kt * 8 + mt) * 128:
                                            (kt * 8 + mt) * 128 + 128],
                                      h_sb[:, kt * TOK + tg * 512:
                                           kt * TOK + tg * 512 + 512],
                                      start=(kt == 0), stop=False),
                             waits=(war2 + wcl + ldd + [(k.svv, hv)] + base_ld)
                             if kt == 0 else ())
                    for kt in range(NKT):
                        k.op("tensor", lambda t, kt=kt, wslot=wslot, tg=tg,
                             bank=bank: t.matmul(
                                 bank[:],
                                 wdr[:, wslot * 1024 + kt * 128:
                                     wslot * 1024 + kt * 128 + 128],
                                 xn1[:, kt * TOK + tg * 512:
                                     kt * TOK + tg * 512 + 512],
                                 start=False, stop=(kt == NKT - 1)),
                             incs=[(k.smm, 1)] if kt == NKT - 1 else ())
                    mmy = k.smm.n
                    if tg == 3:
                        wd_war[wslot] = [(k.smm, mmy)]
                    sg = ystg[:, (tg % 2) * 512:(tg % 2) * 512 + 512]
                    k.op("vector", lambda v, sg=sg, bank=bank: v.tensor_copy(
                        out=sg, in_=bank[:]),
                        waits=[(k.smm, mmy)] + ystg_war[tg % 2],
                        incs=[(k.svv, 1)])
                    accY.done(bj, [(k.svv, k.svv.n)])
                    vvy = k.svv.n
                    ystg_war[tg % 2] = k.store(
                        y_dram[mt * 128:(mt + 1) * 128,
                               tg * 512:(tg + 1) * 512], sg,
                        waits=[(k.svv, vvy)])
                    if _dbg:
                        ystg_war[tg % 2] = ystg_war[tg % 2] + k.store(
                            ydbg[mt * 128:(mt + 1) * 128,
                                 tg * 512:(tg + 1) * 512], sg,
                            waits=[(k.svv, vvy)], queue="gpsimd")
                    ytoks += ystg_war[tg % 2]
            ydone = list(ytoks)
            k.barrier()

        s45 = ExitStack()
        s45.__enter__()
        h1T = k.sb(s45, "h1T", (128, 32 * 1024), BF16)
        w1r = k.sb(s45, "w1r", (128, 4 * 1024), BF16)
        w2r = k.sb(s45, "w2r", (128, 2 * 4096), BF16)
        selBu = k.sb(s45, "selBu", (128, TOK), U8)
        l_sb = k.sb(s45, "l_sb", (128, 16 * 8), F32)
        with ExitStack() as s4:
            spd = k.sb(s4, "spd", (128, 16 * 8), F32) if _dbg else None
            nm = k.sb(s4, "nm", (128, 1), F32)
            eqs = k.sb(s4, "eqs", (128, 8), F32)
            prf = k.sb(s4, "prf", (128, 8), F32)
            esm = k.sb(s4, "esm", (128, 8), F32)
            srw = k.sb(s4, "srw", (128, 1), F32)
            prw = k.sb(s4, "prw", (128, 1), F32)
            rem = k.sb(s4, "rem", (128, 1), F32)
            eq = k.sb(s4, "eq", (128, 1), F32)
            selc = k.sb(s4, "selc", (128, 1), F32)
            sp16 = k.sb(s4, "sp16", (128, 8), BF16)
            p16 = k.sb(s4, "p16", (128, 1), BF16)
            lgl = k.load(l_sb[:].rearrange("p (tt e) -> p tt e", tt=16),
                         logitsT.rearrange("(tt p) e -> p tt e", tt=16, p=128))
            tr_war = []
            tr_done = 0
            for tt in range(16):
                lsl = l_sb[:, tt * 8:(tt + 1) * 8]
                k.op("vector", lambda v, lsl=lsl: v.reduce_max(
                    out=nm[:], in_=lsl, axis=AX, negate=True),
                    waits=lgl + [(k.smm, tr_done)], incs=[(k.svv, 1)])
                nmv = k.svv.n
                k.op("scalar", lambda s, lsl=lsl: s.activation(
                    out=esm[:], in_=lsl, func=AF.Exp, bias=nm[:], scale=1.0),
                    waits=[(k.svv, nmv)], incs=[(k.sev, 1)])
                esv = k.sev.n
                se = lambda: [(k.svv, k.svv.n)]
                k.op("vector", lambda v: v.reduce_sum(
                    out=srw[:], in_=esm[:], axis=AX), waits=[(k.sev, esv)],
                    incs=[(k.svv, 1)])
                k.op("vector", lambda v: v.reciprocal(out=prw[:], in_=srw[:]),
                     waits=se(), incs=[(k.svv, 1)])
                k.op("vector", lambda v: v.tensor_copy(out=p16[:], in_=prw[:]),
                     waits=se(), incs=[(k.svv, 1)])
                k.op("vector", lambda v, lsl=lsl: v.tensor_scalar(
                    out=eqs[:], in0=lsl, scalar1=nm[:], scalar2=0.0,
                    op0=ALU.add, op1=ALU.is_equal), waits=se(),
                    incs=[(k.svv, 1)])
                k.op("vector", lambda v: v.tensor_tensor_scan(
                    out=prf[:], data0=eqs[:], data1=eqs[:], initial=0.0,
                    op0=ALU.add, op1=ALU.bypass), waits=se(),
                    incs=[(k.svv, 1)])
                k.op("vector", lambda v: v.tensor_scalar(
                    out=prf[:], in0=prf[:], scalar1=1.0, scalar2=None,
                    op0=ALU.is_equal), waits=se(), incs=[(k.svv, 1)])
                k.op("vector", lambda v: v.tensor_tensor(
                    out=eqs[:], in0=eqs[:], in1=prf[:], op=ALU.mult),
                    waits=se(), incs=[(k.svv, 1)])
                k.op("vector", lambda v: v.tensor_copy(out=sp16[:], in_=eqs[:]),
                     waits=se(), incs=[(k.svv, 1)])
                spv = k.svv.n
                if _dbg:
                    k.op("vector", lambda v, tt=tt: v.tensor_copy(
                        out=spd[:, tt * 8:(tt + 1) * 8], in_=sp16[:]),
                        incs=[(k.svv, 1)])
                    spv = k.svv.n
                k.op("tensor", lambda t: t.transpose(
                    psT9[0:8, :], sp16[:], id_sb[:]),
                    waits=[(k.svv, spv)] + tr_war, incs=[(k.smm, 1)])
                mmt1 = k.smm.n
                k.op("vector", lambda v, tt=tt: v.tensor_copy(
                    out=selT[0:8, tt * 128:(tt + 1) * 128], in_=psT9[0:8, :]),
                    waits=[(k.smm, mmt1)], incs=[(k.svv, 1)])
                selv = k.svv.n
                k.op("tensor", lambda t: t.transpose(
                    psT9[0:1, :], p16[:], id_sb[:]),
                    waits=[(k.svv, selv)], incs=[(k.smm, 1)])
                mmt2 = k.smm.n
                k.op("vector", lambda v, tt=tt: v.tensor_copy(
                    out=pT[0:1, tt * 128:(tt + 1) * 128], in_=psT9[0:1, :]),
                    waits=[(k.smm, mmt2)], incs=[(k.svv, 1)])
                tr_war = [(k.svv, k.svv.n)]
                tr_done = mmt2
            selv_all = k.svv.n
            if False and _dbg:
                k.store(ldbg, l_sb[:], queue="gpsimd",
                        waits=[(k.svv, selv_all)])
                k.store(spdbg, spd[:], queue="gpsimd",
                        waits=[(k.svv, selv_all)])

        with ExitStack() as s5:
            accM = Acc([psP0, psP1])
            accO = Acc([psQ0, psQ1], init_war=[(k.svv, k.svv.n)])
            w1_war = [[] for _ in range(4)]
            w2_war = [[], []]
            selB_war = []
            h1_war = {}
            for e in range(8):
                sbv = 0
                for tg in range(4):
                    k.op("tensor", lambda t, e=e, tg=tg: t.matmul(
                        psSel[:], e_sb[0:8, e * 128:(e + 1) * 128],
                        selT[0:8, tg * 512:(tg + 1) * 512],
                        start=True, stop=True),
                        waits=(([(k.svv, selv_all)] + base_ld) if tg == 0
                               else []) + selB_war, incs=[(k.smm, 1)])
                    mms = k.smm.n
                    k.op("vector", lambda v, tg=tg: v.tensor_copy(
                        out=selBu[:, tg * 512:(tg + 1) * 512], in_=psSel[:]),
                        waits=[(k.smm, mms)], incs=[(k.svv, 1)])
                    sbv = k.svv.n
                    selB_war = [(k.svv, sbv)]
                for th in range(2):
                    for hm in range(32):
                        ws = hm % 4
                        ld1 = k.load(
                            w1r[:, ws * 1024:(ws + 1) * 1024]
                            .rearrange("p (kt m) -> p kt m", kt=8),
                            w1[e].rearrange("(kt p) m -> p kt m", kt=8, p=128)
                            [:, :, hm * 128:(hm + 1) * 128], "gpsimd",
                            waits=w1_war[ws])
                        for ng in range(2):
                            bank, war2, bj = accM.next()
                            for kt in range(NKT):
                                k.op("tensor", lambda t, kt=kt, ws=ws, th=th,
                                     ng=ng, bank=bank: t.matmul(
                                         bank[:],
                                         w1r[:, ws * 1024 + kt * 128:
                                             ws * 1024 + kt * 128 + 128],
                                         xn1[:, kt * TOK + th * 1024 +
                                             ng * 512: kt * TOK + th * 1024 +
                                             ng * 512 + 512],
                                         start=(kt == 0),
                                         stop=(kt == NKT - 1)),
                                     waits=(ld1 + war2) if kt == 0 else (),
                                     incs=[(k.smm, 1)] if kt == NKT - 1 else ())
                            mm1 = k.smm.n
                            if ng == 1:
                                w1_war[ws] = [(k.smm, mm1)]
                            w = [(k.smm, mm1)] + h1_war.get((hm, ng), base_ld)
                            k.op("scalar", lambda s, e=e, hm=hm, ng=ng,
                                 bank=bank: s.activation(
                                     out=h1T[:, hm * 1024 + ng * 512:
                                             hm * 1024 + ng * 512 + 512],
                                     in_=bank[:], func=AF.Relu,
                                     bias=b1_sb[:, e * 32 + hm:
                                                e * 32 + hm + 1],
                                     scale=1.0),
                                 waits=w, incs=[(k.sev, 1)])
                            accM.done(bj, [(k.sev, k.sev.n)])
                    h1v = k.sev.n
                    for dm in range(8):
                        ws = dm % 2
                        ld2 = k.load(
                            w2r[:, ws * 4096:(ws + 1) * 4096]
                            .rearrange("p (kt m) -> p kt m", kt=32),
                            w2[e].rearrange("(kt p) m -> p kt m", kt=32, p=128)
                            [:, :, dm * 128:(dm + 1) * 128], "gpsimd",
                            waits=w2_war[ws])
                        for ng in range(2):
                            bank, war2, bj = accO.next()
                            for kt in range(32):
                                k.op("tensor", lambda t, kt=kt, ws=ws, ng=ng,
                                     bank=bank: t.matmul(
                                         bank[:],
                                         w2r[:, ws * 4096 + kt * 128:
                                             ws * 4096 + kt * 128 + 128],
                                         h1T[:, kt * 1024 + ng * 512:
                                             kt * 1024 + ng * 512 + 512],
                                         start=(kt == 0), stop=(kt == 31)),
                                     waits=(ld2 + war2 + [(k.sev, h1v)])
                                     if kt == 0 else (),
                                     incs=[(k.smm, 1)] if kt == 31 else ())
                            mm2 = k.smm.n
                            if ng == 1:
                                w2_war[ws] = [(k.smm, mm2)]
                            col = th * 1024 + ng * 512
                            k.op("vector", lambda v, dm=dm, col=col, bank=bank:
                                 v.copy_predicated(
                                     out=acc[:, dm * TOK + col:
                                             dm * TOK + col + 512],
                                     mask=selBu[:, col:col + 512],
                                     data=bank[:]),
                                 waits=[(k.smm, mm2), (k.svv, sbv)],
                                 incs=[(k.svv, 1)])
                            accO.done(bj, [(k.svv, k.svv.n)])
                    for hm in range(32):
                        for ng in range(2):
                            h1_war[(hm, ng)] = [(k.smm, k.smm.n)]
            accv = k.svv.n
        s45.__exit__(None, None, None)
        if True:
            k.barrier()

        with ExitStack() as s6:
            x1r = k.sb(s6, "x1r", (128, 2 * 512), F32)
            yr = k.sb(s6, "yr", (128, 2 * 512), F32)
            t1 = k.sb(s6, "t1", (128, 512), F32)
            t2 = k.sb(s6, "t2", (128, 512), F32)
            t3 = k.sb(s6, "t3", (128, 512), F32)
            ost = k.sb(s6, "ost", (128, 2 * 512), F32)
            xy_war = [[], []]
            ost_war = [[], []]
            accB2 = Acc([psSel, psX], init_war=[(k.svv, accv)])
            pb_war = []
            for tg in range(4):
                k.op("tensor", lambda t, tg=tg: t.matmul(
                    psQ0[:], onesrow[:], pT[0:1, tg * 512:(tg + 1) * 512],
                    start=True, stop=True),
                    waits=[(k.svv, accv)] + pb_war, incs=[(k.smm, 1)])
                mmpb = k.smm.n
                for dm in range(8):
                    sl = dm % 2
                    ldx = k.load(x1r[:, sl * 512: sl * 512 + 512],
                                 x1T[dm * 128:(dm + 1) * 128,
                                     tg * 512:(tg + 1) * 512],
                                 waits=xy_war[sl])
                    ldy = k.load(yr[:, sl * 512: sl * 512 + 512],
                                 y_dram[dm * 128:(dm + 1) * 128,
                                        tg * 512:(tg + 1) * 512],
                                 waits=ydone)
                    bank, war2, bj = accB2.next()
                    k.op("tensor", lambda t, dm=dm, tg=tg, bank=bank: t.matmul(
                        bank[:], b2_sb[0:8, dm * 128:(dm + 1) * 128],
                        selT[0:8, tg * 512:(tg + 1) * 512],
                        start=True, stop=True),
                        waits=war2 + [(k.svv, accv)], incs=[(k.smm, 1)])
                    mmb2 = k.smm.n
                    k.op("vector", lambda v, dm=dm, tg=tg, bank=bank:
                         v.tensor_tensor(
                             out=t1[:], in0=acc[:, dm * TOK + tg * 512:
                                               dm * TOK + tg * 512 + 512],
                             in1=bank[:], op=ALU.add),
                         waits=[(k.smm, mmb2)], incs=[(k.svv, 1)])
                    accB2.done(bj, [(k.svv, k.svv.n)])
                    k.op("vector", lambda v: v.tensor_tensor(
                        out=t2[:], in0=t1[:], in1=psQ0[:], op=ALU.mult),
                        waits=[(k.smm, mmpb), (k.svv, k.svv.n)],
                        incs=[(k.svv, 1)])
                    vt2 = k.svv.n
                    k.op("scalar", lambda s2: s2.activation(
                        out=t3[:], in_=t2[:], func=AF.Sigmoid),
                        waits=[(k.svv, vt2)], incs=[(k.sev, 1)])
                    evs = k.sev.n
                    k.op("vector", lambda v: v.tensor_tensor(
                        out=t2[:], in0=t2[:], in1=t3[:], op=ALU.mult),
                        waits=[(k.sev, evs)])
                    k.op("vector", lambda v, sl=sl: v.tensor_tensor(
                        out=t2[:], in0=t2[:],
                        in1=yr[:, sl * 512: sl * 512 + 512],
                        op=ALU.mult), waits=ldy)
                    k.op("vector", lambda v, sl=sl: v.tensor_tensor(
                        out=ost[:, sl * 512: sl * 512 + 512],
                        in0=t2[:], in1=x1r[:, sl * 512: sl * 512 + 512],
                        op=ALU.add),
                        waits=ldx + ost_war[sl], incs=[(k.svv, 1)])
                    xy_war[sl] = [(k.svv, k.svv.n)]
                    vvo = k.svv.n
                    ost_war[sl] = k.store(
                        outT[dm * 128:(dm + 1) * 128,
                             tg * 512:(tg + 1) * 512],
                        ost[:, sl * 512: sl * 512 + 512],
                        waits=[(k.svv, vvo)])
                pb_war = [(k.svv, k.svv.n)]
            k.wait("sync", k.dma_all())
            k.wait("gpsimd", k.dma_all())
        sC.__exit__(None, None, None)


def build_F():
    nc = _new_nc()
    DI = lambda n, s, d=F32: nc.dram_tensor(n, list(s), d, kind="ExternalInput").ap()
    DO = lambda n, s, d=F32: nc.dram_tensor(n, list(s), d, kind="ExternalOutput").ap()
    NI = lambda n, s, d=F32: nc.dram_tensor(n, list(s), d).ap()
    io = {}
    for n_, sh in (("xbatT", (D, L)), ("xownT", (D, TOK)), ("wqT", (D, 512)),
                   ("wkT", (D, 512)), ("wvT", (D, 512)), ("bqh", (128, 4)),
                   ("bkh", (128, 4)), ("bvrow", (1, 512)), ("woT", (512, D)),
                   ("scaleh", (128, 1)), ("cosT", (128, L)),
                   ("sinST", (128, L)), ("Psw", (128, 128)),
                   ("band", (4 * 128, 512)), ("w3T", (D, 3072)),
                   ("u3T", (D, 3072)), ("b3h", (128, 24)), ("wo0T", (D, D)),
                   ("uo0T", (D, D)), ("bo0h", (128, 8)), ("inch", (128, 8)),
                   ("gateT", (D, 8)), ("boh", (128, 8)), ("waT", (D, CH)),
                   ("wabh", (128, 2)), ("wbT", (D, CH)), ("sel2h", (17, 2)),
                   ("wc1T", (CH, D)), ("wdT", (D, D)), ("w1", (8, D, 4096)),
                   ("b1h", (128, 256)), ("w2", (8, 4096, D)), ("b2", (8, D)),
                   ("Esel", (8, 1024)), ("id16", (128, 128))):
        io[n_] = DI(n_, sh)
    io["outT"] = DO("outT", (D, TOK))
    io["a2a_in"] = NI("fa2a_in", (NCORE, D, 512))
    a2a_out = NI("fa2a_out", (NCORE, D, 512))
    io["gT"] = NI("fgT", (D, TOK))
    io["x1T"] = NI("fx1T", (D, TOK))
    io["xn1T"] = NI("fxn1T", (D, TOK))
    io["aTo"] = NI("faT", (CH, TOK))
    io["bTo"] = NI("fbT", (CH, TOK))
    io["summ"] = NI("fsumm", (1, 4096))
    summ_all = NI("fsumm_all", (NCORE, 4096))
    io["summ_all"] = summ_all
    io["logitsT"] = NI("flogitsT", (TOK, 8))
    io["recv"] = a2a_out
    io["partialT"] = None
    io["dbg_ok"] = False

    with ExitStack() as top:
        k = Kb(nc, top)
        ccA = Sem(top, nc, "ccA")
        ccS = Sem(top, nc, "ccS")
        block = top.enter_context(nc.Block())
        emit_A(k, top, io)
        k.op("gpsimd", lambda g: g.collective_compute(
            "AllToAll", ALU.bypass, replica_groups=[list(range(NCORE))],
            ins=[io["a2a_in"]], outs=[a2a_out]),
            waits=k.dma_all(), incs=[(ccA, 1)])
        io["recv_wait"] = [(ccA, 1)]
        emit_B(k, top, io)
        k.op("gpsimd", lambda g: g.collective_compute(
            "AllGather", ALU.bypass, replica_groups=[list(range(NCORE))],
            ins=[io["summ"]], outs=[summ_all]),
            waits=k.dma_all(), incs=[(ccS, 1)])
        io["summ_wait"] = [(ccS, 1)]
        emit_C(k, top, io)
        with allow_ici_all_to_all():
            k.flush(block)
    return nc


_cache = {}


def _perm64():
    p = []
    for h in range(16):
        p += [64 * h + 2 * i for i in range(32)]
        p += [64 * h + 2 * i + 1 for i in range(32)]
    return np.array(p)


def _rope_tables():
    freqs = 10000.0 ** (-np.arange(0, 64, 2, dtype=np.float64) / 64.0)
    ang = freqs[:, None] * np.arange(L, dtype=np.float64)[None, :]
    c32 = np.cos(ang)
    s32 = np.sin(ang)
    cos128 = np.empty((128, L), np.float32)
    sinS = np.empty((128, L), np.float32)
    for a in range(2):
        cos128[64 * a:64 * a + 32] = c32
        cos128[64 * a + 32:64 * a + 64] = c32
        sinS[64 * a:64 * a + 32] = -s32
        sinS[64 * a + 32:64 * a + 64] = s32
    return cos128, sinS


def _psw():
    P = np.zeros((128, 128), np.float32)
    for a in range(2):
        for i in range(32):
            P[64 * a + i, 64 * a + 32 + i] = 1.0
            P[64 * a + 32 + i, 64 * a + i] = 1.0
    return P


def _band():
    m = np.zeros((4, 128, 512), np.float32)
    for o in range(4):
        for pk in range(128):
            m[o, pk, o * 128 + pk:] = 1.0
    return m.reshape(4 * 128, 512)


def _col128(v):
    return np.ascontiguousarray(v.reshape(-1, 128).T.astype(np.float32))


def _prep_A_inputs(inputs):
    x = np.asarray(inputs["L0_attn_x"], np.float32)
    f32 = lambda a: np.ascontiguousarray(np.asarray(a, np.float32))

    perm = _perm64()
    Wq_p = f32(inputs["L0_attn_Wq"])[perm]
    Wk_p = f32(inputs["L0_attn_Wk"])[perm]
    bq_p = f32(inputs["L0_attn_bq"])[perm]
    bk_p = f32(inputs["L0_attn_bk"])[perm]
    Wv = f32(inputs["L0_attn_Wv"])
    bv = f32(inputs["L0_attn_bv"])
    Wo = f32(inputs["L0_attn_Wo"])
    bo = f32(inputs["L0_attn_bo"])
    scaleh = np.ascontiguousarray(
        np.broadcast_to(f32(inputs["L0_attn_scale"]).reshape(1, 1), (128, 1)))
    cos128, sinS = _rope_tables()
    band = _band()
    Psw = _psw()

    w3 = np.concatenate([f32(inputs["L0_ffn_wf"]), f32(inputs["L0_ffn_wi"]),
                         f32(inputs["L0_ffn_wc"])], 0)
    u3 = np.concatenate([f32(inputs["L0_ffn_uf"]), f32(inputs["L0_ffn_ui"]),
                         f32(inputs["L0_ffn_uc"])], 0)
    b3 = np.concatenate([f32(inputs["L0_ffn_bf"]), f32(inputs["L0_ffn_bi"]),
                         f32(inputs["L0_ffn_bc"])], 0)
    wo0 = f32(inputs["L0_ffn_wo"])
    uo0 = f32(inputs["L0_ffn_uo"])
    bo0 = f32(inputs["L0_ffn_bo"])
    in_c = f32(inputs["L0_ffn_in_c"])

    Wa = f32(inputs["L1_attn_Wa_w"])
    Wab = f32(inputs["L1_attn_Wa_b"])
    Bw = f32(inputs["L1_attn_B_w"])
    Cw = f32(inputs["L1_attn_C_w"])
    Dw = f32(inputs["L1_attn_D_w"])
    gate_w = f32(inputs["L1_ffn_gate_w"])
    w1 = f32(inputs["L1_ffn_w1"])
    b1 = f32(inputs["L1_ffn_b1"])
    w2 = f32(inputs["L1_ffn_w2"])
    b2 = f32(inputs["L1_ffn_b2"])

    Esel = np.zeros((8, 1024), np.float32)
    for e in range(8):
        Esel[e, e * 128:(e + 1) * 128] = 1.0
    id16 = np.eye(128, dtype=np.float32)

    chunks = [(c, 15 - c) for c in range(NCORE)]

    w3T = f32(w3.T)
    u3T = f32(u3.T)
    wo0T = f32(wo0.T)
    uo0T = f32(uo0.T)
    WqT = f32(Wq_p.T)
    WkT = f32(Wk_p.T)
    WvT = f32(Wv.T)
    WoT = f32(Wo.T)
    inA = []
    for c in range(NCORE):
        b0, hh = c // 2, c % 2
        xbatT = f32(x[b0].T)
        xownT = np.empty((D, TOK), np.float32)
        for bb in range(B):
            for slot, ck in enumerate(chunks[c]):
                xownT[:, bb * 512 + slot * 256: bb * 512 + (slot + 1) * 256] = \
                    x[bb, ck * 256:(ck + 1) * 256, :].T
        inA.append(dict(
            xbatT=xbatT, xownT=xownT,
            wqT=f32(WqT[:, 512 * hh:512 * (hh + 1)]),
            wkT=f32(WkT[:, 512 * hh:512 * (hh + 1)]),
            wvT=f32(WvT[:, 512 * hh:512 * (hh + 1)]),
            bqh=_col128(bq_p[512 * hh:512 * (hh + 1)]),
            bkh=_col128(bk_p[512 * hh:512 * (hh + 1)]),
            bvrow=f32(bv[512 * hh:512 * (hh + 1)]).reshape(1, 512),
            woT=f32(WoT[512 * hh:512 * (hh + 1), :]),
            scaleh=scaleh, cosT=cos128, sinST=sinS, Psw=Psw, band=band,
            w3T=w3T, u3T=u3T, b3h=_col128(b3), wo0T=wo0T, uo0T=uo0T,
            bo0h=_col128(bo0), inch=_col128(in_c)))
    return inA, locals()


def _run_A_only(inputs):
    inA, _ = _prep_A_inputs(inputs)
    ncA = _cache.get("A")
    if ncA is None:
        ncA = _cache["A"] = build_A()
    resA = run_bass_kernel_spmd(ncA, inA, core_ids=list(range(NCORE)))
    return resA.results


def kernel(**inputs):
    inA, env = _prep_A_inputs(inputs)
    f32 = env["f32"]
    bo = env["bo"]
    Wa, Wab, Bw, Cw, Dw = env["Wa"], env["Wab"], env["Bw"], env["Cw"], env["Dw"]
    gate_w, w1, b1, w2, b2 = (env["gate_w"], env["w1"], env["b1"], env["w2"],
                              env["b2"])
    Esel, id16, chunks = env["Esel"], env["id16"], env["chunks"]
    WaT = f32(Wa.T)
    WbT = f32(Bw.T)
    wc1T = f32(Cw.T)
    wdT = f32(Dw.T)
    gateTT = f32(gate_w.T)
    b1h = np.ascontiguousarray(
        b1.reshape(8, 32, 128).transpose(2, 0, 1).reshape(128, 256)
        .astype(np.float32))
    boh = _col128(bo)
    wabh = _col128(Wab)
    inF = []
    for c in range(NCORE):
        sel2 = np.zeros((17, 2), np.float32)
        sel2[c - 1 if c >= 1 else 16, 0] = 1.0
        sel2[14 - c, 1] = 1.0
        m = dict(inA[c])
        m.update(gateT=gateTT, boh=boh, waT=WaT, wabh=wabh, wbT=WbT,
                 sel2h=sel2, wc1T=wc1T, wdT=wdT, w1=w1, b1h=b1h, w2=w2,
                 b2=b2, Esel=Esel, id16=id16)
        inF.append(m)
    ncF = _cache.get("F")
    if ncF is None:
        ncF = _cache["F"] = build_F()
    resF = run_bass_kernel_spmd(ncF, inF, core_ids=list(range(NCORE)))

    out = np.empty((B, L, D), np.float32)
    for c in range(NCORE):
        oT = resF.results[c]["outT"]
        for bb in range(B):
            for slot, ck in enumerate(chunks[c]):
                out[bb, ck * 256:(ck + 1) * 256, :] = \
                    oT[:, bb * 512 + slot * 256: bb * 512 + (slot + 1) * 256].T
    return out


def kernel3(**inputs):
    inA, env = _prep_A_inputs(inputs)
    x = env["x"]
    f32 = env["f32"]
    bo = env["bo"]
    Wa, Wab, Bw, Cw, Dw = env["Wa"], env["Wab"], env["Bw"], env["Cw"], env["Dw"]
    gate_w, w1, b1, w2, b2 = (env["gate_w"], env["w1"], env["b1"], env["w2"],
                              env["b2"])
    Esel, id16, chunks = env["Esel"], env["id16"], env["chunks"]
    ncA = _cache.get("A")
    if ncA is None:
        ncA = _cache["A"] = build_A()
    resA = run_bass_kernel_spmd(ncA, inA, core_ids=list(range(NCORE)))
    partial = [resA.results[c]["partialT"] for c in range(NCORE)]
    gouts = [resA.results[c]["gT"] for c in range(NCORE)]

    WaT = f32(Wa.T)
    WbT = f32(Bw.T)
    inB = []
    for c in range(NCORE):
        cols = np.concatenate([np.arange(ck * 256, (ck + 1) * 256)
                               for ck in chunks[c]])
        recv = np.stack([np.ascontiguousarray(partial[i][:, cols])
                         for i in range(NCORE)])
        inB.append(dict(
            xownT=inA[c]["xownT"], gT=gouts[c], recv=recv, boh=_col128(bo),
            waT=WaT, wabh=_col128(Wab), wbT=WbT, gateT=f32(gate_w.T)))
    ncB = _cache.get("B")
    if ncB is None:
        ncB = _cache["B"] = build_B()
    resB = run_bass_kernel_spmd(ncB, inB, core_ids=list(range(NCORE)))
    summ_all = np.stack([resB.results[c]["summ"][0] for c in range(NCORE)])

    wc1T = f32(Cw.T)
    wdT = f32(Dw.T)
    b1h = np.ascontiguousarray(
        b1.reshape(8, 32, 128).transpose(2, 0, 1).reshape(128, 256)
        .astype(np.float32))
    inC = []
    for c in range(NCORE):
        sel2 = np.zeros((17, 2), np.float32)
        sel2[c - 1 if c >= 1 else 16, 0] = 1.0
        sel2[14 - c, 1] = 1.0
        inC.append(dict(
            x1T=resB.results[c]["x1T"], xn1T=resB.results[c]["xn1T"],
            aTi=resB.results[c]["aTo"], bTi=resB.results[c]["bTo"],
            summ_all=summ_all, sel2h=sel2, wc1T=wc1T, wdT=wdT,
            logitsT=resB.results[c]["logitsT"],
            w1=w1, b1h=b1h, w2=w2, b2=b2, Esel=Esel, id16=id16))
    ncC = _cache.get("C")
    if ncC is None:
        ncC = _cache["C"] = build_C()
    resC = run_bass_kernel_spmd(ncC, inC, core_ids=list(range(NCORE)))
    _cache["lastC"] = resC
    _cache["lastB"] = resB

    out = np.empty((B, L, D), np.float32)
    for c in range(NCORE):
        oT = resC.results[c]["outT"]
        for bb in range(B):
            for slot, ck in enumerate(chunks[c]):
                out[bb, ck * 256:(ck + 1) * 256, :] = \
                    oT[:, bb * 512 + slot * 256: bb * 512 + (slot + 1) * 256].T
    return out
